# version 05
# baseline (speedup 1.0000x reference)
import numpy as np
from contextlib import ExitStack
import concourse.bass as bass
import concourse.mybir as mybir
from concourse.bass_utils import run_bass_kernel_spmd
from concourse.bass_core_rust import IndirectOffsetOnAxis

I32 = None

F32 = mybir.dt.float32
BF16 = mybir.dt.bfloat16
AF = mybir.ActivationFunctionType
ALU = mybir.AluOpType
AX = mybir.AxisListType
I32 = mybir.dt.int32

ENGINES = ("sync", "tensor", "vector", "scalar", "gpsimd")

B, S, D = 2, 8192, 1024
NCORE = 8
TLOC = B * S // NCORE


class Sem:
    __slots__ = ("h", "n")

    def __init__(self, h):
        self.h = h
        self.n = 0


class Prog:

    def __init__(self, name="prog"):
        self.nc = bass.Bass("TRN2", target_bir_lowering=False, debug=False)
        self.st = ExitStack()
        self.q = {e: [] for e in ENGINES}
        self._si = 0
        self._ni = 0

    pref = ""

    def sem(self, name=None):
        name = self.pref + (name or f"sm{self._si}")
        self._si += 1
        return Sem(self.st.enter_context(self.nc.semaphore(name)))

    def sbuf(self, shape, dt, name=None):
        name = self.pref + (name or f"sb{self._ni}")
        self._ni += 1
        cm = self.nc.sbuf_tensor(name, shape, dt)
        t = cm.__enter__()
        if not hasattr(self, "_open"):
            self._open = {}
        self._open[id(t)] = cm
        if not hasattr(self, "_order"):
            self._order = []
        self._order.append(t)
        return t

    def free(self, t):
        cm = self._open.pop(id(t))
        if hasattr(self, "_order"):
            try:
                self._order.remove(t)
            except ValueError:
                pass
        cm.__exit__(None, None, None)

    def marker(self):
        return len(getattr(self, "_order", []))

    def free_since(self, mark):
        order = getattr(self, "_order", [])
        while len(order) > mark:
            self.free(order[-1])

    def psum(self, shape, dt=F32, name=None):
        name = self.pref + (name or f"ps{self._ni}")
        self._ni += 1
        cm = self.nc.psum_tensor(name, shape, dt)
        t = cm.__enter__()
        if not hasattr(self, "_open"):
            self._open = {}
        self._open[id(t)] = cm
        if not hasattr(self, "_order"):
            self._order = []
        self._order.append(t)
        return t

    def dram(self, name, shape, dt, kind):
        return self.nc.dram_tensor(self.pref + name, shape, dt, kind=kind).ap()

    def do(self, eng, meth, *args, inc=None, inc_v=1, **kw):
        if inc is not None:
            inc.n += inc_v

            def f(E, meth=meth, args=args, kw=kw, h=inc.h, v=inc_v):
                getattr(E, meth)(*args, **kw).then_inc(h, v)
        else:
            def f(E, meth=meth, args=args, kw=kw):
                getattr(E, meth)(*args, **kw)
        self.q[eng].append(f)

    def wait(self, eng, sem, n=None):
        v = sem.n if n is None else n
        if v <= 0:
            return

        def f(E, h=sem.h, v=v):
            E.wait_ge(h, v)
        self.q[eng].append(f)

    def dma(self, eng, out, in_, sem=None, **kw):
        self.do(eng, "dma_start", inc=sem, inc_v=16, out=out, in_=in_, **kw)

    def dodr(self, eng, meth, *args, **kw):
        if not hasattr(self, "_dr"):
            self._dr = {}
        if eng not in self._dr:
            self._dr[eng] = self.sem(f"dr_{eng}")
        s = self._dr[eng]
        self.do(eng, meth, *args, inc=s, **kw)
        self.wait(eng, s)

    def finalize(self):
        with self.nc.Block() as blk:
            for e in ENGINES:
                ops = self.q[e]
                if ops:
                    getattr(blk, e)(lambda E, ops=ops: [f(E) for f in ops])
        for cm in reversed(list(getattr(self, "_open", {}).values())):
            cm.__exit__(None, None, None)
        self._open = {}
        self.st.close()
        return self.nc



def mk_consts(p):
    c = {}
    c["ones_row"] = p.sbuf([1, 128], BF16, "ones_row")
    c["ones_col"] = p.sbuf([128, 1], BF16, "ones_col")
    c["ident"] = p.sbuf([128, 128], BF16, "ident")
    c["identf"] = p.sbuf([128, 128], F32, "identf")
    c["eps_ln"] = p.sbuf([128, 1], F32, "eps_ln")
    c["eps6"] = p.sbuf([128, 1], F32, "eps6")
    g = "gpsimd"
    p.do(g, "memset", c["ones_row"][:], 1.0)
    p.do(g, "memset", c["ones_col"][:], 1.0)
    p.do(g, "memset", c["eps_ln"][:], 1e-5)
    p.do(g, "memset", c["eps6"][:], 1e-6)
    p.do(g, "memset", c["ident"][:], 1.0)
    p.do(g, "affine_select", c["ident"][:], c["ident"][:], [[1, 128]], ALU.is_equal,
         0.0, base=0, channel_multiplier=-1)
    p.do(g, "memset", c["identf"][:], 1.0)
    c["sem"] = p.sem("consts")
    p.do(g, "affine_select", c["identf"][:], c["identf"][:], [[1, 128]], ALU.is_equal,
         0.0, base=0, channel_multiplier=-1, inc=c["sem"])
    return c


def ln_token_major(p, x_ap, out_ap, tmp, sv, ss, g_bcast=None, b_bcast=None,
                   nfree=1024):
    v, s = "vector", "scalar"
    inv_n = float(1.0 / nfree)
    p.do(v, "reduce_sum", tmp["mean"][:], x_ap, axis=AX.X, inc=sv)
    p.wait(v, sv)
    p.do(v, "tensor_scalar_mul", tmp["mean"][:], tmp["mean"][:], inv_n, inc=sv)
    p.wait(v, sv)
    p.wait(v, ss)
    p.do(v, "tensor_scalar", tmp["xc"][:], x_ap, tmp["mean"][:], None,
         ALU.subtract, inc=sv)
    p.wait(s, sv)
    p.do(s, "activation", tmp["sq"][:], tmp["xc"][:], AF.Square,
         accum_out=tmp["ssq"][:], inc=ss)
    p.wait(s, ss)
    p.do(s, "activation", tmp["rstd"][:], tmp["ssq"][:], AF.Sqrt,
         bias=tmp["eps_ln"][:], scale=inv_n, inc=ss)
    p.wait(v, ss)
    p.do(v, "reciprocal", tmp["rstd"][:], tmp["rstd"][:], inc=sv)
    p.wait(v, sv)
    if g_bcast is None:
        p.do(v, "tensor_scalar_mul", out_ap, tmp["xc"][:], tmp["rstd"][:], inc=sv)
    else:
        p.dodr(v, "tensor_scalar_mul", tmp["xc"][:], tmp["xc"][:], tmp["rstd"][:])
        p.dodr(v, "tensor_mul", tmp["xc"][:], tmp["xc"][:], g_bcast)
        p.do(v, "tensor_add", out_ap, tmp["xc"][:], b_bcast, inc=sv)


def mk_ln_tmp(p, c, nfree=1024):
    return {
        "mean": p.sbuf([128, 1], F32), "ssq": p.sbuf([128, 1], F32),
        "rstd": p.sbuf([128, 1], F32), "xc": p.sbuf([128, nfree], F32),
        "sq": p.sbuf([128, nfree], F32), "eps_ln": c["eps_ln"],
    }


def bcast_row(p, c, row_ap, out_sbuf, ps_bank, sv, st, n=1024, eng="vector"):
    for j in range(0, n, 512):
        w = min(512, n - j)
        p.do("tensor", "matmul", ps_bank[:, 0:w], c["ones_row_f"][:],
             row_ap[:, j:j + w], start=True, stop=True, inc=st)
        p.wait(eng, st)
        p.do(eng, "tensor_copy", out_sbuf[:, j:j + w], ps_bank[:, 0:w], inc=sv)
        p.wait("tensor", sv)



DEBUG_L0B = False


def build_l0b(p=None, io=None):
    fused = p is not None
    if not fused:
        p = Prog("l0b")
    io = io or {}
    mark = p.marker()
    D_ = lambda n, sh, dt: io[n] if n in io else p.dram(n, sh, dt, "ExternalInput")
    x = D_("x", [TLOC, D], F32)
    if fused:
        hdT = io["hdT_view"].rearrange("r (w t) -> (r w) t", t=256)
        gidx = io["gidx"]
    else:
        hdT = D_("hdT", [D, TLOC], BF16)
    Wo = D_("Wo", [D, D], F32)
    W1 = D_("W1", [D, 4096], F32)
    W2 = D_("W2", [4096, D], F32)
    bo = D_("bo", [1, D], F32)
    b1 = D_("b1", [32, 128], F32)
    b2 = D_("b2", [1, D], F32)
    ln1g = D_("ln1g", [1, D], F32)
    ln1b = D_("ln1b", [1, D], F32)
    ln2g = D_("ln2g", [1, D], F32)
    ln2b = D_("ln2b", [1, D], F32)
    xout = io["xout"] if "xout" in io else p.dram("xout", [TLOC, D], F32,
                                                  "ExternalOutput")
    if "gate" in io:
        gs, gv = io["gate"]
        for e in ENGINES:
            p.wait(e, gs, gv)
    if DEBUG_L0B:
        dbg = {nm: p.dram("dbg_" + nm, [256, D], F32, "ExternalOutput")
               for nm in ("xln", "hpre", "h", "y")}

    c = mk_consts(p)
    c["ones_row_f"] = p.sbuf([1, 128], F32, "ones_row_f")
    p.do("gpsimd", "memset", c["ones_row_f"][:], 1.0, inc=c["sem"])

    s_in = p.sem("s_in")
    s_w2 = p.sem("s_w2")
    s_t = p.sem("s_t")
    s_v = p.sem("s_v")
    s_s = p.sem("s_s")
    s_o = p.sem("s_o")
    if fused:
        gidx_sb = p.sbuf([128, 64], I32, "gidx_sb")
        p.dma("gpsimd", gidx_sb[:], gidx, s_o)
        p.wait("gpsimd", s_o)

    WoB = [p.sbuf([128, D], BF16) for _ in range(8)]
    W1B = [p.sbuf([128, 4096], BF16) for _ in range(8)]
    bc = {}
    for nm in ("bo", "b2", "l1g", "l1b", "l2g", "l2b"):
        bc[nm] = p.sbuf([128, D], F32, f"bc_{nm}")
    b1c = p.sbuf([128, 32], F32, "b1c")
    GT = 256
    x_g = p.sbuf([128, 2 * 1024], F32, "x_g")
    xln_g = p.sbuf([128, 2 * 1024], F32, "xln_g")
    hpre_g = p.sbuf([128, 2 * 1024], F32, "hpre_g")
    h_g = p.sbuf([128, 2 * 1024], F32, "h_g")
    hbf_g = p.sbuf([128, 2 * 1024], BF16, "hbf_g")
    hdT_g = [p.sbuf([128, GT], BF16) for _ in range(8)]
    hT_g = [p.sbuf([128, GT], BF16) for _ in range(8)]
    uT = [p.sbuf([128, GT], BF16) for _ in range(2)]
    W2B = [p.sbuf([128, D], BF16) for _ in range(2)]
    w2stg = [p.sbuf([128, D], F32) for _ in range(2)]
    y_g = p.sbuf([128, 2 * 1024], F32, "y_g")
    lt = mk_ln_tmp(p, c)
    psA = p.psum([128, 512], F32, "psA")
    psT = p.psum([128, 128], BF16, "psT")
    psU = [p.psum([128, GT], F32) for _ in range(2)]
    psY = [p.psum([128, 512], F32) for _ in range(4)]

    rows = {"bo": x_g[0:1, 0:1024], "b2": xln_g[0:1, 0:1024],
            "l1g": hpre_g[0:1, 0:1024], "l1b": h_g[0:1, 0:1024],
            "l2g": y_g[0:1, 0:1024], "l2b": y_g[0:1, 1024:2048]}
    for nm, ap in (("bo", bo), ("b2", b2), ("l1g", ln1g), ("l1b", ln1b),
                   ("l2g", ln2g), ("l2b", ln2b)):
        p.dma("sync", rows[nm], ap, s_in)
    b1s = lt["xc"][0:32, 0:128]
    p.dma("sync", b1s, b1, s_in)

    def cast_weight_chunks(dst_list, src, ncols):
        cnt = []
        for i, dst in enumerate(dst_list):
            for k in range(0, ncols, 1024):
                idx = len(cnt)
                buf = idx % 2
                if idx >= 2:
                    p.wait("sync", s_v, cnt[idx - 2])
                p.dma("sync", w2stg[buf][:], src[i * 128:(i + 1) * 128, k:k + 1024],
                      s_in)
                p.wait("vector", s_in)
                p.do("vector", "tensor_copy", dst[:, k:k + 1024], w2stg[buf][:],
                     inc=s_v)
                cnt.append(s_v.n)
    cast_weight_chunks(WoB, Wo, 1024)
    cast_weight_chunks(W1B, W1, 4096)

    p.wait("tensor", s_in, 7 * 16)
    p.wait("tensor", c["sem"])
    p.do("tensor", "transpose", psA[:, 0:32], b1s, c["identf"][0:32, 0:32], inc=s_t)
    p.wait("vector", s_t)
    p.do("vector", "tensor_copy", b1c[:], psA[:, 0:32], inc=s_v)
    p.wait("tensor", s_v)

    for nm in ("bo", "b2", "l1g", "l1b", "l2g", "l2b"):
        bcast_row(p, c, rows[nm], bc[nm], psA, s_v, s_t)
    prep_sv = s_v.n

    NG = TLOC // GT

    def ts(ti):
        return slice(ti * 1024, (ti + 1) * 1024)

    for g in range(NG):
        t0 = g * GT
        if g == 0:
            p.wait("gpsimd", s_v, prep_sv)
            p.wait("sync", s_v, prep_sv)
        for ti in range(2):
            p.dma("sync", x_g[:, ts(ti)], x[t0 + ti * 128:t0 + (ti + 1) * 128, :], s_in)
        for cc in range(8):
            if fused:
                p.do("gpsimd", "indirect_dma_start", hdT_g[cc][:], None,
                     hdT,
                     IndirectOffsetOnAxis(gidx_sb[:, cc * 8 + g:cc * 8 + g + 1], 0),
                     inc=s_o, inc_v=16)
            else:
                p.dma("gpsimd", hdT_g[cc][:], hdT[cc * 128:(cc + 1) * 128, t0:t0 + GT],
                      s_o)

        for ti in range(2):
            p.wait("vector", s_in)
            ln_token_major(p, x_g[:, ts(ti)], xln_g[:, ts(ti)], lt, s_v, s_s)

        p.wait("tensor", s_o)
        p.wait("tensor", s_v)
        for ti in range(2):
            for f in range(2):
                ps = psY[2 * ti + f]
                for cc in range(8):
                    p.do("tensor", "matmul", ps[:],
                         hdT_g[cc][:, ti * 128:(ti + 1) * 128],
                         WoB[cc][:, f * 512:(f + 1) * 512],
                         start=(cc == 0), stop=(cc == 7), inc=(s_t if cc == 7 else None))
            p.wait("vector", s_t)
            for f in range(2):
                ps = psY[2 * ti + f]
                fs = slice(ti * 1024 + f * 512, ti * 1024 + (f + 1) * 512)
                p.dodr("vector", "tensor_add", hpre_g[:, fs], ps[:],
                       bc["bo"][:, f * 512:(f + 1) * 512])
                p.do("vector", "tensor_add", hpre_g[:, fs], hpre_g[:, fs],
                     xln_g[:, fs], inc=s_v)
            ln_token_major(p, hpre_g[:, ts(ti)], h_g[:, ts(ti)], lt, s_v, s_s,
                           g_bcast=bc["l1g"][:], b_bcast=bc["l1b"][:])
            p.wait("vector", s_v)
            p.do("vector", "tensor_copy", hbf_g[:, ts(ti)], h_g[:, ts(ti)], inc=s_v)
        p.wait("tensor", s_v)
        for cc in range(8):
            for ti in range(2):
                p.do("tensor", "transpose", psT[:],
                     hbf_g[:, ti * 1024 + cc * 128:ti * 1024 + (cc + 1) * 128],
                     c["ident"][:], inc=s_t)
                p.wait("scalar", s_t)
                p.do("scalar", "copy", hT_g[cc][:, ti * 128:(ti + 1) * 128], psT[:],
                     inc=s_s)
                p.wait("tensor", s_s)

        w2cnt = []
        for i in range(32):
            buf = i % 2
            if i >= 2:
                p.wait("sync", s_v, w2cnt[i - 2])
            p.dma("sync", w2stg[buf][:], W2[i * 128:(i + 1) * 128, :], s_w2)
            p.wait("vector", s_w2)
            p.wait("vector", s_t)
            p.do("vector", "tensor_copy", W2B[buf][:], w2stg[buf][:], inc=s_v)
            w2cnt.append(s_v.n)
            p.wait("tensor", s_s)
            for cc in range(8):
                p.do("tensor", "matmul", psU[buf][:], W1B[cc][:, i * 128:(i + 1) * 128],
                     hT_g[cc][:], start=(cc == 0), stop=(cc == 7),
                     inc=(s_t if cc == 7 else None))
            p.wait("scalar", s_t)
            p.do("scalar", "activation", uT[buf][:], psU[buf][:], AF.Gelu,
                 bias=b1c[:, i:i + 1], inc=s_s)
            p.wait("tensor", s_s)
            p.wait("tensor", s_v)
            for ti in range(2):
                for f in range(2):
                    p.do("tensor", "matmul", psY[2 * ti + f][:],
                         uT[buf][:, ti * 128:(ti + 1) * 128],
                         W2B[buf][:, f * 512:(f + 1) * 512],
                         start=(i == 0), stop=(i == 31),
                         inc=(s_t if i == 31 else None))
        p.wait("vector", s_t)
        if DEBUG_L0B and g == 0:
            p.wait("gpsimd", s_v)
            for ti in range(2):
                tsl = slice(ti * 128, (ti + 1) * 128)
                p.dma("gpsimd", dbg["xln"][tsl, :], xln_g[:, ts(ti)], s_o)
                p.dma("gpsimd", dbg["hpre"][tsl, :], hpre_g[:, ts(ti)], s_o)
                p.dma("gpsimd", dbg["h"][tsl, :], h_g[:, ts(ti)], s_o)
        for ti in range(2):
            for f in range(2):
                fs = slice(ti * 1024 + f * 512, ti * 1024 + (f + 1) * 512)
                p.dodr("vector", "tensor_add", y_g[:, fs], psY[2 * ti + f][:],
                       bc["b2"][:, f * 512:(f + 1) * 512])
            p.do("vector", "tensor_add", y_g[:, ts(ti)], y_g[:, ts(ti)],
                 h_g[:, ts(ti)], inc=s_v)
            if DEBUG_L0B and g == 0:
                p.wait("gpsimd", s_v)
                p.dma("gpsimd", dbg["y"][ti * 128:(ti + 1) * 128, :],
                      y_g[:, ts(ti)], s_o)
                p.wait("vector", s_o)
            ln_token_major(p, y_g[:, ts(ti)], y_g[:, ts(ti)], lt, s_v, s_s,
                           g_bcast=bc["l2g"][:], b_bcast=bc["l2b"][:])
            p.wait("vector", s_v)
            p.do("vector", "tensor_add", y_g[:, ts(ti)], y_g[:, ts(ti)],
                 x_g[:, ts(ti)], inc=s_v)
            p.wait("gpsimd", s_v)
            p.dma("gpsimd", xout[t0 + ti * 128:t0 + (ti + 1) * 128, :],
                  y_g[:, ts(ti)], s_o)
        p.wait("sync", s_v)
        p.wait("gpsimd", s_o)

    p.wait("gpsimd", s_o)
    if fused:
        out_sem = (s_o, s_o.n)
        p.free_since(mark)
        return out_sem
    return p.finalize()



DEBUG_FLASH = False
DEBUG_FLASH2 = False
FLASH_STOP_AFTER = None
L1F_STOP = None
L1F_NE = 9
DEBUG_L1F = False
DEBUG_L0A = False


def build_flash(kind, p=None, io=None):
    l1a = kind == "l1a"
    H = 16 if l1a else 8
    QF = 64 * H
    NP = H // 2
    SB = S
    fused = p is not None
    if not fused:
        p = Prog(kind)
    io = io or {}
    mark = p.marker()
    D_ = lambda n, sh, dt: io[n] if n in io else p.dram(n, sh, dt, "ExternalInput")
    xb = D_("xb", [SB, D], F32)
    xloc = D_("xloc", [TLOC, D], F32)
    wq = D_("wq", [D, QF], F32)
    wk = D_("wk", [D, QF], F32)
    wv = D_("wv", [D, QF], F32)
    wo = D_("wo", [D, D] if l1a else [QF, D], F32)
    if not l1a:
        ws = D_("ws", [D, 8], F32)
        bo8 = D_("bo8", [8, D], F32)
    xout = io["xout"] if "xout" in io else p.dram("xout", [TLOC, D], F32,
                                                  "ExternalOutput")
    if "gate" in io:
        gs, gv = io["gate"]
        for e in ENGINES:
            p.wait(e, gs, gv)
    if DEBUG_FLASH:
        dbg_out = p.dram("dbg_outT", [QF, TLOC], F32, "ExternalOutput")
        dbg_q = p.dram("dbg_qT", [QF, TLOC], F32, "ExternalOutput")
    if DEBUG_FLASH2:
        dbg_kts = p.dram("dbg_kts", [QF, SB], BF16, "ExternalOutput")
        dbg_vsc = p.dram("dbg_vsc", [H, SB // 128, 128, 64], BF16, "ExternalOutput")
        dbg_q2 = p.dram("dbg_qT2", [QF, TLOC], BF16, "ExternalOutput")
    kts = p.dram("kts", [QF, SB], BF16, "Internal")
    vsc = p.dram("vsc", [H, SB // 128, 128, 64], BF16, "Internal")

    c = mk_consts(p)
    c["ones_row_f"] = p.sbuf([1, 128], F32, "ones_row_f")
    p.do("gpsimd", "memset", c["ones_row_f"][:], 1.0, inc=c["sem"])
    if not l1a:
        sel8_in = io["sel8c"] if "sel8c" in io else p.dram("sel8c", [8, 512], F32,
                                                           "ExternalInput")
        c["sel8"] = p.sbuf([8, 512], F32, "sel8")

    s_in = p.sem("s_in")
    s_vl = p.sem("s_vl")
    s_kv = p.sem("s_kv")
    s_t = p.sem("s_t")
    s_v = p.sem("s_v")
    s_s = p.sem("s_s")
    s_o = p.sem("s_o")
    if not l1a:
        p.dma("sync", c["sel8"][:], sel8_in, s_in)

    QTp = [p.sbuf([128, TLOC], BF16) for _ in range(NP)]
    outTp = [p.sbuf([128, TLOC], BF16) for _ in range(NP)]
    KTp = p.sbuf([128, SB], BF16, "KTp")
    V1 = p.sbuf([128, 64 * 65], BF16, "V1")
    Pt = [p.sbuf([128, 1024], BF16) for _ in range(2)]
    xlnT_g = [p.sbuf([128, 512], BF16) for _ in range(8)]
    xt_f = [p.sbuf([128, 1024], F32) for _ in range(2)]
    xln_f = p.sbuf([128, 1024], F32, "xln_f")
    xbf = p.sbuf([128, 1024], BF16, "xbf")
    evac = p.sbuf([128, 512], BF16, "evac")
    otmp = p.sbuf([64, 512], F32, "otmp")
    otmp2 = p.sbuf([64, 512], F32, "otmp2")
    rd = p.sbuf([1, 512], F32, "rd")
    lt = mk_ln_tmp(p, c)
    if not l1a:
        s_tok = p.sbuf([128, 8], F32, "s_tok")
        g_m1 = p.sbuf([128, 1], F32)
        g_t1 = p.sbuf([128, 8], F32)
        g_t2 = p.sbuf([128, 8], F32)
        g_bf = p.sbuf([128, 8], BF16)
        gatedT = p.sbuf([8, TLOC], F32, "gatedT")
        gatedT_bf = p.sbuf([8, TLOC], BF16, "gatedT_bf")
        bo8B = p.sbuf([8, D], BF16, "bo8B")

    psT = p.psum([128, 128], BF16, "psT")
    psA = p.psum([128, 512], F32, "psA")
    psB = p.psum([128, 512], F32, "psB")
    psC = p.psum([128, 512], F32, "psC")

    WqB = [p.sbuf([128, QF], BF16) for _ in range(8)]
    WkB = [p.sbuf([128, QF], BF16) for _ in range(8)]
    WvB = [p.sbuf([128, QF], BF16) for _ in range(8)]
    wstg = [p.sbuf([128, 1024], F32) for _ in range(2)]
    if not l1a:
        WsB = [p.sbuf([128, 8], BF16) for _ in range(8)]

    _cstC = []

    def cast_chunks(dst_list, src, ncols):
        for i, dst in enumerate(dst_list):
            for k in range(0, ncols, 1024):
                idx = len(_cstC)
                buf = idx % 2
                if idx >= 2:
                    p.wait("sync", s_v, _cstC[idx - 2])
                p.dma("sync", wstg[buf][:, 0:min(1024, ncols)],
                      src[i * 128:(i + 1) * 128, k:k + min(1024, ncols)], s_in)
                p.wait("vector", s_in)
                p.do("vector", "tensor_copy", dst[:, k:k + min(1024, ncols)],
                     wstg[buf][:, 0:min(1024, ncols)], inc=s_v)
                _cstC.append(s_v.n)

    cast_chunks(WqB, wq, QF)
    cast_chunks(WkB, wk, QF)
    cast_chunks(WvB, wv, QF)
    if not l1a:
        for i in range(8):
            p.wait("sync", s_v)
            p.dma("sync", wstg[i % 2][:, 0:8], ws[i * 128:(i + 1) * 128, :], s_in)
            p.wait("vector", s_in)
            p.do("vector", "tensor_copy", WsB[i][:], wstg[i % 2][:, 0:8], inc=s_v)
        p.dma("gpsimd", bo8B[:], bo8, s_vl)

    def stream_tiles(src, t0, nt, emit_consumer, local_q):
        for q in range(nt):
            buf = q % 2
            p.wait("sync", s_v)
            p.dma("sync", xt_f[buf][:], src[t0 + q * 128:t0 + (q + 1) * 128, :], s_in)
            p.wait("vector", s_in)
            ln_token_major(p, xt_f[buf][:], xln_f[:], lt, s_v, s_s)
            p.wait("vector", s_v)
            p.wait("vector", s_t)
            p.do("vector", "tensor_copy", xbf[:], xln_f[:], inc=s_v)
            p.wait("tensor", s_v)
            p.wait("tensor", c["sem"])
            for cc in range(8):
                p.do("tensor", "transpose", psT[:], xbf[:, cc * 128:(cc + 1) * 128],
                     c["ident"][:], inc=s_t)
                p.wait("scalar", s_t)
                p.do("scalar", "copy", xlnT_g[cc][:, q * 128:(q + 1) * 128], psT[:],
                     inc=s_s)
                p.wait("tensor", s_s)
        emit_consumer()

    def kv_consumer(t0):
        for kf in range(QF // 128):
            p.wait("tensor", s_s)
            for cc in range(8):
                p.do("tensor", "matmul", psA[:], WkB[cc][:, kf * 128:(kf + 1) * 128],
                     xlnT_g[cc][:], start=(cc == 0), stop=(cc == 7),
                     inc=(s_t if cc == 7 else None))
            p.wait("scalar", s_t)
            p.wait("scalar", s_kv)
            p.do("scalar", "copy", evac[:], psA[:], inc=s_s)
            p.wait("gpsimd", s_s)
            p.dma("gpsimd", kts[kf * 128:(kf + 1) * 128, t0:t0 + 512], evac[:], s_kv)
        for q in range(4):
            for half in range(QF // 512):
                fs = slice(half * 512, (half + 1) * 512)
                p.wait("tensor", s_s)
                for cc in range(8):
                    p.do("tensor", "matmul", psB[:],
                         xlnT_g[cc][:, q * 128:(q + 1) * 128], WvB[cc][:, fs],
                         start=(cc == 0), stop=(cc == 7),
                         inc=(s_t if cc == 7 else None))
                p.wait("scalar", s_t)
                p.wait("scalar", s_kv)
                p.do("scalar", "copy", evac[:], psB[:], inc=s_s)
                p.wait("gpsimd", s_s)
                j = (t0 + q * 128) // 128
                p.dma("gpsimd",
                      vsc[half * 8:(half + 1) * 8, j].rearrange("h p d -> p h d"),
                      evac[:].rearrange("p (h d) -> p h d", d=64),
                      s_kv)

    for g in range(SB // 512):
        stream_tiles(xb, g * 512, 4, lambda g=g: kv_consumer(g * 512), False)

    def q_consumer(t0):
        for qf in range(QF // 128):
            p.wait("tensor", s_s)
            for cc in range(8):
                p.do("tensor", "matmul", psA[:], WqB[cc][:, qf * 128:(qf + 1) * 128],
                     xlnT_g[cc][:], start=(cc == 0), stop=(cc == 7),
                     inc=(s_t if cc == 7 else None))
            p.wait("scalar", s_t)
            p.do("scalar", "activation", QTp[qf][:, t0:t0 + 512], psA[:], AF.Copy,
                 scale=0.125, inc=s_s)
            p.wait("tensor", s_s)
        if (not l1a) and FLASH_STOP_AFTER != "nogate":
            for q in range(4):
                tsl = slice(t0 + q * 128, t0 + (q + 1) * 128)
                for cc in range(8):
                    p.do("tensor", "matmul", psC[0:128, 0:8],
                         xlnT_g[cc][:, q * 128:(q + 1) * 128], WsB[cc][:],
                         start=(cc == 0), stop=(cc == 7),
                         inc=(s_t if cc == 7 else None))
                p.wait("scalar", s_t)
                p.do("scalar", "activation", s_tok[:], psC[0:128, 0:8], AF.Sigmoid,
                     inc=s_s)
                if FLASH_STOP_AFTER == "gate1":
                    continue
                v = "vector"
                p.wait(v, s_s)
                p.dodr(v, "reduce_max", g_m1[:], s_tok[:], axis=AX.X)
                p.dodr(v, "tensor_scalar", g_t1[:], s_tok[:], g_m1[:], None,
                       ALU.is_equal)
                p.dodr(v, "scalar_tensor_tensor", g_t2[:], g_t1[:], -1e30, s_tok[:],
                       ALU.mult, ALU.add)
                p.dodr(v, "reduce_max", g_m1[:], g_t2[:], axis=AX.X)
                p.dodr(v, "tensor_scalar", g_t1[:], g_t2[:], g_m1[:], None,
                       ALU.is_equal)
                p.dodr(v, "scalar_tensor_tensor", g_t2[:], g_t1[:], -1e30, g_t2[:],
                       ALU.mult, ALU.add)
                p.dodr(v, "reduce_max", g_m1[:], g_t2[:], axis=AX.X)
                p.dodr(v, "tensor_scalar", g_t1[:], s_tok[:], g_m1[:], None,
                       ALU.is_ge)
                p.dodr(v, "tensor_mul", g_t2[:], s_tok[:], g_t1[:])
                p.do(v, "tensor_copy", g_bf[:], g_t2[:], inc=s_v)
                if FLASH_STOP_AFTER == "gate2":
                    continue
                p.wait("tensor", s_v)
                p.do("tensor", "transpose", psT[0:8, :], g_bf[:], c["ident"][:],
                     inc=s_t)
                p.wait("scalar", s_t)
                p.do("scalar", "copy", gatedT[0:8, tsl], psT[0:8, :], inc=s_s)
                p.wait("vector", s_s)
                p.do("vector", "tensor_copy", gatedT_bf[0:8, tsl], gatedT[0:8, tsl],
                     inc=s_v)
                p.wait("tensor", s_v)
                p.wait("tensor", s_s)

    if FLASH_STOP_AFTER == "p2":
        p.wait("gpsimd", s_kv, s_kv.n)
        p.dma("gpsimd", xout[0:128, :], xt_f[0][:], s_o)
        p.wait("gpsimd", s_o)
        return p.finalize()
    for g in range(TLOC // 512):
        stream_tiles(xloc, g * 512, 4,
                     lambda g=g: q_consumer(g * 512), True)

    kv_done = s_kv.n
    q_done = s_s.n
    if DEBUG_FLASH2:
        p.wait("gpsimd", s_kv, kv_done)
        p.wait("gpsimd", s_s, q_done)
        p.dma("gpsimd", dbg_kts, kts, s_o)
        p.dma("gpsimd", dbg_vsc, vsc, s_o)
        for j in range(NP):
            p.dma("gpsimd", dbg_q2[j * 128:(j + 1) * 128, :], QTp[j][:], s_o)
    if not l1a:
        for t in reversed(WsB):
            p.free(t)
    for t in reversed(wstg):
        p.free(t)
    for t in reversed(WvB):
        p.free(t)
    for t in reversed(WkB):
        p.free(t)
    for t in reversed(WqB):
        p.free(t)
    psW = [p.psum([128, 1024], F32) for _ in range(2)]
    WoB_n = 8 if l1a else 4
    WoB = [p.sbuf([128, D], BF16) for _ in range(WoB_n)]
    wstg2 = [p.sbuf([128, 1024], F32) for _ in range(2)]
    p.wait("sync", s_t)
    for i in range(WoB_n):
        buf = i % 2
        p.wait("sync", s_v)
        p.dma("sync", wstg2[buf][:], wo[i * 128:(i + 1) * 128, :], s_in)
        p.wait("vector", s_in)
        p.do("vector", "tensor_copy", WoB[i][:], wstg2[buf][:], inc=s_v)

    p.do("gpsimd", "memset",
         V1[:].rearrange("p (j x) -> p j x", x=65)[:, :, 64:65], 1.0, inc=s_kv,
         inc_v=1)
    ones_done = s_kv.n

    if FLASH_STOP_AFTER == "p3":
        p.wait("gpsimd", s_kv, kv_done)
        p.wait("gpsimd", s_s, q_done)
        p.dma("gpsimd", xout[0:128, :], xt_f[0][:], s_o)
        p.wait("gpsimd", s_o)
        return p.finalize()

    exp_cnt = []
    pv_cnt = []
    p.wait("sync", s_kv, kv_done)
    p.wait("gpsimd", s_kv, kv_done)
    for hp in range(NP):
        p.wait("sync", s_t)
        p.dma("sync", KTp[:], kts[hp * 128:(hp + 1) * 128, :], s_in)
        for hh in range(2):
            h = 2 * hp + hh
            base = 64 * hh
            p.wait("gpsimd", s_t)
            p.wait("gpsimd", s_kv, ones_done)
            p.do("gpsimd", "dma_start", inc=s_vl, inc_v=16,
                 out=V1[:].rearrange("p (j x) -> p j x", x=65)[:, :, 0:64],
                 in_=vsc[h].rearrange("j p d -> p j d"))
            for qtp in range(TLOC // 1024):
                q0 = qtp * 1024
                p.wait("tensor", s_v)
                p.wait("tensor", s_in)
                p.wait("tensor", s_vl)
                exp_cnt.clear()
                NJ = SB // 128

                def emit_scores(j):
                    b = j % 2
                    if j >= 2:
                        p.wait("tensor", s_s, exp_cnt[j - 2])
                    for half in range(2):
                        p.do("tensor", "matmul",
                             psW[b][:, half * 512:(half + 1) * 512],
                             KTp[base:base + 64, j * 128:(j + 1) * 128],
                             QTp[hp][base:base + 64,
                                     q0 + half * 512:q0 + (half + 1) * 512],
                             start=True, stop=True,
                             inc=(s_t if half == 1 else None))

                def emit_exp(j):
                    b = j % 2
                    p.wait("scalar", s_t)
                    p.do("scalar", "activation", Pt[b][:], psW[b][:], AF.Exp,
                         inc=s_s)
                    exp_cnt.append(s_s.n)

                def emit_pv(j):
                    b = j % 2
                    p.wait("tensor", s_s, exp_cnt[j])
                    for half in range(2):
                        p.do("tensor", "matmul",
                             (psA if half == 0 else psB)[0:65, :],
                             V1[:, j * 65:j * 65 + 65],
                             Pt[b][:, half * 512:(half + 1) * 512],
                             start=(j == 0), stop=(j == NJ - 1),
                             inc=(s_t if half == 1 else None))

                emit_scores(0)
                emit_exp(0)
                for j in range(1, NJ):
                    emit_scores(j)
                    emit_exp(j)
                    emit_pv(j - 1)
                emit_pv(NJ - 1)
                for half in range(2):
                    ps = psA if half == 0 else psB
                    qsl = slice(q0 + half * 512, q0 + (half + 1) * 512)
                    p.wait("scalar", s_t)
                    p.wait("scalar", s_v)
                    p.do("scalar", "copy", otmp[:], ps[0:64, :], inc=s_s)
                    p.wait("vector", s_t)
                    p.do("vector", "reciprocal", rd[:], ps[64:65, :], inc=s_v)
                    p.wait("tensor", s_v)
                    p.wait("tensor", s_s)
                    p.wait("vector", s_s)
                    if l1a:
                        p.do("tensor", "matmul", psC[0:64, :], c["ones_row_f"][0:1, 0:64],
                             rd[:], start=True, stop=True, inc=s_t)
                        p.wait("vector", s_t)
                        p.do("vector", "tensor_mul", outTp[hp][base:base + 64, qsl],
                             otmp[:], psC[0:64, :], inc=s_v)
                    else:
                        p.do("tensor", "matmul", psC[0:64, :],
                             c["sel8"][:, h * 64:(h + 1) * 64], gatedT[0:8, qsl],
                             start=True, stop=True, inc=s_t)
                        p.wait("vector", s_t)
                        p.do("vector", "tensor_mul", otmp2[:], otmp[:], psC[0:64, :],
                             inc=s_v)
                        p.wait("vector", s_v)
                        p.wait("tensor", s_v)
                        p.do("tensor", "matmul", psC[0:64, :], c["ones_row_f"][0:1, 0:64],
                             rd[:], start=True, stop=True, inc=s_t)
                        p.wait("vector", s_t)
                        p.do("vector", "tensor_mul", outTp[hp][base:base + 64, qsl],
                             otmp2[:], psC[0:64, :], inc=s_v)
                    p.wait("tensor", s_v)

    if DEBUG_FLASH:
        dstg = p.sbuf([128, TLOC], F32, "dstg")
        for j in range(NP):
            p.wait("vector", s_o)
            p.do("vector", "tensor_copy", dstg[:], outTp[j][:], inc=s_v)
            p.wait("gpsimd", s_v)
            p.dma("gpsimd", dbg_out[j * 128:(j + 1) * 128, :], dstg[:], s_o)
            p.wait("vector", s_o)
            p.do("vector", "tensor_copy", dstg[:], QTp[j][:], inc=s_v)
            p.wait("gpsimd", s_v)
            p.dma("gpsimd", dbg_q[j * 128:(j + 1) * 128, :], dstg[:], s_o)
        dbg_wo = p.dram("dbg_wo", [WoB_n * 128, D], F32, "ExternalOutput")
        for j in range(WoB_n):
            p.wait("vector", s_o)
            p.do("vector", "tensor_copy", dstg[:, 0:1024], WoB[j][:], inc=s_v)
            p.wait("gpsimd", s_v)
            p.dma("gpsimd", dbg_wo[j * 128:(j + 1) * 128, :], dstg[:, 0:1024], s_o)
        p.wait("vector", s_o)

    if FLASH_STOP_AFTER == "flash":
        p.wait("gpsimd", s_v)
        p.dma("gpsimd", xout[0:128, :], xt_f[0][:], s_o)
        p.wait("gpsimd", s_o)
        return p.finalize()

    for t in range(TLOC // 128):
        buf = t % 2
        p.wait("sync", s_v)
        p.dma("sync", xt_f[buf][:], xloc[t * 128:(t + 1) * 128, :], s_in)
        for f in range(2):
            ps = psW[f][:, 0:512]
            p.wait("tensor", s_v)
            p.wait("tensor", s_in)
            p.wait("tensor", s_vl)
            for j in range(NP):
                p.do("tensor", "matmul", ps, outTp[j][:, t * 128:(t + 1) * 128],
                     WoB[j][:, f * 512:(f + 1) * 512],
                     start=(j == 0), stop=(j == NP - 1 and l1a),
                     inc=(s_t if (j == NP - 1 and l1a) else None))
            if not l1a:
                p.do("tensor", "matmul", ps, gatedT_bf[0:8, t * 128:(t + 1) * 128],
                     bo8B[0:8, f * 512:(f + 1) * 512], start=False, stop=True,
                     inc=s_t)
            p.wait("vector", s_t)
            p.wait("vector", s_in)
            p.wait("vector", s_o)
            p.do("vector", "tensor_add", lt["xc"][:, 0:512], ps,
                 xt_f[buf][:, f * 512:(f + 1) * 512], inc=s_v)
            p.wait("gpsimd", s_v)
            p.dma("gpsimd", xout[t * 128:(t + 1) * 128, f * 512:(f + 1) * 512],
                  lt["xc"][:, 0:512], s_o)
    p.wait("gpsimd", s_o)
    if fused:
        out_sem = (s_o, s_o.n)
        p.free_since(mark)
        return out_sem
    return p.finalize()



def build_l1f(p=None, io=None):
    fused = p is not None
    if not fused:
        p = Prog("l1f")
    io = io or {}
    mark = p.marker()
    D_ = lambda n, sh, dt: io[n] if n in io else p.dram(n, sh, dt, "ExternalInput")
    xloc = D_("xloc", [TLOC, D], F32)
    gate_w = D_("gate_w", [D, 8], F32)
    gbias = D_("gbias", [1, 8], F32)
    up_w = D_("up_w", [8, D, 4096], F32)
    up_b = D_("up_b", [8, 4096], F32)
    down_w = D_("down_w", [8, 4096, D], F32)
    down_b = D_("down_b", [8, D], F32)
    sw_up = D_("sw_up", [D, 4096], F32)
    sb_up = D_("sb_up", [32, 128], F32)
    sw_down = D_("sw_down", [4096, D], F32)
    sb_down = D_("sb_down", [8, 128], F32)
    sel8w = D_("sel8w", [8, 1024], BF16)
    xout = io["xout"] if "xout" in io else p.dram("xout", [TLOC, D], F32,
                                                  "ExternalOutput")
    if "gate" in io:
        gs, gv = io["gate"]
        for e in ENGINES:
            p.wait(e, gs, gv)
    if DEBUG_L1F:
        dbg_w = p.dram("dbg_w01T", [8, TLOC], F32, "ExternalOutput")
        dbg_u = p.dram("dbg_uscr", [32, 128, 512], BF16, "ExternalOutput")
        dbg_cb = p.dram("dbg_comb", [D, TLOC], F32, "ExternalOutput")
    comb = p.dram("comb", [D, TLOC], F32, "Internal")
    uscr = p.dram("uscr", [32, 128, 512], BF16, "Internal")

    c = mk_consts(p)
    c["ones_row_f"] = p.sbuf([1, 128], F32, "ones_row_f")
    p.do("gpsimd", "memset", c["ones_row_f"][:], 1.0, inc=c["sem"])

    s_in = p.sem("s_in")
    s_t = p.sem("s_t")
    s_v = p.sem("s_v")
    s_s = p.sem("s_s")
    s_o = p.sem("s_o")

    QT = 512
    NQ = TLOC // QT
    UpB = [p.sbuf([128, 4096], BF16) for _ in range(8)]
    xbf = UpB[0][:, 0:1024]
    DnB = [p.sbuf([128, D], BF16) for _ in range(32)]
    uT2 = [p.sbuf([128, QT], BF16) for _ in range(2)]
    ub2 = [p.sbuf([128, QT], BF16) for _ in range(2)]
    xlnT = [p.sbuf([128, TLOC], BF16) for _ in range(8)]
    GwBt = p.sbuf([128, 64], BF16, "GwBt")
    GwB = [GwBt[:, cc * 8:(cc + 1) * 8] for cc in range(8)]
    w01T = p.sbuf([8, TLOC], BF16, "w01T")
    sel8B = p.sbuf([8, 1024], BF16, "sel8B")
    wb_sb = p.sbuf([128, QT], F32, "wb_sb")
    ev_one = p.sbuf([128, QT], F32, "ev_one")
    ev_sb = [ev_one[:], ev_one[:]]
    up_bT = p.sbuf([128, 32 * 8], F32, "up_bT")
    db_sb = p.sbuf([128, 8 * 8], F32, "db_sb")
    sbu_c = p.sbuf([128, 32], F32, "sbu_c")
    sbd_c = p.sbuf([128, 8], F32, "sbd_c")
    xt_one = p.sbuf([128, 1024], F32, "xt_one")
    xt_f = [xt_one, xt_one]
    fac = p.sbuf([128, 4], F32, "fac")
    stg = [p.sbuf([128, 512], F32) for _ in range(2)]
    lt = mk_ln_tmp(p, c)
    g_s = p.sbuf([128, 8], F32)
    g_m = p.sbuf([128, 1], F32)
    g_a = p.sbuf([128, 8], F32)
    g_k = p.sbuf([128, 8], F32)
    g_bf = p.sbuf([128, 8], BF16)
    gb_bc = p.sbuf([128, 8], F32, "gb_bc")
    gbrow = p.sbuf([1, 8], F32, "gbrow")
    hbuf = [xt_f[0][:, 0:QT], xt_f[0][:, QT:2 * QT]]
    macc = None
    xsl = [lt["xc"][:, 0:128], lt["xc"][:, 128:256]]
    osl = [lt["xc"][:, 256:384], lt["xc"][:, 384:512]]
    macc = lt["sq"][:, 0:QT]
    psT = p.psum([128, 128], BF16, "psT")
    psU = [p.psum([128, QT], F32) for _ in range(2)]
    psD = [p.psum([128, QT], F32) for _ in range(5)]

    p.dma("sync", sel8B[:], sel8w, s_in)
    p.dma("sync", gbrow[:], gbias, s_in)
    for cc in range(8):
        p.wait("sync", s_v)
        p.dma("sync", stg[cc % 2][:, 0:8], gate_w[cc * 128:(cc + 1) * 128, :], s_in)
        p.wait("vector", s_in)
        p.do("vector", "tensor_copy", GwB[cc], stg[cc % 2][:, 0:8], inc=s_v)
    p.wait("tensor", s_in, 32)
    p.wait("tensor", c["sem"])
    p.do("tensor", "matmul", psU[0][0:128, 0:8], c["ones_row_f"][:], gbrow[:],
         start=True, stop=True, inc=s_t)
    p.wait("vector", s_t)
    p.do("vector", "tensor_copy", gb_bc[:], psU[0][0:128, 0:8], inc=s_v)
    p.wait("tensor", s_v)

    def load_T(src, rows, dst, dstc0, nsl=1):
        for c0 in range(0, nsl, 4):
            cn = min(4, nsl - c0)
            p.wait("sync", s_v)
            p.wait("sync", s_s)
            p.dma("sync", stg[0][0:rows, 0:cn * 128],
                  src[:, c0 * 128:(c0 + cn) * 128], s_in)
            for jj in range(cn):
                j = c0 + jj
                p.wait("tensor", s_in)
                p.wait("tensor", s_s)
                p.do("tensor", "transpose", psU[1][0:128, 0:rows],
                     stg[0][0:rows, jj * 128:(jj + 1) * 128],
                     c["identf"][0:rows, 0:rows], inc=s_t)
                p.wait("scalar", s_t)
                p.do("scalar", "copy",
                     dst[:, dstc0 + j * rows:dstc0 + (j + 1) * rows],
                     psU[1][0:128, 0:rows], inc=s_s)

    for e in range(8):
        load_T(up_b[e].rearrange("(i x) -> i x", x=128), 32, up_bT, e * 32, nsl=1)
    for e in range(8):
        load_T(down_b[e].rearrange("(i x) -> i x", x=128), 8, db_sb, e * 8, nsl=1)
    load_T(sb_up, 32, sbu_c, 0, nsl=1)
    load_T(sb_down, 8, sbd_c, 0, nsl=1)
    p.wait("tensor", s_s)

    if L1F_STOP == "prep":
        p.wait("gpsimd", s_s, s_s.n)
        p.dma("gpsimd", xout[0:128, :], stg[0][0:128, 0:512], s_o)
        p.wait("gpsimd", s_o)
        return p.finalize()

    for t in range(TLOC // 128):
        t0 = t * 128
        buf = t % 2
        p.wait("sync", s_v)
        p.dma("sync", xt_f[buf][:], xloc[t0:t0 + 128, :], s_in)
        p.wait("vector", s_in)
        ln_token_major(p, xt_f[buf][:], lt["sq"][:], lt, s_v, s_s)
        p.wait("vector", s_v)
        p.wait("vector", s_t)
        p.do("vector", "tensor_copy", xbf, lt["sq"][:], inc=s_v)
        p.wait("tensor", s_v)
        for cc in range(8):
            p.do("tensor", "transpose", psT[:], xbf[:, cc * 128:(cc + 1) * 128]
                 if False else UpB[0][:, cc * 128:(cc + 1) * 128],
                 c["ident"][:], inc=s_t)
            p.wait("scalar", s_t)
            p.do("scalar", "copy", xlnT[cc][:, t0:t0 + 128], psT[:], inc=s_s)
            p.wait("tensor", s_s)
        for cc in range(8):
            p.do("tensor", "matmul", psU[0][0:128, 0:8], xlnT[cc][:, t0:t0 + 128],
                 GwB[cc], start=(cc == 0), stop=(cc == 7),
                 inc=(s_t if cc == 7 else None))
        v = "vector"
        p.wait(v, s_t)
        p.dodr(v, "tensor_add", g_a[:], psU[0][0:128, 0:8], gb_bc[:])
        p.wait("scalar", s_v)
        p.wait("scalar", p._dr["vector"])
        p.do("scalar", "activation", g_s[:], g_a[:], AF.Sigmoid, inc=s_s)
        p.wait(v, s_s)
        p.dodr(v, "reduce_max", g_m[:], g_s[:], axis=AX.X)
        p.dodr(v, "tensor_scalar", g_a[:], g_s[:], g_m[:], None, ALU.is_equal)
        p.dodr(v, "scalar_tensor_tensor", g_a[:], g_a[:], -1e30, g_s[:],
               ALU.mult, ALU.add)
        p.dodr(v, "reduce_max", g_m[:], g_a[:], axis=AX.X)
        p.dodr(v, "tensor_scalar", g_k[:], g_s[:], g_m[:], None, ALU.is_ge)
        p.dodr(v, "tensor_mul", g_k[:], g_s[:], g_k[:])
        p.dodr(v, "reduce_sum", g_m[:], g_k[:], axis=AX.X)
        p.dodr(v, "tensor_scalar_add", g_m[:], g_m[:], 1e-6)
        p.dodr(v, "reciprocal", g_m[:], g_m[:])
        p.dodr(v, "tensor_scalar_mul", g_m[:], g_m[:], 0.1)
        p.dodr(v, "tensor_scalar_mul", g_k[:], g_k[:], g_m[:])
        p.do(v, "tensor_copy", g_bf[:], g_k[:], inc=s_v)
        p.wait("tensor", s_v)
        p.do("tensor", "transpose", psT[0:8, :], g_bf[:], c["ident"][:], inc=s_t)
        p.wait("scalar", s_t)
        p.do("scalar", "copy", w01T[0:8, t0:t0 + 128], psT[0:8, :], inc=s_s)
        p.wait("tensor", s_s)
        p.wait("vector", s_s)

    if L1F_STOP == "p1":
        p.wait("gpsimd", s_s, s_s.n)
        p.dma("gpsimd", xout[0:128, :], xt_f[0][:], s_o)
        p.wait("gpsimd", s_o)
        return p.finalize()

    region_cnt = {}
    for e in range(L1F_NE):
        shared = e == 8
        up_src = sw_up if shared else up_w[e]
        dn_src = sw_down if shared else down_w[e]
        p.wait("vector", s_t)
        cast_cnt = []
        srcs = ([(UpB[cc][:, k4 * 512:(k4 + 1) * 512],
                  up_src[cc * 128:(cc + 1) * 128, k4 * 512:(k4 + 1) * 512])
                 for cc in range(8) for k4 in range(8)] +
                [(DnB[i][:, k2 * 512:(k2 + 1) * 512],
                  dn_src[i * 128:(i + 1) * 128, k2 * 512:(k2 + 1) * 512])
                 for i in range(32) for k2 in range(2)])
        for idx, (dst, src_ap) in enumerate(srcs):
            b = idx % 2
            if idx >= 2:
                p.wait("sync", s_v, cast_cnt[idx - 2])
            p.dma("sync", stg[b][:], src_ap, s_in)
            p.wait("vector", s_in)
            p.do("vector", "tensor_copy", dst, stg[b][:], inc=s_v)
            cast_cnt.append(s_v.n)

        if L1F_STOP == "wonly":
            continue
        for g in range(NQ):
            q0 = g * QT
            if L1F_STOP == "noup":
                break
            if L1F_STOP == "nowb":
                pass
            elif shared:
                p.wait("vector", s_o)
                p.do("vector", "memset", wb_sb[:], 0.1, inc=s_v)
            else:
                p.wait("tensor", s_v)
                p.wait("tensor", s_s)
                p.do("tensor", "matmul", psU[0][:, 0:QT],
                     sel8B[:, e * 128:(e + 1) * 128], w01T[0:8, q0:q0 + QT],
                     start=True, stop=True, inc=s_t)
                p.wait("vector", s_t)
                p.do("vector", "tensor_copy", wb_sb[:], psU[0][:, 0:QT], inc=s_v)
            if L1F_STOP == "nogelu":
                continue
            if not hasattr(p, "_l1f_gelu"):
                p._l1f_gelu = []
                p._l1f_udma = []
            gelu_cnt = p._l1f_gelu
            udma_cnt = p._l1f_udma
            gb0 = len(gelu_cnt)
            ub0 = len(udma_cnt)
            p.wait("tensor", s_v)

            def emit_up(i):
                b = i % 2
                if gb0 + i >= 2:
                    p.wait("tensor", s_s, gelu_cnt[gb0 + i - 2])
                for cc in range(8):
                    p.do("tensor", "matmul", psU[b][:, 0:QT],
                         UpB[cc][:, i * 128:(i + 1) * 128], xlnT[cc][:, q0:q0 + QT],
                         start=(cc == 0), stop=(cc == 7),
                         inc=(s_t if cc == 7 else None))

            def emit_gelu(i):
                b = i % 2
                p.wait("scalar", s_t)
                if ub0 + i >= 2:
                    p.wait("scalar", s_o, udma_cnt[ub0 + i - 2])
                bias_col = (sbu_c[:, i:i + 1] if shared
                            else up_bT[:, e * 32 + i:e * 32 + i + 1])
                p.do("scalar", "activation", uT2[b][:], psU[b][:, 0:QT],
                     AF.Gelu, bias=bias_col, inc=s_s)
                gelu_cnt.append(s_s.n)

            def emit_sweepA(i):
                b = i % 2
                p.wait("tensor", s_s, gelu_cnt[gb0 + i])
                for hi, h in enumerate(range(0, 5)):
                    p.do("tensor", "matmul", psD[hi][:, 0:QT],
                         DnB[i][:, h * 128:(h + 1) * 128], uT2[b][:],
                         start=(i == 0), stop=(i == 31),
                         inc=(s_t if (i == 31 and hi == 4) else None))
                p.wait("gpsimd", s_s, gelu_cnt[gb0 + i])
                p.dma("gpsimd", uscr[i], uT2[b][:], s_o)
                udma_cnt.append(s_o.n)

            emit_up(0)
            emit_gelu(0)
            for i in range(1, 32):
                emit_up(i)
                emit_gelu(i)
                emit_sweepA(i - 1)
            emit_sweepA(31)

            def emit_evac(hi, h):
                eb = ev_sb[0]
                db_col = (sbd_c[:, h:h + 1] if shared
                          else db_sb[:, e * 8 + h:e * 8 + h + 1])
                p.wait("vector", s_o)
                p.do("vector", "scalar_tensor_tensor", eb[:], psD[hi][:, 0:QT],
                     db_col, wb_sb[:], ALU.add, ALU.mult, inc=s_v)
                p.wait("gpsimd", s_v)
                key = (g, h)
                if key in region_cnt:
                    p.wait("gpsimd", s_o, region_cnt[key])
                p.dma("gpsimd", comb[h * 128:(h + 1) * 128, q0:q0 + QT], eb[:],
                      s_o, accum_op=(ALU.bypass if e == 0 else ALU.add))
                region_cnt[key] = s_o.n

            if L1F_STOP not in ("noswa", "noevac"):
                p.wait("vector", s_t)
                for hi, h in enumerate(range(0, 5)):
                    emit_evac(hi, h)
            if L1F_STOP in ("noswb", "noswa", "noevac"):
                continue
            p.wait("tensor", s_v)
            p.wait("sync", s_t)
            mmB_cnt = []
            for i in range(32):
                b = i % 2
                p.wait("sync", s_o)
                if i >= 2:
                    p.wait("sync", s_t, mmB_cnt[i - 2])
                p.dma("sync", ub2[b][:], uscr[i], s_in)
                p.wait("tensor", s_in)
                for hi, h in enumerate(range(5, 8)):
                    p.do("tensor", "matmul", psD[hi][:, 0:QT],
                         DnB[i][:, h * 128:(h + 1) * 128], ub2[b][:],
                         start=(i == 0), stop=(i == 31),
                         inc=(s_t if hi == 2 else None))
                mmB_cnt.append(s_t.n)
            p.wait("vector", s_t)
            for hi, h in enumerate(range(5, 8)):
                emit_evac(hi, h)

    if L1F_STOP == "p2":
        p.wait("gpsimd", s_o, s_o.n)
        p.dma("gpsimd", xout[0:128, :], xt_f[0][:], s_o)
        p.wait("gpsimd", s_o)
        return p.finalize()

    if DEBUG_L1F:
        for gq in range(4):
            p.wait("vector", s_o)
            p.do("vector", "tensor_copy", ev_sb[0][0:8, 0:512],
                 w01T[0:8, gq * 512:(gq + 1) * 512], inc=s_v)
            p.wait("gpsimd", s_v)
            p.dma("gpsimd", dbg_w[0:8, gq * 512:(gq + 1) * 512],
                  ev_sb[0][0:8, 0:512], s_o)
        p.wait("gpsimd", s_o)
        p.dma("gpsimd", dbg_u, uscr, s_o)
        p.dma("gpsimd", dbg_cb, comb, s_o)
        p.wait("gpsimd", s_o)

    all_acc = s_o.n
    p.wait("sync", s_o, all_acc)
    for g in range(NQ):
        q0 = g * QT
        for h in range(8):
            hb = hbuf[h % 2]
            p.wait("sync", s_v)
            p.wait("sync", s_s)
            p.dma("sync", hb, comb[h * 128:(h + 1) * 128, q0:q0 + QT], s_in)
            p.wait("scalar", s_in)
            p.wait("scalar", s_v)
            p.wait("scalar", p._dr.get("vector", s_v))
            p.do("scalar", "activation", ev_sb[0][:], hb, AF.Abs, inc=s_s)
            p.wait("vector", s_s)
            if h == 0:
                p.do("vector", "tensor_copy", macc, ev_sb[0][:], inc=s_v)
            else:
                p.dodr("vector", "tensor_max", macc, macc, ev_sb[0][:])
        p.wait("tensor", p._dr["vector"])
        for j in range(4):
            p.wait("tensor", s_v)
            p.wait("tensor", s_s)
            p.do("tensor", "transpose", psU[1][0:128, 0:128],
                 lt["sq"][:, j * 128:(j + 1) * 128], c["identf"][:], inc=s_t)
            p.wait("vector", s_t)
            p.do("vector", "reduce_max", fac[:, j:j + 1], psU[1][0:128, 0:128],
                 axis=AX.X, inc=s_v)
            p.wait("tensor", s_v)
        p.wait("vector", s_v)
        p.dodr("vector", "tensor_scalar_add", fac[:], fac[:], 1e-6)
        p.dodr("vector", "reciprocal", fac[:], fac[:])
        for h in range(8):
            hb = hbuf[h % 2]
            p.wait("sync", s_v)
            p.dma("sync", hb, comb[h * 128:(h + 1) * 128, q0:q0 + QT], s_in)
            for j in range(4):
                bx = j % 2
                p.wait("sync", s_v)
                p.dma("sync", xsl[bx],
                      xloc[q0 + j * 128:q0 + (j + 1) * 128,
                           h * 128:(h + 1) * 128], s_in)
                p.wait("tensor", s_in)
                p.wait("tensor", s_v)
                p.do("tensor", "transpose", psU[1][0:128, 0:128],
                     hb.tensor[0:128, hb.offset + j * 128:hb.offset + (j + 1) * 128]
                     if False else hb[:, j * 128:(j + 1) * 128],
                     c["identf"][:], inc=s_t)
                p.wait("vector", s_t)
                p.wait("vector", s_in)
                p.wait("vector", s_o)
                p.do("vector", "scalar_tensor_tensor", osl[bx],
                     psU[1][0:128, 0:128], fac[:, j:j + 1], xsl[bx],
                     ALU.mult, ALU.add, inc=s_v)
                p.wait("gpsimd", s_v)
                p.dma("gpsimd", xout[q0 + j * 128:q0 + (j + 1) * 128,
                                     h * 128:(h + 1) * 128], osl[bx], s_o)
    p.wait("gpsimd", s_o)
    if fused:
        out_sem = (s_o, s_o.n)
        p.free_since(mark)
        return out_sem
    return p.finalize()



def build_l0a(p=None, io=None):
    fused = p is not None
    if not fused:
        p = Prog("l0a")
    io = io or {}
    mark = p.marker()
    D_ = lambda n, sh, dt: io[n] if n in io else p.dram(n, sh, dt, "ExternalInput")
    xb = D_("xb", [S, D], F32)
    wq = D_("wq", [D, 256], F32)
    wk = D_("wk", [D, 256], F32)
    wv = D_("wv", [D, 256], F32)
    bq = D_("bq", [1, 256], F32)
    bk = D_("bk", [1, 256], F32)
    bv = D_("bv", [1, 256], F32)
    outT = io["outT"] if "outT" in io else p.dram("outT", [D, TLOC], BF16,
                                                  "ExternalOutput")
    if "gate" in io:
        gs, gv = io["gate"]
        for e in ENGINES:
            p.wait(e, gs, gv)
    if DEBUG_L0A:
        dbg_c = p.dram("dbg_cols", [S // 128, 128, 16], F32, "ExternalOutput")
        dbg_sra = p.dram("dbg_sra", [S // 128, 1, 512], F32, "ExternalOutput")

    c = mk_consts(p)
    g = "gpsimd"
    c["ones_row_f"] = p.sbuf([1, 128], F32, "ones_row_f")
    c["ones_col_f"] = p.sbuf([128, 1], F32, "ones_col_f")
    c["triu_f"] = p.sbuf([128, 128], F32, "triu_f")
    c["triu_b"] = p.sbuf([128, 128], BF16, "triu_b")
    p.do(g, "memset", c["ones_row_f"][:], 1.0)
    p.do(g, "memset", c["ones_col_f"][:], 1.0)
    p.do(g, "memset", c["triu_f"][:], 1.0)
    p.do(g, "affine_select", c["triu_f"][:], c["triu_f"][:], [[1, 128]], ALU.is_ge,
         0.0, base=0, channel_multiplier=-1)
    p.do(g, "memset", c["triu_b"][:], 1.0)
    p.do(g, "affine_select", c["triu_b"][:], c["triu_b"][:], [[1, 128]], ALU.is_ge,
         0.0, base=0, channel_multiplier=-1, inc=c["sem"])

    s_in = p.sem("s_in")
    s_t = p.sem("s_t")
    s_v = p.sem("s_v")
    s_s = p.sem("s_s")
    s_g = p.sem("s_g")
    s_o = p.sem("s_o")

    NH = 4
    NC = S // 128
    EPS = 1e-6

    WB = [p.sbuf([128, 768], BF16) for _ in range(8)]
    bbc = p.sbuf([128, 768], F32, "bbc")
    bqkv = p.sbuf([1, 768], F32, "bqkv")
    outS = [p.sbuf([128, S], BF16) for _ in range(2)]
    xlnT_c = [p.sbuf([128, 128], BF16) for _ in range(8)]
    xt_f = [p.sbuf([128, 1024], F32) for _ in range(2)]
    xbf = p.sbuf([128, 1024], BF16, "xbf")
    lt = mk_ln_tmp(p, c)
    qkv_f = [p.sbuf([128, 192], F32) for _ in range(NH)]
    qk_f = [p.sbuf([128, 128], F32) for _ in range(NH)]
    qk_b = [p.sbuf([128, 128], BF16) for _ in range(NH)]
    qT_b = [p.sbuf([64, 128], BF16) for _ in range(NH)]
    kT_b = [p.sbuf([64, 128], BF16) for _ in range(NH)]
    ksqs_f = [p.sbuf([128, 128], F32) for _ in range(NH)]
    vs_b = [p.sbuf([128, 64], BF16) for _ in range(NH)]
    A_b = [p.sbuf([128, 128], BF16) for _ in range(NH)]
    kvprev_b = [p.sbuf([64, 64], BF16) for _ in range(NH)]
    cum1 = [p.sbuf([128, 128], F32) for _ in range(NH)]
    cum2 = [p.sbuf([128, 128], F32) for _ in range(NH)]
    o_f = [p.sbuf([64, 128], F32) for _ in range(NH)]
    facT = p.sbuf([1, 128], F32, "facT")
    facE = p.sbuf([64, 128], F32, "facE")
    colsc = p.sbuf([128, 16 * NH], F32, "colsc")
    sra_sb = p.sbuf([1, 512], F32, "sra_sb")
    srb_sb = p.sbuf([1, 512], F32, "srb_sb")
    ce_sb = p.sbuf([1, 4], F32, "ce_sb")
    cex_sb = p.sbuf([128, NH], F32, "cex_sb")
    nrm = p.sbuf([128, 1], F32, "nrm")
    rnrm = p.sbuf([128, 1], F32, "rnrm")
    stg = [p.sbuf([128, 768], F32) for _ in range(2)]
    sqs = [p.sbuf([128, 64], F32) for _ in range(NH)]

    pC = p.psum([128, 512], F32, "pC")
    pS = p.psum([128, 512], F32, "pS")
    pO = p.psum([128, 512], F32, "pO")
    pKV = p.psum([128, 512], F32, "pKV")
    pA = p.psum([128, 512], F32, "pA")
    pB = p.psum([128, 512], F32, "pB")
    pE = p.psum([128, 512], F32, "pE")
    psT = p.psum([128, 128], BF16, "psT")

    for cc in range(8):
        b = cc % 2
        sl = slice(cc * 128, (cc + 1) * 128)
        p.wait("sync", s_v)
        p.dma("sync", stg[b][:, 0:256], wq[sl, :], s_in)
        p.dma("sync", stg[b][:, 256:512], wk[sl, :], s_in)
        p.dma("sync", stg[b][:, 512:768], wv[sl, :], s_in)
        p.wait("vector", s_in)
        for h in range(NH):
            p.do("vector", "tensor_copy", WB[cc][:, h * 192:h * 192 + 64],
                 stg[b][:, h * 64:h * 64 + 64])
            p.do("vector", "tensor_copy", WB[cc][:, h * 192 + 64:h * 192 + 128],
                 stg[b][:, 256 + h * 64:256 + h * 64 + 64])
            p.do("vector", "tensor_copy", WB[cc][:, h * 192 + 128:h * 192 + 192],
                 stg[b][:, 512 + h * 64:512 + h * 64 + 64],
                 inc=(s_v if h == NH - 1 else None))
    for h in range(NH):
        p.dma("sync", bqkv[0:1, h * 192:h * 192 + 64],
              bq[0:1, h * 64:h * 64 + 64], s_in)
        p.dma("sync", bqkv[0:1, h * 192 + 64:h * 192 + 128],
              bk[0:1, h * 64:h * 64 + 64], s_in)
        p.dma("sync", bqkv[0:1, h * 192 + 128:h * 192 + 192],
              bv[0:1, h * 64:h * 64 + 64], s_in)
    p.wait("tensor", s_in)
    p.wait("tensor", c["sem"])
    for j in range(0, 768, 512):
        w = min(512, 768 - j)
        p.do("tensor", "matmul", pE[:, 0:w], c["ones_row_f"][:], bqkv[0:1, j:j + w],
             start=True, stop=True, inc=s_t)
        p.wait("vector", s_t)
        p.do("vector", "tensor_copy", bbc[:, j:j + w], pE[:, 0:w], inc=s_v)
        p.wait("tensor", s_v)

    def col(h, i):
        return colsc[:, h * 16 + i:h * 16 + i + 1]

    p.do("vector", "memset", sra_sb[:], 0.0)
    p.do("vector", "memset", srb_sb[:], 0.0)
    p.do("vector", "memset", ce_sb[:], 0.0, inc=s_v)

    first_kvbank_mm = [True]

    for ch in range(NC):
        t0 = ch * 128
        p.wait("gpsimd", s_v)
        p.do(g, "iota", nrm[:], [[1, 1]], base=t0 + 1, channel_multiplier=1,
             allow_small_or_imprecise_dtypes=True, inc=s_g)
        p.wait("vector", s_g)
        p.dodr("vector", "reciprocal", rnrm[:], nrm[:])
        buf = ch % 2
        p.wait("sync", s_v)
        p.dma("sync", xt_f[buf][:], xb[t0:t0 + 128, :], s_in)
        p.wait("vector", s_in)
        ln_token_major(p, xt_f[buf][:], lt["sq"][:], lt, s_v, s_s)
        p.wait("vector", s_v)
        p.wait("vector", s_t)
        p.do("vector", "tensor_copy", xbf[:], lt["sq"][:], inc=s_v)
        p.wait("tensor", s_v)
        for cc in range(8):
            p.do("tensor", "transpose", psT[:], xbf[:, cc * 128:(cc + 1) * 128],
                 c["ident"][:], inc=s_t)
            p.wait("scalar", s_t)
            p.do("scalar", "copy", xlnT_c[cc][:], psT[:], inc=s_s)
            p.wait("tensor", s_s)

        for h in range(NH):
            p.wait("tensor", s_s)
            p.wait("tensor", s_v)
            if "vector" in getattr(p, "_dr", {}):
                p.wait("tensor", p._dr["vector"])
            for cc in range(8):
                p.do("tensor", "matmul", pS[:, 128:320], xlnT_c[cc][:],
                     WB[cc][:, h * 192:(h + 1) * 192],
                     start=(cc == 0), stop=(cc == 7),
                     inc=(s_t if cc == 7 else None))
            p.wait("vector", s_t)
            p.dodr("vector", "tensor_add", qkv_f[h][:], pS[:, 128:320],
                   bbc[:, h * 192:(h + 1) * 192])
            p.wait("scalar", p._dr["vector"])
            p.do("scalar", "activation", qk_f[h][:], qkv_f[h][:, 0:128],
                 AF.Sigmoid, inc=s_s)
            p.wait("vector", s_s)
            p.do("vector", "tensor_copy", qk_b[h][:], qk_f[h][:], inc=s_v)
            p.wait("tensor", s_v)
            p.do("tensor", "transpose", psT[:], qk_b[h][:], c["ident"][:], inc=s_t)
            p.wait("scalar", s_t)
            p.do("scalar", "copy", qT_b[h][:], psT[0:64, :])
            p.do("scalar", "copy", kT_b[h][:], psT[64:128, :], inc=s_s)
            p.wait("tensor", s_s)

        for h in range(NH):
            p.wait("tensor", s_s)
            p.wait("tensor", s_v)
            p.do("tensor", "matmul", pC[:, 0:128], c["triu_f"][:], qk_f[h][:],
                 start=True, stop=False)
            p.do("tensor", "matmul", pC[:, 0:128], c["ones_row_f"][:],
                 sra_sb[0:1, h * 128:(h + 1) * 128], start=False, stop=True,
                 inc=s_t)
            p.do("tensor", "matmul", pA[0:1, h * 128:(h + 1) * 128],
                 c["ones_col_f"][:], qk_f[h][:],
                 start=(ch == 0 and h == 0), stop=(ch == NC - 1))
            p.wait("scalar", s_t)
            p.do("scalar", "activation", cum1[h][:], pC[:, 0:128], AF.Identity,
                 bias=c["eps6"][:], inc=s_s)
            p.wait("vector", s_s)
            p.do("vector", "scalar_tensor_tensor", sqs[h][:], qk_f[h][:, 0:64],
                 EPS, cum1[h][:, 64:128], ALU.add, ALU.mult, accum_out=col(h, 0),
                 inc=p._dr["vector"])
            p.wait("vector", p._dr["vector"])
            p.do("vector", "scalar_tensor_tensor", sqs[h][:], qk_f[h][:, 64:128],
                 EPS, cum1[h][:, 0:64], ALU.add, ALU.mult, accum_out=col(h, 1),
                 inc=p._dr["vector"])
            p.wait("vector", p._dr["vector"])
            p.dodr("vector", "reciprocal", col(h, 0), col(h, 0))
            p.dodr("vector", "reciprocal", col(h, 1), col(h, 1))
            p.dodr("vector", "tensor_mul", col(h, 0), col(h, 0), nrm[:])
            p.dodr("vector", "tensor_mul", col(h, 1), col(h, 1), nrm[:])
            p.dodr("vector", "tensor_scalar_mul", ksqs_f[h][:, 0:64],
                   qk_f[h][:, 64:128], col(h, 1))
            p.dodr("vector", "tensor_scalar_mul", ksqs_f[h][:, 64:128],
                   qk_f[h][:, 0:64], col(h, 0))
            p.wait("tensor", p._dr["vector"])
            p.wait("tensor", s_s)
            p.do("tensor", "matmul", pC[:, 128:256], c["triu_f"][:], ksqs_f[h][:],
                 start=True, stop=False)
            p.do("tensor", "matmul", pC[:, 128:256], c["ones_row_f"][:],
                 srb_sb[0:1, h * 128:(h + 1) * 128], start=False, stop=True,
                 inc=s_t)
            p.do("tensor", "matmul", pB[0:1, h * 128:(h + 1) * 128],
                 c["ones_col_f"][:], ksqs_f[h][:],
                 start=(ch == 0 and h == 0), stop=(ch == NC - 1))
            p.wait("scalar", s_t)
            p.do("scalar", "activation", cum2[h][:], pC[:, 128:256], AF.Identity,
                 bias=c["eps6"][:], inc=s_s)
            p.wait("vector", s_s)
            p.do("vector", "scalar_tensor_tensor", sqs[h][:], qk_f[h][:, 0:64],
                 EPS, cum2[h][:, 0:64], ALU.add, ALU.mult, accum_out=col(h, 2),
                 inc=p._dr["vector"])
            p.wait("vector", p._dr["vector"])
            p.do("vector", "scalar_tensor_tensor", sqs[h][:], qk_f[h][:, 64:128],
                 EPS, cum2[h][:, 64:128], ALU.add, ALU.mult, accum_out=col(h, 3),
                 inc=p._dr["vector"])
            p.wait("vector", p._dr["vector"])
            p.dodr("vector", "tensor_mul", col(h, 2), col(h, 2), rnrm[:])
            p.dodr("vector", "tensor_mul", col(h, 3), col(h, 3), rnrm[:])
            p.dodr("vector", "tensor_scalar", col(h, 3), col(h, 3), -1.0, 1.0,
                   ALU.max, ALU.min)
            p.wait("scalar", p._dr["vector"])
            p.do("scalar", "activation", col(h, 8), col(h, 2), AF.Sigmoid)
            p.do("scalar", "activation", col(h, 4), col(h, 3), AF.Exp, inc=s_s)
            p.wait("tensor", s_s)
            p.wait("tensor", p._dr["vector"])
            p.do("tensor", "matmul", pC[:, 256 + h:257 + h], c["triu_f"][:],
                 col(h, 4), start=True, stop=False)
            p.do("tensor", "matmul", pC[:, 256 + h:257 + h], c["ones_row_f"][:],
                 ce_sb[0:1, h:h + 1], start=False, stop=True, inc=s_t)
            p.do("tensor", "matmul", pKV[0:1, 256 + h:257 + h],
                 c["ones_col_f"][:], col(h, 4),
                 start=first_kvbank_mm[0], stop=(ch == NC - 1))
            first_kvbank_mm[0] = False
            p.wait("vector", s_t)
            p.do("vector", "tensor_copy", cex_sb[:, h:h + 1], pC[:, 256 + h:257 + h],
                 inc=s_v)
            p.wait("vector", s_v)
            p.dodr("vector", "reciprocal", col(h, 6), cex_sb[:, h:h + 1])
            p.dodr("vector", "tensor_mul", col(h, 6), col(h, 6), col(h, 4))
            p.dodr("vector", "tensor_mul", col(h, 6), col(h, 6), nrm[:])
            p.dodr("vector", "tensor_scalar_mul", vs_b[h][:], qkv_f[h][:, 128:192],
                   col(h, 6))
            p.wait("tensor", p._dr["vector"])
            p.wait("tensor", s_v)
            p.do("tensor", "matmul", pS[:, 0:128], kT_b[h][:], qT_b[h][:],
                 start=True, stop=True, inc=s_t)
            p.wait("vector", s_t)
            p.dodr("vector", "tensor_mul", A_b[h][:], pS[:, 0:128], c["triu_b"][:])
            if ch == 0:
                p.dodr("vector", "memset", kvprev_b[h][:], 0.0)
            else:
                p.dodr("vector", "tensor_copy", kvprev_b[h][:],
                       pKV[0:64, h * 64:(h + 1) * 64])
            p.wait("tensor", p._dr["vector"])
            p.wait("tensor", s_s)
            p.do("tensor", "matmul", pO[0:64, h * 128:(h + 1) * 128], vs_b[h][:],
                 A_b[h][:], start=True, stop=False)
            p.do("tensor", "matmul", pO[0:64, h * 128:(h + 1) * 128],
                 kvprev_b[h][:], qT_b[h][:], start=False, stop=True, inc=s_t)
            p.do("tensor", "matmul", pKV[0:64, h * 64:(h + 1) * 64],
                 qk_b[h][:, 64:128], vs_b[h][:], start=False,
                 stop=(ch == NC - 1))
            p.wait("vector", s_s)
            p.dodr("vector", "tensor_mul", col(h, 9), col(h, 8), col(h, 0))
            p.dodr("vector", "tensor_mul", col(h, 9), col(h, 9), rnrm[:])
            p.dodr("vector", "tensor_copy", col(h, 7), col(h, 9))
            p.wait("tensor", p._dr["vector"])
            p.do("tensor", "transpose", pE[0:1, 0:128], col(h, 7),
                 c["identf"][:], inc=s_t)
            p.wait("vector", s_t)
            p.do("vector", "tensor_copy", facT[:], pE[0:1, 0:128], inc=s_v)
            p.wait("tensor", s_v)
            p.do("tensor", "matmul", pE[0:64, 128:256], c["ones_row_f"][0:1, 0:64],
                 facT[:], start=True, stop=True, inc=s_t)
            p.wait("scalar", s_t)
            p.do("scalar", "copy", o_f[h][:], pO[0:64, h * 128:(h + 1) * 128],
                 inc=s_s)
            p.wait("vector", s_s)
            row0 = 64 * h
            dst = outS[0] if h < 2 else outS[1]
            rsl = slice((h % 2) * 64, (h % 2) * 64 + 64)
            p.do("vector", "tensor_mul",
                 dst[rsl, t0:t0 + 128], o_f[h][:], pE[0:64, 128:256], inc=s_v)
            p.wait("tensor", s_v)

        if DEBUG_L0A:
            p.wait("gpsimd", s_v)
            p.wait("gpsimd", s_s)
            p.dma("gpsimd", dbg_c[ch], colsc[:, 0:16], s_o)
            p.wait("vector", s_o)
        p.wait("vector", s_t)
        p.do("vector", "tensor_copy", sra_sb[:], pA[0:1, 0:512])
        p.do("vector", "tensor_copy", srb_sb[:], pB[0:1, 0:512])
        p.do("vector", "tensor_copy", ce_sb[:], pKV[0:1, 256:260], inc=s_v)
        if DEBUG_L0A:
            p.wait("gpsimd", s_v)
            p.dma("gpsimd", dbg_sra[ch], sra_sb[:], s_o)
            p.wait("vector", s_o)
        p.wait("tensor", s_v)

    p.wait("gpsimd", s_v)
    for j in range(4):
        cs = slice(2048 * j, 2048 * (j + 1))
        p.dma("gpsimd", outT[256 * j:256 * j + 128, :], outS[0][:, cs], s_o)
        p.dma("gpsimd", outT[256 * j + 128:256 * j + 256, :], outS[1][:, cs], s_o)
    p.wait("gpsimd", s_o)
    if fused:
        out_sem = (s_o, s_o.n)
        p.free_since(mark)
        return out_sem
    return p.finalize()



def _ca(a):
    return np.ascontiguousarray(a)


def kernel(**inp):
    if FUSED:
        return kernel_fused(**inp)
    TL = TLOC
    _ = inp.get("L0_attn_attn_mask")
    x = np.asarray(inp["L0_attn_inputs"], np.float32)

    nc = build_l0a()
    maps = []
    Wq, Wk, Wv = (np.asarray(inp[f"L0_attn_W{z}"], np.float32) for z in "qkv")
    bq, bk, bv = (np.asarray(inp[f"L0_attn_b{z}"], np.float32) for z in "qkv")
    for c in range(NCORE):
        b, gr = c // 4, c % 4
        sl = slice(256 * gr, 256 * (gr + 1))
        maps.append({"xb": _ca(x[b]), "wq": _ca(Wq[:, sl]), "wk": _ca(Wk[:, sl]),
                     "wv": _ca(Wv[:, sl]), "bq": _ca(bq[None, sl]),
                     "bk": _ca(bk[None, sl]), "bv": _ca(bv[None, sl])})
    res = run_bass_kernel_spmd(nc, maps, core_ids=list(range(NCORE)))
    xheadsT = [np.concatenate([res.results[4 * b + gr]["outT"] for gr in range(4)], 0)
               for b in range(2)]

    nc = build_l0b()
    maps = []
    for c in range(NCORE):
        b, j = c // 4, c % 4
        ts = slice(TL * j, TL * (j + 1))
        maps.append({
            "x": _ca(x[b][ts]), "hdT": _ca(xheadsT[b][:, ts]),
            "Wo": np.asarray(inp["L0_attn_Wo"], np.float32),
            "W1": np.asarray(inp["L0_attn_W1"], np.float32),
            "W2": np.asarray(inp["L0_attn_W2"], np.float32),
            "bo": _ca(np.asarray(inp["L0_attn_bo"], np.float32)[None, :]),
            "b1": _ca(np.asarray(inp["L0_attn_b1"], np.float32).reshape(32, 128)),
            "b2": _ca(np.asarray(inp["L0_attn_b2"], np.float32)[None, :]),
            "ln1g": _ca(np.asarray(inp["L0_attn_ln1_g"], np.float32)[None, :]),
            "ln1b": _ca(np.asarray(inp["L0_attn_ln1_b"], np.float32)[None, :]),
            "ln2g": _ca(np.asarray(inp["L0_attn_ln2_g"], np.float32)[None, :]),
            "ln2b": _ca(np.asarray(inp["L0_attn_ln2_b"], np.float32)[None, :]),
        })
    res = run_bass_kernel_spmd(nc, maps, core_ids=list(range(NCORE)))
    x1 = np.stack([np.concatenate([res.results[4 * b + j]["xout"] for j in range(4)], 0)
                   for b in range(2)])

    nc = build_flash("l0f")
    Wkv = np.asarray(inp["L0_ffn_Wkv"], np.float32)
    sel8c = np.kron(np.eye(8), np.ones((1, 64))).astype(np.float32).reshape(8, 512)
    maps = []
    for c in range(NCORE):
        b, j = c // 4, c % 4
        ts = slice(TL * j, TL * (j + 1))
        maps.append({
            "xb": _ca(x1[b]), "xloc": _ca(x1[b][ts]),
            "wq": np.asarray(inp["L0_ffn_Wq"], np.float32),
            "wk": _ca(Wkv[:, 0:512]), "wv": _ca(Wkv[:, 512:1024]),
            "wo": _ca(np.asarray(inp["L0_ffn_Wo"], np.float32).reshape(512, 1024)),
            "ws": np.asarray(inp["L0_ffn_Ws"], np.float32),
            "bo8": np.asarray(inp["L0_ffn_bo"], np.float32),
            "sel8c": sel8c,
        })
    res = run_bass_kernel_spmd(nc, maps, core_ids=list(range(NCORE)))
    x2 = np.stack([np.concatenate([res.results[4 * b + j]["xout"] for j in range(4)], 0)
                   for b in range(2)])

    nc = build_flash("l1a")
    wqkv = np.asarray(inp["L1_attn_w_qkv"], np.float32)
    maps = []
    for c in range(NCORE):
        b, j = c // 4, c % 4
        ts = slice(TL * j, TL * (j + 1))
        maps.append({
            "xb": _ca(x2[b]), "xloc": _ca(x2[b][ts]),
            "wq": _ca(wqkv[:, 0:1024]), "wk": _ca(wqkv[:, 1024:2048]),
            "wv": _ca(wqkv[:, 2048:3072]),
            "wo": np.asarray(inp["L1_attn_w_out"], np.float32),
        })
    res = run_bass_kernel_spmd(nc, maps, core_ids=list(range(NCORE)))
    x3 = np.stack([np.concatenate([res.results[4 * b + j]["xout"] for j in range(4)], 0)
                   for b in range(2)])

    nc = build_l1f()
    import ml_dtypes
    sel8w = np.kron(np.eye(8), np.ones((1, 128))).astype(
        ml_dtypes.bfloat16).reshape(8, 1024)
    base = {
        "gate_w": np.asarray(inp["L1_ffn_gate_w"], np.float32),
        "gbias": _ca(np.asarray(inp["L1_ffn_bias"], np.float32)[None, :]),
        "up_w": np.asarray(inp["L1_ffn_up_w"], np.float32),
        "up_b": np.asarray(inp["L1_ffn_up_b"], np.float32),
        "down_w": np.asarray(inp["L1_ffn_down_w"], np.float32),
        "down_b": np.asarray(inp["L1_ffn_down_b"], np.float32),
        "sw_up": np.asarray(inp["L1_ffn_sw_up"], np.float32),
        "sb_up": _ca(np.asarray(inp["L1_ffn_sb_up"], np.float32).reshape(32, 128)),
        "sw_down": np.asarray(inp["L1_ffn_sw_down"], np.float32),
        "sb_down": _ca(np.asarray(inp["L1_ffn_sb_down"], np.float32).reshape(8, 128)),
        "sel8w": sel8w,
    }
    maps = []
    for c in range(NCORE):
        b, j = c // 4, c % 4
        maps.append(dict(base, xloc=_ca(x3[b][TL * j:TL * (j + 1)])))
    res = run_bass_kernel_spmd(nc, maps, core_ids=list(range(NCORE)))
    x4 = np.stack([np.concatenate([res.results[4 * b + j]["xout"] for j in range(4)], 0)
                   for b in range(2)])
    return x4



RG8 = [[0, 1, 2, 3], [4, 5, 6, 7]]


def build_fused():
    p = Prog("mk")
    dram = p.nc.dram_tensor

    def I(n, sh, dt):
        return p.dram(n, sh, dt, "ExternalInput")

    xb0 = I("xb0", [S, D], F32)
    xloc0 = I("xloc0", [TLOC, D], F32)
    a_wq = I("a_wq", [D, 256], F32)
    a_wk = I("a_wk", [D, 256], F32)
    a_wv = I("a_wv", [D, 256], F32)
    a_bq = I("a_bq", [1, 256], F32)
    a_bk = I("a_bk", [1, 256], F32)
    a_bv = I("a_bv", [1, 256], F32)
    b_Wo = I("b_Wo", [D, D], F32)
    b_W1 = I("b_W1", [D, 4096], F32)
    b_W2 = I("b_W2", [4096, D], F32)
    b_bo = I("b_bo", [1, D], F32)
    b_b1 = I("b_b1", [32, 128], F32)
    b_b2 = I("b_b2", [1, D], F32)
    b_l1g = I("b_l1g", [1, D], F32)
    b_l1b = I("b_l1b", [1, D], F32)
    b_l2g = I("b_l2g", [1, D], F32)
    b_l2b = I("b_l2b", [1, D], F32)
    gidx = I("gidx", [128, 64], I32)
    f_wq = I("f_wq", [D, 512], F32)
    f_wk = I("f_wk", [D, 512], F32)
    f_wv = I("f_wv", [D, 512], F32)
    f_wo = I("f_wo", [512, D], F32)
    f_ws = I("f_ws", [D, 8], F32)
    f_bo8 = I("f_bo8", [8, D], F32)
    f_sel = I("f_sel", [8, 512], F32)
    g_wq = I("g_wq", [D, 1024], F32)
    g_wk = I("g_wk", [D, 1024], F32)
    g_wv = I("g_wv", [D, 1024], F32)
    g_wo = I("g_wo", [D, D], F32)
    m_gw = I("m_gw", [D, 8], F32)
    m_gb = I("m_gb", [1, 8], F32)
    m_uw = I("m_uw", [8, D, 4096], F32)
    m_ub = I("m_ub", [8, 4096], F32)
    m_dw = I("m_dw", [8, 4096, D], F32)
    m_db = I("m_db", [8, D], F32)
    m_su = I("m_su", [D, 4096], F32)
    m_sbu = I("m_sbu", [32, 128], F32)
    m_sd = I("m_sd", [4096, D], F32)
    m_sbd = I("m_sbd", [8, 128], F32)
    m_sel = I("m_sel", [8, 1024], BF16)
    yout = p.dram("yout", [TLOC, D], F32, "ExternalOutput")

    a_out = p.dram("a_out", [256, S], BF16, "Internal")
    gAll = p.dram("gAll", [D, S], BF16, "Internal")
    x1_loc = p.dram("x1_loc", [TLOC, D], F32, "Internal")
    x1_full = p.dram("x1_full", [S, D], F32, "Internal")
    x2_loc = p.dram("x2_loc", [TLOC, D], F32, "Internal")
    x2_full = p.dram("x2_full", [S, D], F32, "Internal")
    x3_loc = p.dram("x3_loc", [TLOC, D], F32, "Internal")

    s_cc = p.sem("cc")
    s_bar = p.sem("bar")

    def barrier():
        for e in ENGINES:
            p.do(e, "nop", inc=s_bar)
        v = s_bar.n
        for e in ENGINES:
            p.wait(e, s_bar, v)

    def collective(kind, src, dst, gate_sem):
        p.wait("gpsimd", *gate_sem)
        p.do("gpsimd", "collective_compute", kind, ALU.bypass, RG8,
             [src.opt()], [dst.opt()], inc=s_cc)

    a_shard = p.dram("a_shard", [D, TLOC], BF16, "Internal")
    p.pref = "A_"
    g1 = build_l0a(p, {"xb": xb0, "wq": a_wq, "wk": a_wk, "wv": a_wv,
                       "bq": a_bq, "bk": a_bk, "bv": a_bv, "outT": a_shard})
    p.pref = ""
    gAll2 = p.dram("gAll2", [4 * D, TLOC], BF16, "Internal")
    collective("AllGather", a_shard, gAll2, g1)
    barrier()
    p.pref = "B_"
    g2 = build_l0b(p, {"x": xloc0, "hdT_view": gAll2, "gidx": gidx,
                       "Wo": b_Wo, "W1": b_W1, "W2": b_W2, "bo": b_bo,
                       "b1": b_b1, "b2": b_b2, "ln1g": b_l1g, "ln1b": b_l1b,
                       "ln2g": b_l2g, "ln2b": b_l2b, "xout": x1_loc,
                       "gate": (s_cc, s_cc.n)})
    p.pref = ""
    collective("AllGather", x1_loc, x1_full, g2)
    barrier()
    p.pref = "F_"
    g3 = build_flash("l0f", p, {"xb": x1_full, "xloc": x1_loc,
                                "wq": f_wq, "wk": f_wk, "wv": f_wv, "wo": f_wo,
                                "ws": f_ws, "bo8": f_bo8, "sel8c": f_sel,
                                "xout": x2_loc, "gate": (s_cc, s_cc.n)})
    p.pref = ""
    collective("AllGather", x2_loc, x2_full, g3)
    barrier()
    p.pref = "G_"
    g4 = build_flash("l1a", p, {"xb": x2_full, "xloc": x2_loc,
                                "wq": g_wq, "wk": g_wk, "wv": g_wv, "wo": g_wo,
                                "xout": x3_loc, "gate": (s_cc, s_cc.n)})
    p.pref = ""
    barrier()
    p.pref = "M_"
    g5 = build_l1f(p, {"xloc": x3_loc, "gate": (g4[0], g4[1]),
                       "gate_w": m_gw, "gbias": m_gb, "up_w": m_uw,
                       "up_b": m_ub, "down_w": m_dw, "down_b": m_db,
                       "sw_up": m_su, "sb_up": m_sbu, "sw_down": m_sd,
                       "sb_down": m_sbd, "sel8w": m_sel, "xout": yout})
    p.pref = ""
    return p.finalize()


FUSED = True


def kernel_fused(**inp):
    TL = TLOC
    x = np.asarray(inp["L0_attn_inputs"], np.float32)
    _ = inp.get("L0_attn_attn_mask")
    import ml_dtypes
    nc = build_fused()
    Wq, Wk, Wv = (np.asarray(inp[f"L0_attn_W{z}"], np.float32) for z in "qkv")
    bq, bk, bv = (np.asarray(inp[f"L0_attn_b{z}"], np.float32) for z in "qkv")
    Wkv = np.asarray(inp["L0_ffn_Wkv"], np.float32)
    wqkv = np.asarray(inp["L1_attn_w_qkv"], np.float32)
    sel8c = np.kron(np.eye(8), np.ones((1, 64))).astype(np.float32).reshape(8, 512)
    sel8w = np.kron(np.eye(8), np.ones((1, 128))).astype(
        ml_dtypes.bfloat16).reshape(8, 1024)
    maps = []
    for c in range(NCORE):
        b, j = c // 4, c % 4
        sl = slice(256 * j, 256 * (j + 1))
        ts = slice(TL * j, TL * (j + 1))
        gi = np.empty((128, 64), np.int32)
        for cc in range(8):
            hr = 128 * cc + np.arange(128)
            base = 1024 * (hr // 256) + 256 * j + (hr % 256)
            for gg in range(8):
                gi[:, cc * 8 + gg] = base * 8 + gg
        maps.append({
            "xb0": _ca(x[b]), "xloc0": _ca(x[b][ts]),
            "a_wq": _ca(Wq[:, sl]), "a_wk": _ca(Wk[:, sl]), "a_wv": _ca(Wv[:, sl]),
            "a_bq": _ca(bq[None, sl]), "a_bk": _ca(bk[None, sl]),
            "a_bv": _ca(bv[None, sl]),
            "b_Wo": np.asarray(inp["L0_attn_Wo"], np.float32),
            "b_W1": np.asarray(inp["L0_attn_W1"], np.float32),
            "b_W2": np.asarray(inp["L0_attn_W2"], np.float32),
            "b_bo": _ca(np.asarray(inp["L0_attn_bo"], np.float32)[None, :]),
            "b_b1": _ca(np.asarray(inp["L0_attn_b1"], np.float32).reshape(32, 128)),
            "b_b2": _ca(np.asarray(inp["L0_attn_b2"], np.float32)[None, :]),
            "b_l1g": _ca(np.asarray(inp["L0_attn_ln1_g"], np.float32)[None, :]),
            "b_l1b": _ca(np.asarray(inp["L0_attn_ln1_b"], np.float32)[None, :]),
            "b_l2g": _ca(np.asarray(inp["L0_attn_ln2_g"], np.float32)[None, :]),
            "b_l2b": _ca(np.asarray(inp["L0_attn_ln2_b"], np.float32)[None, :]),
            "gidx": gi,
            "f_wq": np.asarray(inp["L0_ffn_Wq"], np.float32),
            "f_wk": _ca(Wkv[:, 0:512]), "f_wv": _ca(Wkv[:, 512:1024]),
            "f_wo": _ca(np.asarray(inp["L0_ffn_Wo"], np.float32).reshape(512, 1024)),
            "f_ws": np.asarray(inp["L0_ffn_Ws"], np.float32),
            "f_bo8": np.asarray(inp["L0_ffn_bo"], np.float32),
            "f_sel": sel8c,
            "g_wq": _ca(wqkv[:, 0:1024]), "g_wk": _ca(wqkv[:, 1024:2048]),
            "g_wv": _ca(wqkv[:, 2048:3072]),
            "g_wo": np.asarray(inp["L1_attn_w_out"], np.float32),
            "m_gw": np.asarray(inp["L1_ffn_gate_w"], np.float32),
            "m_gb": _ca(np.asarray(inp["L1_ffn_bias"], np.float32)[None, :]),
            "m_uw": np.asarray(inp["L1_ffn_up_w"], np.float32),
            "m_ub": np.asarray(inp["L1_ffn_up_b"], np.float32),
            "m_dw": np.asarray(inp["L1_ffn_down_w"], np.float32),
            "m_db": np.asarray(inp["L1_ffn_down_b"], np.float32),
            "m_su": np.asarray(inp["L1_ffn_sw_up"], np.float32),
            "m_sbu": _ca(np.asarray(inp["L1_ffn_sb_up"], np.float32).reshape(32, 128)),
            "m_sd": np.asarray(inp["L1_ffn_sw_down"], np.float32),
            "m_sbd": _ca(np.asarray(inp["L1_ffn_sb_down"], np.float32).reshape(8, 128)),
            "m_sel": sel8w,
        })
    res = run_bass_kernel_spmd(nc, maps, core_ids=list(range(NCORE)))
    return np.stack([np.concatenate([res.results[4 * b + j]["yout"]
                                     for j in range(4)], 0) for b in range(2)])


# version 06
# speedup vs baseline: 1.1514x; 1.1514x over previous
import numpy as np
from contextlib import ExitStack
import concourse.bass as bass
import concourse.mybir as mybir
from concourse.bass_utils import run_bass_kernel_spmd
from concourse.bass_core_rust import IndirectOffsetOnAxis

I32 = None

F32 = mybir.dt.float32
BF16 = mybir.dt.bfloat16
AF = mybir.ActivationFunctionType
ALU = mybir.AluOpType
AX = mybir.AxisListType
I32 = mybir.dt.int32

ENGINES = ("sync", "tensor", "vector", "scalar", "gpsimd")

B, S, D = 2, 8192, 1024
NCORE = 8
TLOC = B * S // NCORE


class Sem:
    __slots__ = ("h", "n")

    def __init__(self, h):
        self.h = h
        self.n = 0


class Prog:

    def __init__(self, name="prog"):
        self.nc = bass.Bass("TRN2", target_bir_lowering=False, debug=False)
        self.st = ExitStack()
        self.q = {e: [] for e in ENGINES}
        self._si = 0
        self._ni = 0

    pref = ""

    def sem(self, name=None):
        name = self.pref + (name or f"sm{self._si}")
        self._si += 1
        return Sem(self.st.enter_context(self.nc.semaphore(name)))

    def sbuf(self, shape, dt, name=None):
        name = self.pref + (name or f"sb{self._ni}")
        self._ni += 1
        cm = self.nc.sbuf_tensor(name, shape, dt)
        t = cm.__enter__()
        if not hasattr(self, "_open"):
            self._open = {}
        self._open[id(t)] = cm
        if not hasattr(self, "_order"):
            self._order = []
        self._order.append(t)
        return t

    def free(self, t):
        cm = self._open.pop(id(t))
        if hasattr(self, "_order"):
            try:
                self._order.remove(t)
            except ValueError:
                pass
        cm.__exit__(None, None, None)

    def marker(self):
        return len(getattr(self, "_order", []))

    def free_since(self, mark):
        order = getattr(self, "_order", [])
        while len(order) > mark:
            self.free(order[-1])

    def psum(self, shape, dt=F32, name=None):
        name = self.pref + (name or f"ps{self._ni}")
        self._ni += 1
        cm = self.nc.psum_tensor(name, shape, dt)
        t = cm.__enter__()
        if not hasattr(self, "_open"):
            self._open = {}
        self._open[id(t)] = cm
        if not hasattr(self, "_order"):
            self._order = []
        self._order.append(t)
        return t

    def dram(self, name, shape, dt, kind):
        return self.nc.dram_tensor(self.pref + name, shape, dt, kind=kind).ap()

    def do(self, eng, meth, *args, inc=None, inc_v=1, **kw):
        if inc is not None:
            inc.n += inc_v

            def f(E, meth=meth, args=args, kw=kw, h=inc.h, v=inc_v):
                getattr(E, meth)(*args, **kw).then_inc(h, v)
        else:
            def f(E, meth=meth, args=args, kw=kw):
                getattr(E, meth)(*args, **kw)
        self.q[eng].append(f)

    def wait(self, eng, sem, n=None):
        v = sem.n if n is None else n
        if v <= 0:
            return

        def f(E, h=sem.h, v=v):
            E.wait_ge(h, v)
        self.q[eng].append(f)

    def dma(self, eng, out, in_, sem=None, **kw):
        self.do(eng, "dma_start", inc=sem, inc_v=16, out=out, in_=in_, **kw)

    def dodr(self, eng, meth, *args, **kw):
        if not hasattr(self, "_dr"):
            self._dr = {}
        if eng not in self._dr:
            self._dr[eng] = self.sem(f"dr_{eng}")
        s = self._dr[eng]
        self.do(eng, meth, *args, inc=s, **kw)
        self.wait(eng, s)

    def finalize(self):
        with self.nc.Block() as blk:
            for e in ENGINES:
                ops = self.q[e]
                if ops:
                    getattr(blk, e)(lambda E, ops=ops: [f(E) for f in ops])
        for cm in reversed(list(getattr(self, "_open", {}).values())):
            cm.__exit__(None, None, None)
        self._open = {}
        self.st.close()
        return self.nc



def mk_consts(p):
    c = {}
    c["ones_row"] = p.sbuf([1, 128], BF16, "ones_row")
    c["ones_col"] = p.sbuf([128, 1], BF16, "ones_col")
    c["ident"] = p.sbuf([128, 128], BF16, "ident")
    c["identf"] = p.sbuf([128, 128], F32, "identf")
    c["eps_ln"] = p.sbuf([128, 1], F32, "eps_ln")
    c["eps6"] = p.sbuf([128, 1], F32, "eps6")
    g = "gpsimd"
    p.do(g, "memset", c["ones_row"][:], 1.0)
    p.do(g, "memset", c["ones_col"][:], 1.0)
    p.do(g, "memset", c["eps_ln"][:], 1e-5)
    p.do(g, "memset", c["eps6"][:], 1e-6)
    p.do(g, "memset", c["ident"][:], 1.0)
    p.do(g, "affine_select", c["ident"][:], c["ident"][:], [[1, 128]], ALU.is_equal,
         0.0, base=0, channel_multiplier=-1)
    p.do(g, "memset", c["identf"][:], 1.0)
    c["sem"] = p.sem("consts")
    p.do(g, "affine_select", c["identf"][:], c["identf"][:], [[1, 128]], ALU.is_equal,
         0.0, base=0, channel_multiplier=-1, inc=c["sem"])
    return c


def ln_token_major(p, x_ap, out_ap, tmp, sv, ss, g_bcast=None, b_bcast=None,
                   nfree=1024):
    v, s = "vector", "scalar"
    inv_n = float(1.0 / nfree)
    p.do(v, "reduce_sum", tmp["mean"][:], x_ap, axis=AX.X, inc=sv)
    p.wait(v, sv)
    p.do(v, "tensor_scalar_mul", tmp["mean"][:], tmp["mean"][:], inv_n, inc=sv)
    p.wait(v, sv)
    p.wait(v, ss)
    p.do(v, "tensor_scalar", tmp["xc"][:], x_ap, tmp["mean"][:], None,
         ALU.subtract, inc=sv)
    p.wait(s, sv)
    p.do(s, "activation", tmp["sq"][:], tmp["xc"][:], AF.Square,
         accum_out=tmp["ssq"][:], inc=ss)
    p.wait(s, ss)
    p.do(s, "activation", tmp["rstd"][:], tmp["ssq"][:], AF.Sqrt,
         bias=tmp["eps_ln"][:], scale=inv_n, inc=ss)
    p.wait(v, ss)
    p.do(v, "reciprocal", tmp["rstd"][:], tmp["rstd"][:], inc=sv)
    p.wait(v, sv)
    if g_bcast is None:
        p.do(v, "tensor_scalar_mul", out_ap, tmp["xc"][:], tmp["rstd"][:], inc=sv)
    else:
        p.dodr(v, "tensor_scalar_mul", tmp["xc"][:], tmp["xc"][:], tmp["rstd"][:])
        p.dodr(v, "tensor_mul", tmp["xc"][:], tmp["xc"][:], g_bcast)
        p.do(v, "tensor_add", out_ap, tmp["xc"][:], b_bcast, inc=sv)


def mk_ln_tmp(p, c, nfree=1024):
    return {
        "mean": p.sbuf([128, 1], F32), "ssq": p.sbuf([128, 1], F32),
        "rstd": p.sbuf([128, 1], F32), "xc": p.sbuf([128, nfree], F32),
        "sq": p.sbuf([128, nfree], F32), "eps_ln": c["eps_ln"],
    }


def bcast_row(p, c, row_ap, out_sbuf, ps_bank, sv, st, n=1024, eng="vector"):
    for j in range(0, n, 512):
        w = min(512, n - j)
        p.do("tensor", "matmul", ps_bank[:, 0:w], c["ones_row_f"][:],
             row_ap[:, j:j + w], start=True, stop=True, inc=st)
        p.wait(eng, st)
        p.do(eng, "tensor_copy", out_sbuf[:, j:j + w], ps_bank[:, 0:w], inc=sv)
        p.wait("tensor", sv)



DEBUG_L0B = False


def build_l0b(p=None, io=None):
    fused = p is not None
    if not fused:
        p = Prog("l0b")
    io = io or {}
    mark = p.marker()
    D_ = lambda n, sh, dt: io[n] if n in io else p.dram(n, sh, dt, "ExternalInput")
    x = D_("x", [TLOC, D], F32)
    if fused:
        hdT = io["hdT_view"].rearrange("r (w t) -> (r w) t", t=256)
        gidx = io["gidx"]
    else:
        hdT = D_("hdT", [D, TLOC], BF16)
    Wo = D_("Wo", [D, D], F32)
    W1 = D_("W1", [D, 4096], F32)
    W2 = D_("W2", [4096, D], F32)
    bo = D_("bo", [1, D], F32)
    b1 = D_("b1", [32, 128], F32)
    b2 = D_("b2", [1, D], F32)
    ln1g = D_("ln1g", [1, D], F32)
    ln1b = D_("ln1b", [1, D], F32)
    ln2g = D_("ln2g", [1, D], F32)
    ln2b = D_("ln2b", [1, D], F32)
    xout = io["xout"] if "xout" in io else p.dram("xout", [TLOC, D], F32,
                                                  "ExternalOutput")
    if "gate" in io:
        gs, gv = io["gate"]
        for e in ENGINES:
            p.wait(e, gs, gv)
    if DEBUG_L0B:
        dbg = {nm: p.dram("dbg_" + nm, [256, D], F32, "ExternalOutput")
               for nm in ("xln", "hpre", "h", "y")}

    c = mk_consts(p)
    c["ones_row_f"] = p.sbuf([1, 128], F32, "ones_row_f")
    p.do("gpsimd", "memset", c["ones_row_f"][:], 1.0, inc=c["sem"])

    s_in = p.sem("s_in")
    s_w2 = p.sem("s_w2")
    s_t = p.sem("s_t")
    s_v = p.sem("s_v")
    s_s = p.sem("s_s")
    s_o = p.sem("s_o")
    if fused:
        gidx_sb = p.sbuf([128, 64], I32, "gidx_sb")
        p.dma("gpsimd", gidx_sb[:], gidx, s_o)
        p.wait("gpsimd", s_o)

    WoB = [p.sbuf([128, D], BF16) for _ in range(8)]
    W1B = [p.sbuf([128, 4096], BF16) for _ in range(8)]
    bc = {}
    for nm in ("bo", "b2", "l1g", "l1b", "l2g", "l2b"):
        bc[nm] = p.sbuf([128, D], F32, f"bc_{nm}")
    b1c = p.sbuf([128, 32], F32, "b1c")
    GT = 256
    x_g = p.sbuf([128, 2 * 1024], F32, "x_g")
    xln_g = p.sbuf([128, 2 * 1024], F32, "xln_g")
    hpre_g = p.sbuf([128, 2 * 1024], F32, "hpre_g")
    h_g = p.sbuf([128, 2 * 1024], F32, "h_g")
    hbf_g = p.sbuf([128, 2 * 1024], BF16, "hbf_g")
    hdT_g = [p.sbuf([128, GT], BF16) for _ in range(8)]
    hT_g = [p.sbuf([128, GT], BF16) for _ in range(8)]
    uT = [p.sbuf([128, GT], BF16) for _ in range(2)]
    W2B = [p.sbuf([128, D], BF16) for _ in range(2)]
    w2stg = [p.sbuf([128, D], F32) for _ in range(2)]
    y_g = p.sbuf([128, 2 * 1024], F32, "y_g")
    lt = mk_ln_tmp(p, c)
    psA = p.psum([128, 512], F32, "psA")
    psT = p.psum([128, 128], BF16, "psT")
    psU = [p.psum([128, GT], F32) for _ in range(2)]
    psY = [p.psum([128, 512], F32) for _ in range(4)]

    rows = {"bo": x_g[0:1, 0:1024], "b2": xln_g[0:1, 0:1024],
            "l1g": hpre_g[0:1, 0:1024], "l1b": h_g[0:1, 0:1024],
            "l2g": y_g[0:1, 0:1024], "l2b": y_g[0:1, 1024:2048]}
    for nm, ap in (("bo", bo), ("b2", b2), ("l1g", ln1g), ("l1b", ln1b),
                   ("l2g", ln2g), ("l2b", ln2b)):
        p.dma("sync", rows[nm], ap, s_in)
    b1s = lt["xc"][0:32, 0:128]
    p.dma("sync", b1s, b1, s_in)

    w2c_cnt = []

    def cast_weight_chunks(dst_list, src, ncols):
        cnt = w2c_cnt
        for i, dst in enumerate(dst_list):
            for k in range(0, ncols, 1024):
                idx = len(cnt)
                buf = idx % 2
                if idx >= 2:
                    p.wait("sync", s_v, cnt[idx - 2])
                else:
                    p.wait("sync", s_v)
                p.dma("sync", w2stg[buf][:], src[i * 128:(i + 1) * 128, k:k + 1024],
                      s_in)
                p.wait("vector", s_in)
                p.do("vector", "tensor_copy", dst[:, k:k + 1024], w2stg[buf][:],
                     inc=s_v)
                cnt.append(s_v.n)
    cast_weight_chunks(WoB, Wo, 1024)
    cast_weight_chunks(W1B, W1, 4096)

    p.wait("tensor", s_in, 7 * 16)
    p.wait("tensor", c["sem"])
    p.do("tensor", "transpose", psA[:, 0:32], b1s, c["identf"][0:32, 0:32], inc=s_t)
    p.wait("vector", s_t)
    p.do("vector", "tensor_copy", b1c[:], psA[:, 0:32], inc=s_v)
    p.wait("tensor", s_v)

    for nm in ("bo", "b2", "l1g", "l1b", "l2g", "l2b"):
        bcast_row(p, c, rows[nm], bc[nm], psA, s_v, s_t)
    prep_sv = s_v.n

    NG = TLOC // GT

    def ts(ti):
        return slice(ti * 1024, (ti + 1) * 1024)

    for g in range(NG):
        t0 = g * GT
        if g == 0:
            p.wait("gpsimd", s_v, prep_sv)
            p.wait("sync", s_v, prep_sv)
        for ti in range(2):
            p.dma("sync", x_g[:, ts(ti)], x[t0 + ti * 128:t0 + (ti + 1) * 128, :], s_in)
        for cc in range(8):
            if fused:
                p.do("gpsimd", "indirect_dma_start", hdT_g[cc][:], None,
                     hdT,
                     IndirectOffsetOnAxis(gidx_sb[:, cc * 8 + g:cc * 8 + g + 1], 0),
                     inc=s_o, inc_v=16)
            else:
                p.dma("gpsimd", hdT_g[cc][:], hdT[cc * 128:(cc + 1) * 128, t0:t0 + GT],
                      s_o)

        for ti in range(2):
            p.wait("vector", s_in)
            ln_token_major(p, x_g[:, ts(ti)], xln_g[:, ts(ti)], lt, s_v, s_s)

        p.wait("tensor", s_o)
        p.wait("tensor", s_v)
        for ti in range(2):
            for f in range(2):
                ps = psY[2 * ti + f]
                for cc in range(8):
                    p.do("tensor", "matmul", ps[:],
                         hdT_g[cc][:, ti * 128:(ti + 1) * 128],
                         WoB[cc][:, f * 512:(f + 1) * 512],
                         start=(cc == 0), stop=(cc == 7), inc=(s_t if cc == 7 else None))
            p.wait("vector", s_t)
            for f in range(2):
                ps = psY[2 * ti + f]
                fs = slice(ti * 1024 + f * 512, ti * 1024 + (f + 1) * 512)
                p.dodr("vector", "tensor_add", hpre_g[:, fs], ps[:],
                       bc["bo"][:, f * 512:(f + 1) * 512])
                p.do("vector", "tensor_add", hpre_g[:, fs], hpre_g[:, fs],
                     xln_g[:, fs], inc=s_v)
            ln_token_major(p, hpre_g[:, ts(ti)], h_g[:, ts(ti)], lt, s_v, s_s,
                           g_bcast=bc["l1g"][:], b_bcast=bc["l1b"][:])
            p.wait("vector", s_v)
            p.do("vector", "tensor_copy", hbf_g[:, ts(ti)], h_g[:, ts(ti)], inc=s_v)
        p.wait("tensor", s_v)
        for cc in range(8):
            for ti in range(2):
                p.do("tensor", "transpose", psT[:],
                     hbf_g[:, ti * 1024 + cc * 128:ti * 1024 + (cc + 1) * 128],
                     c["ident"][:], inc=s_t)
                p.wait("scalar", s_t)
                p.do("scalar", "copy", hT_g[cc][:, ti * 128:(ti + 1) * 128], psT[:],
                     inc=s_s)
                p.wait("tensor", s_s)

        for i in range(32):
            buf = i % 2
            sidx = len(w2c_cnt)
            sb = sidx % 2
            p.wait("sync", s_v, w2c_cnt[sidx - 2])
            p.dma("sync", w2stg[sb][:], W2[i * 128:(i + 1) * 128, :], s_w2)
            p.wait("vector", s_w2)
            p.wait("vector", s_t)
            p.do("vector", "tensor_copy", W2B[buf][:], w2stg[sb][:], inc=s_v)
            w2c_cnt.append(s_v.n)
            p.wait("tensor", s_s)
            for cc in range(8):
                p.do("tensor", "matmul", psU[buf][:], W1B[cc][:, i * 128:(i + 1) * 128],
                     hT_g[cc][:], start=(cc == 0), stop=(cc == 7),
                     inc=(s_t if cc == 7 else None))
            p.wait("scalar", s_t)
            p.do("scalar", "activation", uT[buf][:], psU[buf][:], AF.Gelu,
                 bias=b1c[:, i:i + 1], inc=s_s)
            p.wait("tensor", s_s)
            p.wait("tensor", s_v)
            for ti in range(2):
                for f in range(2):
                    p.do("tensor", "matmul", psY[2 * ti + f][:],
                         uT[buf][:, ti * 128:(ti + 1) * 128],
                         W2B[buf][:, f * 512:(f + 1) * 512],
                         start=(i == 0), stop=(i == 31),
                         inc=(s_t if i == 31 else None))
        p.wait("vector", s_t)
        if DEBUG_L0B and g == 0:
            p.wait("gpsimd", s_v)
            for ti in range(2):
                tsl = slice(ti * 128, (ti + 1) * 128)
                p.dma("gpsimd", dbg["xln"][tsl, :], xln_g[:, ts(ti)], s_o)
                p.dma("gpsimd", dbg["hpre"][tsl, :], hpre_g[:, ts(ti)], s_o)
                p.dma("gpsimd", dbg["h"][tsl, :], h_g[:, ts(ti)], s_o)
        for ti in range(2):
            for f in range(2):
                fs = slice(ti * 1024 + f * 512, ti * 1024 + (f + 1) * 512)
                p.dodr("vector", "tensor_add", y_g[:, fs], psY[2 * ti + f][:],
                       bc["b2"][:, f * 512:(f + 1) * 512])
            p.do("vector", "tensor_add", y_g[:, ts(ti)], y_g[:, ts(ti)],
                 h_g[:, ts(ti)], inc=s_v)
            if DEBUG_L0B and g == 0:
                p.wait("gpsimd", s_v)
                p.dma("gpsimd", dbg["y"][ti * 128:(ti + 1) * 128, :],
                      y_g[:, ts(ti)], s_o)
                p.wait("vector", s_o)
            ln_token_major(p, y_g[:, ts(ti)], y_g[:, ts(ti)], lt, s_v, s_s,
                           g_bcast=bc["l2g"][:], b_bcast=bc["l2b"][:])
            p.wait("vector", s_v)
            p.do("vector", "tensor_add", y_g[:, ts(ti)], y_g[:, ts(ti)],
                 x_g[:, ts(ti)], inc=s_v)
            p.wait("gpsimd", s_v)
            p.dma("gpsimd", xout[t0 + ti * 128:t0 + (ti + 1) * 128, :],
                  y_g[:, ts(ti)], s_o)
        p.wait("sync", s_v)
        p.wait("gpsimd", s_o)

    p.wait("gpsimd", s_o)
    if fused:
        out_sem = (s_o, s_o.n)
        p.free_since(mark)
        return out_sem
    return p.finalize()



DEBUG_FLASH = False
DEBUG_FLASH2 = False
FLASH_STOP_AFTER = None
L1F_STOP = None
L1F_NE = 9
DEBUG_L1F = False
DEBUG_L0A = False


def build_flash(kind, p=None, io=None):
    l1a = kind == "l1a"
    H = 16 if l1a else 8
    QF = 64 * H
    NP = H // 2
    SB = S
    fused = p is not None
    if not fused:
        p = Prog(kind)
    io = io or {}
    mark = p.marker()
    D_ = lambda n, sh, dt: io[n] if n in io else p.dram(n, sh, dt, "ExternalInput")
    xb = D_("xb", [SB, D], F32)
    xloc = D_("xloc", [TLOC, D], F32)
    wq = D_("wq", [D, QF], F32)
    wk = D_("wk", [D, QF], F32)
    wv = D_("wv", [D, QF], F32)
    wo = D_("wo", [D, D] if l1a else [QF, D], F32)
    if not l1a:
        ws = D_("ws", [D, 8], F32)
        bo8 = D_("bo8", [8, D], F32)
    xout = io["xout"] if "xout" in io else p.dram("xout", [TLOC, D], F32,
                                                  "ExternalOutput")
    if "gate" in io:
        gs, gv = io["gate"]
        for e in ENGINES:
            p.wait(e, gs, gv)
    if DEBUG_FLASH:
        dbg_out = p.dram("dbg_outT", [QF, TLOC], F32, "ExternalOutput")
        dbg_q = p.dram("dbg_qT", [QF, TLOC], F32, "ExternalOutput")
    if DEBUG_FLASH2:
        dbg_kts = p.dram("dbg_kts", [QF, SB], BF16, "ExternalOutput")
        dbg_vsc = p.dram("dbg_vsc", [H, SB // 128, 128, 64], BF16, "ExternalOutput")
        dbg_q2 = p.dram("dbg_qT2", [QF, TLOC], BF16, "ExternalOutput")
    kts = p.dram("kts", [QF, SB], BF16, "Internal")
    vsc = p.dram("vsc", [H, SB // 128, 128, 64], BF16, "Internal")

    c = mk_consts(p)
    c["ones_row_f"] = p.sbuf([1, 128], F32, "ones_row_f")
    p.do("gpsimd", "memset", c["ones_row_f"][:], 1.0, inc=c["sem"])
    if not l1a:
        sel8_in = io["sel8c"] if "sel8c" in io else p.dram("sel8c", [8, 512], F32,
                                                           "ExternalInput")
        c["sel8"] = p.sbuf([8, 512], F32, "sel8")

    s_in = p.sem("s_in")
    s_vl = p.sem("s_vl")
    s_kv = p.sem("s_kv")
    s_t = p.sem("s_t")
    s_v = p.sem("s_v")
    s_s = p.sem("s_s")
    s_o = p.sem("s_o")
    if not l1a:
        p.dma("sync", c["sel8"][:], sel8_in, s_in)

    QTp = [p.sbuf([128, TLOC], BF16) for _ in range(NP)]
    outTp = [p.sbuf([128, TLOC], BF16) for _ in range(NP)]
    KTp = p.sbuf([128, SB], BF16, "KTp")
    V1 = p.sbuf([128, 64 * 65], BF16, "V1")
    Pt = [p.sbuf([128, 1024], BF16) for _ in range(2)]
    xlnT_g = [p.sbuf([128, 512], BF16) for _ in range(8)]
    xt_f = [p.sbuf([128, 1024], F32) for _ in range(2)]
    xln_f = p.sbuf([128, 1024], F32, "xln_f")
    xbf = p.sbuf([128, 1024], BF16, "xbf")
    evac = p.sbuf([128, 512], BF16, "evac")
    otmp = p.sbuf([64, 512], F32, "otmp")
    otmp2 = p.sbuf([64, 512], F32, "otmp2")
    rd = p.sbuf([1, 512], F32, "rd")
    lt = mk_ln_tmp(p, c)
    if not l1a:
        s_tok = p.sbuf([128, 8], F32, "s_tok")
        g_m1 = p.sbuf([128, 1], F32)
        g_t1 = p.sbuf([128, 8], F32)
        g_t2 = p.sbuf([128, 8], F32)
        g_bf = p.sbuf([128, 8], BF16)
        gatedT = p.sbuf([8, TLOC], F32, "gatedT")
        gatedT_bf = p.sbuf([8, TLOC], BF16, "gatedT_bf")
        bo8B = p.sbuf([8, D], BF16, "bo8B")

    psT = p.psum([128, 128], BF16, "psT")
    psA = p.psum([128, 512], F32, "psA")
    psB = p.psum([128, 512], F32, "psB")
    psC = p.psum([128, 512], F32, "psC")

    WqB = [p.sbuf([128, QF], BF16) for _ in range(8)]
    WkB = [p.sbuf([128, QF], BF16) for _ in range(8)]
    WvB = [p.sbuf([128, QF], BF16) for _ in range(8)]
    wstg = [p.sbuf([128, 1024], F32) for _ in range(2)]
    if not l1a:
        WsB = [p.sbuf([128, 8], BF16) for _ in range(8)]

    _cstC = []

    def cast_chunks(dst_list, src, ncols):
        for i, dst in enumerate(dst_list):
            for k in range(0, ncols, 1024):
                idx = len(_cstC)
                buf = idx % 2
                if idx >= 2:
                    p.wait("sync", s_v, _cstC[idx - 2])
                else:
                    p.wait("sync", s_v)
                p.dma("sync", wstg[buf][:, 0:min(1024, ncols)],
                      src[i * 128:(i + 1) * 128, k:k + min(1024, ncols)], s_in)
                p.wait("vector", s_in)
                p.do("vector", "tensor_copy", dst[:, k:k + min(1024, ncols)],
                     wstg[buf][:, 0:min(1024, ncols)], inc=s_v)
                _cstC.append(s_v.n)

    cast_chunks(WqB, wq, QF)
    cast_chunks(WkB, wk, QF)
    cast_chunks(WvB, wv, QF)
    if not l1a:
        for i in range(8):
            p.wait("sync", s_v)
            p.dma("sync", wstg[i % 2][:, 0:8], ws[i * 128:(i + 1) * 128, :], s_in)
            p.wait("vector", s_in)
            p.do("vector", "tensor_copy", WsB[i][:], wstg[i % 2][:, 0:8], inc=s_v)
        p.dma("gpsimd", bo8B[:], bo8, s_vl)

    def stream_tiles(src, t0, nt, emit_consumer, local_q):
        for q in range(nt):
            buf = q % 2
            p.wait("sync", s_v)
            p.dma("sync", xt_f[buf][:], src[t0 + q * 128:t0 + (q + 1) * 128, :], s_in)
            p.wait("vector", s_in)
            ln_token_major(p, xt_f[buf][:], xln_f[:], lt, s_v, s_s)
            p.wait("vector", s_v)
            p.wait("vector", s_t)
            p.do("vector", "tensor_copy", xbf[:], xln_f[:], inc=s_v)
            p.wait("tensor", s_v)
            p.wait("tensor", c["sem"])
            for cc in range(8):
                p.do("tensor", "transpose", psT[:], xbf[:, cc * 128:(cc + 1) * 128],
                     c["ident"][:], inc=s_t)
                p.wait("scalar", s_t)
                p.do("scalar", "copy", xlnT_g[cc][:, q * 128:(q + 1) * 128], psT[:],
                     inc=s_s)
                p.wait("tensor", s_s)
        emit_consumer()

    def kv_consumer(t0):
        for kf in range(QF // 128):
            p.wait("tensor", s_s)
            for cc in range(8):
                p.do("tensor", "matmul", psA[:], WkB[cc][:, kf * 128:(kf + 1) * 128],
                     xlnT_g[cc][:], start=(cc == 0), stop=(cc == 7),
                     inc=(s_t if cc == 7 else None))
            p.wait("scalar", s_t)
            p.wait("scalar", s_kv)
            p.do("scalar", "copy", evac[:], psA[:], inc=s_s)
            p.wait("gpsimd", s_s)
            p.dma("gpsimd", kts[kf * 128:(kf + 1) * 128, t0:t0 + 512], evac[:], s_kv)
        for q in range(4):
            for half in range(QF // 512):
                fs = slice(half * 512, (half + 1) * 512)
                p.wait("tensor", s_s)
                for cc in range(8):
                    p.do("tensor", "matmul", psB[:],
                         xlnT_g[cc][:, q * 128:(q + 1) * 128], WvB[cc][:, fs],
                         start=(cc == 0), stop=(cc == 7),
                         inc=(s_t if cc == 7 else None))
                p.wait("scalar", s_t)
                p.wait("scalar", s_kv)
                p.do("scalar", "copy", evac[:], psB[:], inc=s_s)
                p.wait("gpsimd", s_s)
                j = (t0 + q * 128) // 128
                p.dma("gpsimd",
                      vsc[half * 8:(half + 1) * 8, j].rearrange("h p d -> p h d"),
                      evac[:].rearrange("p (h d) -> p h d", d=64),
                      s_kv)

    for g in range(SB // 512):
        stream_tiles(xb, g * 512, 4, lambda g=g: kv_consumer(g * 512), False)

    def q_consumer(t0):
        for qf in range(QF // 128):
            p.wait("tensor", s_s)
            for cc in range(8):
                p.do("tensor", "matmul", psA[:], WqB[cc][:, qf * 128:(qf + 1) * 128],
                     xlnT_g[cc][:], start=(cc == 0), stop=(cc == 7),
                     inc=(s_t if cc == 7 else None))
            p.wait("scalar", s_t)
            p.do("scalar", "activation", QTp[qf][:, t0:t0 + 512], psA[:], AF.Copy,
                 scale=0.125, inc=s_s)
            p.wait("tensor", s_s)
        if (not l1a) and FLASH_STOP_AFTER != "nogate":
            for q in range(4):
                tsl = slice(t0 + q * 128, t0 + (q + 1) * 128)
                for cc in range(8):
                    p.do("tensor", "matmul", psC[0:128, 0:8],
                         xlnT_g[cc][:, q * 128:(q + 1) * 128], WsB[cc][:],
                         start=(cc == 0), stop=(cc == 7),
                         inc=(s_t if cc == 7 else None))
                p.wait("scalar", s_t)
                p.do("scalar", "activation", s_tok[:], psC[0:128, 0:8], AF.Sigmoid,
                     inc=s_s)
                if FLASH_STOP_AFTER == "gate1":
                    continue
                v = "vector"
                p.wait(v, s_s)
                p.dodr(v, "reduce_max", g_m1[:], s_tok[:], axis=AX.X)
                p.dodr(v, "tensor_scalar", g_t1[:], s_tok[:], g_m1[:], None,
                       ALU.is_equal)
                p.dodr(v, "scalar_tensor_tensor", g_t2[:], g_t1[:], -1e30, s_tok[:],
                       ALU.mult, ALU.add)
                p.dodr(v, "reduce_max", g_m1[:], g_t2[:], axis=AX.X)
                p.dodr(v, "tensor_scalar", g_t1[:], g_t2[:], g_m1[:], None,
                       ALU.is_equal)
                p.dodr(v, "scalar_tensor_tensor", g_t2[:], g_t1[:], -1e30, g_t2[:],
                       ALU.mult, ALU.add)
                p.dodr(v, "reduce_max", g_m1[:], g_t2[:], axis=AX.X)
                p.dodr(v, "tensor_scalar", g_t1[:], s_tok[:], g_m1[:], None,
                       ALU.is_ge)
                p.dodr(v, "tensor_mul", g_t2[:], s_tok[:], g_t1[:])
                p.do(v, "tensor_copy", g_bf[:], g_t2[:], inc=s_v)
                if FLASH_STOP_AFTER == "gate2":
                    continue
                p.wait("tensor", s_v)
                p.do("tensor", "transpose", psT[0:8, :], g_bf[:], c["ident"][:],
                     inc=s_t)
                p.wait("scalar", s_t)
                p.do("scalar", "copy", gatedT[0:8, tsl], psT[0:8, :], inc=s_s)
                p.wait("vector", s_s)
                p.do("vector", "tensor_copy", gatedT_bf[0:8, tsl], gatedT[0:8, tsl],
                     inc=s_v)
                p.wait("tensor", s_v)
                p.wait("tensor", s_s)

    if FLASH_STOP_AFTER == "p2":
        p.wait("gpsimd", s_kv, s_kv.n)
        p.dma("gpsimd", xout[0:128, :], xt_f[0][:], s_o)
        p.wait("gpsimd", s_o)
        return p.finalize()
    for g in range(TLOC // 512):
        stream_tiles(xloc, g * 512, 4,
                     lambda g=g: q_consumer(g * 512), True)

    kv_done = s_kv.n
    q_done = s_s.n
    if DEBUG_FLASH2:
        p.wait("gpsimd", s_kv, kv_done)
        p.wait("gpsimd", s_s, q_done)
        p.dma("gpsimd", dbg_kts, kts, s_o)
        p.dma("gpsimd", dbg_vsc, vsc, s_o)
        for j in range(NP):
            p.dma("gpsimd", dbg_q2[j * 128:(j + 1) * 128, :], QTp[j][:], s_o)
    if not l1a:
        for t in reversed(WsB):
            p.free(t)
    for t in reversed(wstg):
        p.free(t)
    for t in reversed(WvB):
        p.free(t)
    for t in reversed(WkB):
        p.free(t)
    for t in reversed(WqB):
        p.free(t)
    psW = [p.psum([128, 1024], F32) for _ in range(2)]
    WoB_n = 8 if l1a else 4
    WoB = [p.sbuf([128, D], BF16) for _ in range(WoB_n)]
    wstg2 = [p.sbuf([128, 1024], F32) for _ in range(2)]
    p.wait("sync", s_t)
    for i in range(WoB_n):
        buf = i % 2
        p.wait("sync", s_v)
        p.dma("sync", wstg2[buf][:], wo[i * 128:(i + 1) * 128, :], s_in)
        p.wait("vector", s_in)
        p.do("vector", "tensor_copy", WoB[i][:], wstg2[buf][:], inc=s_v)

    p.do("gpsimd", "memset",
         V1[:].rearrange("p (j x) -> p j x", x=65)[:, :, 64:65], 1.0, inc=s_kv,
         inc_v=1)
    ones_done = s_kv.n

    if FLASH_STOP_AFTER == "p3":
        p.wait("gpsimd", s_kv, kv_done)
        p.wait("gpsimd", s_s, q_done)
        p.dma("gpsimd", xout[0:128, :], xt_f[0][:], s_o)
        p.wait("gpsimd", s_o)
        return p.finalize()

    exp_cnt = []
    pv_cnt = []
    p.wait("sync", s_kv, kv_done)
    p.wait("gpsimd", s_kv, kv_done)
    for hp in range(NP):
        p.wait("sync", s_t)
        p.dma("sync", KTp[:], kts[hp * 128:(hp + 1) * 128, :], s_in)
        for hh in range(2):
            h = 2 * hp + hh
            base = 64 * hh
            p.wait("gpsimd", s_t)
            p.wait("gpsimd", s_kv, ones_done)
            p.do("gpsimd", "dma_start", inc=s_vl, inc_v=16,
                 out=V1[:].rearrange("p (j x) -> p j x", x=65)[:, :, 0:64],
                 in_=vsc[h].rearrange("j p d -> p j d"))
            for qtp in range(TLOC // 1024):
                q0 = qtp * 1024
                p.wait("tensor", s_v)
                p.wait("tensor", s_in)
                p.wait("tensor", s_vl)
                exp_cnt.clear()
                NJ = SB // 128

                def emit_scores(j):
                    b = j % 2
                    if j >= 2:
                        p.wait("tensor", s_s, exp_cnt[j - 2])
                    for half in range(2):
                        p.do("tensor", "matmul",
                             psW[b][:, half * 512:(half + 1) * 512],
                             KTp[base:base + 64, j * 128:(j + 1) * 128],
                             QTp[hp][base:base + 64,
                                     q0 + half * 512:q0 + (half + 1) * 512],
                             start=True, stop=True,
                             inc=(s_t if half == 1 else None))

                def emit_exp(j):
                    b = j % 2
                    p.wait("scalar", s_t)
                    p.do("scalar", "activation", Pt[b][:], psW[b][:], AF.Exp,
                         inc=s_s)
                    exp_cnt.append(s_s.n)

                def emit_pv(j):
                    b = j % 2
                    p.wait("tensor", s_s, exp_cnt[j])
                    for half in range(2):
                        p.do("tensor", "matmul",
                             (psA if half == 0 else psB)[0:65, :],
                             V1[:, j * 65:j * 65 + 65],
                             Pt[b][:, half * 512:(half + 1) * 512],
                             start=(j == 0), stop=(j == NJ - 1),
                             inc=(s_t if half == 1 else None))

                emit_scores(0)
                emit_exp(0)
                for j in range(1, NJ):
                    emit_scores(j)
                    emit_exp(j)
                    emit_pv(j - 1)
                emit_pv(NJ - 1)
                for half in range(2):
                    ps = psA if half == 0 else psB
                    qsl = slice(q0 + half * 512, q0 + (half + 1) * 512)
                    p.wait("scalar", s_t)
                    p.wait("scalar", s_v)
                    p.do("scalar", "copy", otmp[:], ps[0:64, :], inc=s_s)
                    p.wait("vector", s_t)
                    p.do("vector", "reciprocal", rd[:], ps[64:65, :], inc=s_v)
                    p.wait("tensor", s_v)
                    p.wait("tensor", s_s)
                    p.wait("vector", s_s)
                    if l1a:
                        p.do("tensor", "matmul", psC[0:64, :], c["ones_row_f"][0:1, 0:64],
                             rd[:], start=True, stop=True, inc=s_t)
                        p.wait("vector", s_t)
                        p.do("vector", "tensor_mul", outTp[hp][base:base + 64, qsl],
                             otmp[:], psC[0:64, :], inc=s_v)
                    else:
                        p.do("tensor", "matmul", psC[0:64, :],
                             c["sel8"][:, h * 64:(h + 1) * 64], gatedT[0:8, qsl],
                             start=True, stop=True, inc=s_t)
                        p.wait("vector", s_t)
                        p.do("vector", "tensor_mul", otmp2[:], otmp[:], psC[0:64, :],
                             inc=s_v)
                        p.wait("vector", s_v)
                        p.wait("tensor", s_v)
                        p.do("tensor", "matmul", psC[0:64, :], c["ones_row_f"][0:1, 0:64],
                             rd[:], start=True, stop=True, inc=s_t)
                        p.wait("vector", s_t)
                        p.do("vector", "tensor_mul", outTp[hp][base:base + 64, qsl],
                             otmp2[:], psC[0:64, :], inc=s_v)
                    p.wait("tensor", s_v)

    if DEBUG_FLASH:
        dstg = p.sbuf([128, TLOC], F32, "dstg")
        for j in range(NP):
            p.wait("vector", s_o)
            p.do("vector", "tensor_copy", dstg[:], outTp[j][:], inc=s_v)
            p.wait("gpsimd", s_v)
            p.dma("gpsimd", dbg_out[j * 128:(j + 1) * 128, :], dstg[:], s_o)
            p.wait("vector", s_o)
            p.do("vector", "tensor_copy", dstg[:], QTp[j][:], inc=s_v)
            p.wait("gpsimd", s_v)
            p.dma("gpsimd", dbg_q[j * 128:(j + 1) * 128, :], dstg[:], s_o)
        dbg_wo = p.dram("dbg_wo", [WoB_n * 128, D], F32, "ExternalOutput")
        for j in range(WoB_n):
            p.wait("vector", s_o)
            p.do("vector", "tensor_copy", dstg[:, 0:1024], WoB[j][:], inc=s_v)
            p.wait("gpsimd", s_v)
            p.dma("gpsimd", dbg_wo[j * 128:(j + 1) * 128, :], dstg[:, 0:1024], s_o)
        p.wait("vector", s_o)

    if FLASH_STOP_AFTER == "flash":
        p.wait("gpsimd", s_v)
        p.dma("gpsimd", xout[0:128, :], xt_f[0][:], s_o)
        p.wait("gpsimd", s_o)
        return p.finalize()

    for t in range(TLOC // 128):
        buf = t % 2
        p.wait("sync", s_v)
        p.dma("sync", xt_f[buf][:], xloc[t * 128:(t + 1) * 128, :], s_in)
        for f in range(2):
            ps = psW[f][:, 0:512]
            p.wait("tensor", s_v)
            p.wait("tensor", s_in)
            p.wait("tensor", s_vl)
            for j in range(NP):
                p.do("tensor", "matmul", ps, outTp[j][:, t * 128:(t + 1) * 128],
                     WoB[j][:, f * 512:(f + 1) * 512],
                     start=(j == 0), stop=(j == NP - 1 and l1a),
                     inc=(s_t if (j == NP - 1 and l1a) else None))
            if not l1a:
                p.do("tensor", "matmul", ps, gatedT_bf[0:8, t * 128:(t + 1) * 128],
                     bo8B[0:8, f * 512:(f + 1) * 512], start=False, stop=True,
                     inc=s_t)
            p.wait("vector", s_t)
            p.wait("vector", s_in)
            p.wait("vector", s_o)
            p.do("vector", "tensor_add", lt["xc"][:, 0:512], ps,
                 xt_f[buf][:, f * 512:(f + 1) * 512], inc=s_v)
            p.wait("gpsimd", s_v)
            p.dma("gpsimd", xout[t * 128:(t + 1) * 128, f * 512:(f + 1) * 512],
                  lt["xc"][:, 0:512], s_o)
    p.wait("gpsimd", s_o)
    if fused:
        out_sem = (s_o, s_o.n)
        p.free_since(mark)
        return out_sem
    return p.finalize()



def build_l1f(p=None, io=None):
    fused = p is not None
    if not fused:
        p = Prog("l1f")
    io = io or {}
    mark = p.marker()
    D_ = lambda n, sh, dt: io[n] if n in io else p.dram(n, sh, dt, "ExternalInput")
    xloc = D_("xloc", [TLOC, D], F32)
    gate_w = D_("gate_w", [D, 8], F32)
    gbias = D_("gbias", [1, 8], F32)
    up_w = D_("up_w", [8, D, 4096], F32)
    up_b = D_("up_b", [8, 4096], F32)
    down_w = D_("down_w", [8, 4096, D], F32)
    down_b = D_("down_b", [8, D], F32)
    sw_up = D_("sw_up", [D, 4096], F32)
    sb_up = D_("sb_up", [32, 128], F32)
    sw_down = D_("sw_down", [4096, D], F32)
    sb_down = D_("sb_down", [8, 128], F32)
    sel8w = D_("sel8w", [8, 1024], BF16)
    xout = io["xout"] if "xout" in io else p.dram("xout", [TLOC, D], F32,
                                                  "ExternalOutput")
    if "gate" in io:
        gs, gv = io["gate"]
        for e in ENGINES:
            p.wait(e, gs, gv)
    if DEBUG_L1F:
        dbg_w = p.dram("dbg_w01T", [8, TLOC], F32, "ExternalOutput")
        dbg_u = p.dram("dbg_uscr", [32, 128, 512], BF16, "ExternalOutput")
        dbg_cb = p.dram("dbg_comb", [D, TLOC], F32, "ExternalOutput")
    comb = p.dram("comb", [D, TLOC], F32, "Internal")
    uscr = p.dram("uscr", [32, 128, 512], BF16, "Internal")

    c = mk_consts(p)
    c["ones_row_f"] = p.sbuf([1, 128], F32, "ones_row_f")
    p.do("gpsimd", "memset", c["ones_row_f"][:], 1.0, inc=c["sem"])

    s_in = p.sem("s_in")
    s_t = p.sem("s_t")
    s_v = p.sem("s_v")
    s_s = p.sem("s_s")
    s_o = p.sem("s_o")

    QT = 512
    NQ = TLOC // QT
    UpB = [p.sbuf([128, 4096], BF16) for _ in range(8)]
    xbf = UpB[0][:, 0:1024]
    DnB = [p.sbuf([128, D], BF16) for _ in range(32)]
    uT2 = [p.sbuf([128, QT], BF16) for _ in range(2)]
    ub2 = [p.sbuf([128, QT], BF16) for _ in range(2)]
    xlnT = [p.sbuf([128, TLOC], BF16) for _ in range(8)]
    GwBt = p.sbuf([128, 64], BF16, "GwBt")
    GwB = [GwBt[:, cc * 8:(cc + 1) * 8] for cc in range(8)]
    w01T = p.sbuf([8, TLOC], BF16, "w01T")
    sel8B = p.sbuf([8, 1024], BF16, "sel8B")
    wb_sb = p.sbuf([128, QT], F32, "wb_sb")
    ev_one = p.sbuf([128, QT], F32, "ev_one")
    ev_sb = [ev_one[:], ev_one[:]]
    up_bT = p.sbuf([128, 32 * 8], F32, "up_bT")
    db_sb = p.sbuf([128, 8 * 8], F32, "db_sb")
    sbu_c = p.sbuf([128, 32], F32, "sbu_c")
    sbd_c = p.sbuf([128, 8], F32, "sbd_c")
    xt_one = p.sbuf([128, 1024], F32, "xt_one")
    xt_f = [xt_one, xt_one]
    fac = p.sbuf([128, 4], F32, "fac")
    stg = [p.sbuf([128, 512], F32) for _ in range(2)]
    lt = mk_ln_tmp(p, c)
    g_s = p.sbuf([128, 8], F32)
    g_m = p.sbuf([128, 1], F32)
    g_a = p.sbuf([128, 8], F32)
    g_k = p.sbuf([128, 8], F32)
    g_bf = p.sbuf([128, 8], BF16)
    gb_bc = p.sbuf([128, 8], F32, "gb_bc")
    gbrow = p.sbuf([1, 8], F32, "gbrow")
    hbuf = [xt_f[0][:, 0:QT], xt_f[0][:, QT:2 * QT]]
    macc = None
    xsl = [lt["xc"][:, 0:128], lt["xc"][:, 128:256]]
    osl = [lt["xc"][:, 256:384], lt["xc"][:, 384:512]]
    macc = lt["sq"][:, 0:QT]
    psT = p.psum([128, 128], BF16, "psT")
    psU = [p.psum([128, QT], F32) for _ in range(2)]
    psD = [p.psum([128, QT], F32) for _ in range(5)]

    p.dma("sync", sel8B[:], sel8w, s_in)
    p.dma("sync", gbrow[:], gbias, s_in)
    for cc in range(8):
        p.wait("sync", s_v)
        p.dma("sync", stg[cc % 2][:, 0:8], gate_w[cc * 128:(cc + 1) * 128, :], s_in)
        p.wait("vector", s_in)
        p.do("vector", "tensor_copy", GwB[cc], stg[cc % 2][:, 0:8], inc=s_v)
    p.wait("tensor", s_in, 32)
    p.wait("tensor", c["sem"])
    p.do("tensor", "matmul", psU[0][0:128, 0:8], c["ones_row_f"][:], gbrow[:],
         start=True, stop=True, inc=s_t)
    p.wait("vector", s_t)
    p.do("vector", "tensor_copy", gb_bc[:], psU[0][0:128, 0:8], inc=s_v)
    p.wait("tensor", s_v)

    def load_T(src, rows, dst, dstc0, nsl=1):
        for c0 in range(0, nsl, 4):
            cn = min(4, nsl - c0)
            p.wait("sync", s_v)
            p.wait("sync", s_s)
            p.dma("sync", stg[0][0:rows, 0:cn * 128],
                  src[:, c0 * 128:(c0 + cn) * 128], s_in)
            for jj in range(cn):
                j = c0 + jj
                p.wait("tensor", s_in)
                p.wait("tensor", s_s)
                p.do("tensor", "transpose", psU[1][0:128, 0:rows],
                     stg[0][0:rows, jj * 128:(jj + 1) * 128],
                     c["identf"][0:rows, 0:rows], inc=s_t)
                p.wait("scalar", s_t)
                p.do("scalar", "copy",
                     dst[:, dstc0 + j * rows:dstc0 + (j + 1) * rows],
                     psU[1][0:128, 0:rows], inc=s_s)

    for e in range(8):
        load_T(up_b[e].rearrange("(i x) -> i x", x=128), 32, up_bT, e * 32, nsl=1)
    for e in range(8):
        load_T(down_b[e].rearrange("(i x) -> i x", x=128), 8, db_sb, e * 8, nsl=1)
    load_T(sb_up, 32, sbu_c, 0, nsl=1)
    load_T(sb_down, 8, sbd_c, 0, nsl=1)
    p.wait("tensor", s_s)

    if L1F_STOP == "prep":
        p.wait("gpsimd", s_s, s_s.n)
        p.dma("gpsimd", xout[0:128, :], stg[0][0:128, 0:512], s_o)
        p.wait("gpsimd", s_o)
        return p.finalize()

    for t in range(TLOC // 128):
        t0 = t * 128
        buf = t % 2
        p.wait("sync", s_v)
        p.dma("sync", xt_f[buf][:], xloc[t0:t0 + 128, :], s_in)
        p.wait("vector", s_in)
        ln_token_major(p, xt_f[buf][:], lt["sq"][:], lt, s_v, s_s)
        p.wait("vector", s_v)
        p.wait("vector", s_t)
        p.do("vector", "tensor_copy", xbf, lt["sq"][:], inc=s_v)
        p.wait("tensor", s_v)
        for cc in range(8):
            p.do("tensor", "transpose", psT[:], xbf[:, cc * 128:(cc + 1) * 128]
                 if False else UpB[0][:, cc * 128:(cc + 1) * 128],
                 c["ident"][:], inc=s_t)
            p.wait("scalar", s_t)
            p.do("scalar", "copy", xlnT[cc][:, t0:t0 + 128], psT[:], inc=s_s)
            p.wait("tensor", s_s)
        for cc in range(8):
            p.do("tensor", "matmul", psU[0][0:128, 0:8], xlnT[cc][:, t0:t0 + 128],
                 GwB[cc], start=(cc == 0), stop=(cc == 7),
                 inc=(s_t if cc == 7 else None))
        v = "vector"
        p.wait(v, s_t)
        p.dodr(v, "tensor_add", g_a[:], psU[0][0:128, 0:8], gb_bc[:])
        p.wait("scalar", s_v)
        p.wait("scalar", p._dr["vector"])
        p.do("scalar", "activation", g_s[:], g_a[:], AF.Sigmoid, inc=s_s)
        p.wait(v, s_s)
        p.dodr(v, "reduce_max", g_m[:], g_s[:], axis=AX.X)
        p.dodr(v, "tensor_scalar", g_a[:], g_s[:], g_m[:], None, ALU.is_equal)
        p.dodr(v, "scalar_tensor_tensor", g_a[:], g_a[:], -1e30, g_s[:],
               ALU.mult, ALU.add)
        p.dodr(v, "reduce_max", g_m[:], g_a[:], axis=AX.X)
        p.dodr(v, "tensor_scalar", g_k[:], g_s[:], g_m[:], None, ALU.is_ge)
        p.dodr(v, "tensor_mul", g_k[:], g_s[:], g_k[:])
        p.dodr(v, "reduce_sum", g_m[:], g_k[:], axis=AX.X)
        p.dodr(v, "tensor_scalar_add", g_m[:], g_m[:], 1e-6)
        p.dodr(v, "reciprocal", g_m[:], g_m[:])
        p.dodr(v, "tensor_scalar_mul", g_m[:], g_m[:], 0.1)
        p.dodr(v, "tensor_scalar_mul", g_k[:], g_k[:], g_m[:])
        p.do(v, "tensor_copy", g_bf[:], g_k[:], inc=s_v)
        p.wait("tensor", s_v)
        p.do("tensor", "transpose", psT[0:8, :], g_bf[:], c["ident"][:], inc=s_t)
        p.wait("scalar", s_t)
        p.do("scalar", "copy", w01T[0:8, t0:t0 + 128], psT[0:8, :], inc=s_s)
        p.wait("tensor", s_s)
        p.wait("vector", s_s)

    if L1F_STOP == "p1":
        p.wait("gpsimd", s_s, s_s.n)
        p.dma("gpsimd", xout[0:128, :], xt_f[0][:], s_o)
        p.wait("gpsimd", s_o)
        return p.finalize()

    region_cnt = {}
    ew_cnt = []
    for e in range(L1F_NE):
        shared = e == 8
        up_src = sw_up if shared else up_w[e]
        dn_src = sw_down if shared else down_w[e]
        p.wait("vector", s_t)
        srcs = ([(UpB[cc][:, k4 * 512:(k4 + 1) * 512],
                  up_src[cc * 128:(cc + 1) * 128, k4 * 512:(k4 + 1) * 512])
                 for cc in range(8) for k4 in range(8)] +
                [(DnB[i][:, k2 * 512:(k2 + 1) * 512],
                  dn_src[i * 128:(i + 1) * 128, k2 * 512:(k2 + 1) * 512])
                 for i in range(32) for k2 in range(2)])
        for dst, src_ap in srcs:
            idx = len(ew_cnt)
            b = idx % 2
            if idx >= 2:
                p.wait("sync", s_v, ew_cnt[idx - 2])
            else:
                p.wait("sync", s_v)
            p.dma("sync", stg[b][:], src_ap, s_in)
            p.wait("vector", s_in)
            p.do("vector", "tensor_copy", dst, stg[b][:], inc=s_v)
            ew_cnt.append(s_v.n)

        if L1F_STOP == "wonly":
            continue
        for g in range(NQ):
            q0 = g * QT
            if L1F_STOP == "noup":
                break
            if L1F_STOP == "nowb":
                pass
            elif shared:
                p.wait("vector", s_o)
                p.do("vector", "memset", wb_sb[:], 0.1, inc=s_v)
            else:
                p.wait("tensor", s_v)
                p.wait("tensor", s_s)
                p.do("tensor", "matmul", psU[0][:, 0:QT],
                     sel8B[:, e * 128:(e + 1) * 128], w01T[0:8, q0:q0 + QT],
                     start=True, stop=True, inc=s_t)
                p.wait("vector", s_t)
                p.do("vector", "tensor_copy", wb_sb[:], psU[0][:, 0:QT], inc=s_v)
            if L1F_STOP == "nogelu":
                continue
            if not hasattr(p, "_l1f_gelu"):
                p._l1f_gelu = []
                p._l1f_udma = []
            gelu_cnt = p._l1f_gelu
            udma_cnt = p._l1f_udma
            gb0 = len(gelu_cnt)
            ub0 = len(udma_cnt)
            p.wait("tensor", s_v)

            def emit_up(i):
                b = i % 2
                if gb0 + i >= 2:
                    p.wait("tensor", s_s, gelu_cnt[gb0 + i - 2])
                for cc in range(8):
                    p.do("tensor", "matmul", psU[b][:, 0:QT],
                         UpB[cc][:, i * 128:(i + 1) * 128], xlnT[cc][:, q0:q0 + QT],
                         start=(cc == 0), stop=(cc == 7),
                         inc=(s_t if cc == 7 else None))

            def emit_gelu(i):
                b = i % 2
                p.wait("scalar", s_t)
                if ub0 + i >= 2:
                    p.wait("scalar", s_o, udma_cnt[ub0 + i - 2])
                bias_col = (sbu_c[:, i:i + 1] if shared
                            else up_bT[:, e * 32 + i:e * 32 + i + 1])
                p.do("scalar", "activation", uT2[b][:], psU[b][:, 0:QT],
                     AF.Gelu, bias=bias_col, inc=s_s)
                gelu_cnt.append(s_s.n)

            def emit_sweepA(i):
                b = i % 2
                p.wait("tensor", s_s, gelu_cnt[gb0 + i])
                for hi, h in enumerate(range(0, 5)):
                    p.do("tensor", "matmul", psD[hi][:, 0:QT],
                         DnB[i][:, h * 128:(h + 1) * 128], uT2[b][:],
                         start=(i == 0), stop=(i == 31),
                         inc=(s_t if (i == 31 and hi == 4) else None))
                p.wait("gpsimd", s_s, gelu_cnt[gb0 + i])
                p.dma("gpsimd", uscr[i], uT2[b][:], s_o)
                udma_cnt.append(s_o.n)

            emit_up(0)
            emit_gelu(0)
            for i in range(1, 32):
                emit_up(i)
                emit_gelu(i)
                emit_sweepA(i - 1)
            emit_sweepA(31)

            def emit_evac(hi, h):
                eb = ev_sb[0]
                db_col = (sbd_c[:, h:h + 1] if shared
                          else db_sb[:, e * 8 + h:e * 8 + h + 1])
                p.wait("vector", s_o)
                p.do("vector", "scalar_tensor_tensor", eb[:], psD[hi][:, 0:QT],
                     db_col, wb_sb[:], ALU.add, ALU.mult, inc=s_v)
                p.wait("gpsimd", s_v)
                key = (g, h)
                if key in region_cnt:
                    p.wait("gpsimd", s_o, region_cnt[key])
                p.dma("gpsimd", comb[h * 128:(h + 1) * 128, q0:q0 + QT], eb[:],
                      s_o, accum_op=(ALU.bypass if e == 0 else ALU.add))
                region_cnt[key] = s_o.n

            if L1F_STOP not in ("noswa", "noevac"):
                p.wait("vector", s_t)
                for hi, h in enumerate(range(0, 5)):
                    emit_evac(hi, h)
            if L1F_STOP in ("noswb", "noswa", "noevac"):
                continue
            p.wait("tensor", s_v)
            p.wait("sync", s_t)
            mmB_cnt = []
            for i in range(32):
                b = i % 2
                p.wait("sync", s_o)
                if i >= 2:
                    p.wait("sync", s_t, mmB_cnt[i - 2])
                p.dma("sync", ub2[b][:], uscr[i], s_in)
                p.wait("tensor", s_in)
                for hi, h in enumerate(range(5, 8)):
                    p.do("tensor", "matmul", psD[hi][:, 0:QT],
                         DnB[i][:, h * 128:(h + 1) * 128], ub2[b][:],
                         start=(i == 0), stop=(i == 31),
                         inc=(s_t if hi == 2 else None))
                mmB_cnt.append(s_t.n)
            p.wait("vector", s_t)
            for hi, h in enumerate(range(5, 8)):
                emit_evac(hi, h)

    if L1F_STOP == "p2":
        p.wait("gpsimd", s_o, s_o.n)
        p.dma("gpsimd", xout[0:128, :], xt_f[0][:], s_o)
        p.wait("gpsimd", s_o)
        return p.finalize()

    if DEBUG_L1F:
        for gq in range(4):
            p.wait("vector", s_o)
            p.do("vector", "tensor_copy", ev_sb[0][0:8, 0:512],
                 w01T[0:8, gq * 512:(gq + 1) * 512], inc=s_v)
            p.wait("gpsimd", s_v)
            p.dma("gpsimd", dbg_w[0:8, gq * 512:(gq + 1) * 512],
                  ev_sb[0][0:8, 0:512], s_o)
        p.wait("gpsimd", s_o)
        p.dma("gpsimd", dbg_u, uscr, s_o)
        p.dma("gpsimd", dbg_cb, comb, s_o)
        p.wait("gpsimd", s_o)

    all_acc = s_o.n
    p.wait("sync", s_o, all_acc)
    for g in range(NQ):
        q0 = g * QT
        for h in range(8):
            hb = hbuf[h % 2]
            p.wait("sync", s_v)
            p.wait("sync", s_s)
            p.dma("sync", hb, comb[h * 128:(h + 1) * 128, q0:q0 + QT], s_in)
            p.wait("scalar", s_in)
            p.wait("scalar", s_v)
            p.wait("scalar", p._dr.get("vector", s_v))
            p.do("scalar", "activation", ev_sb[0][:], hb, AF.Abs, inc=s_s)
            p.wait("vector", s_s)
            if h == 0:
                p.do("vector", "tensor_copy", macc, ev_sb[0][:], inc=s_v)
            else:
                p.dodr("vector", "tensor_max", macc, macc, ev_sb[0][:])
        p.wait("tensor", p._dr["vector"])
        for j in range(4):
            p.wait("tensor", s_v)
            p.wait("tensor", s_s)
            p.do("tensor", "transpose", psU[1][0:128, 0:128],
                 lt["sq"][:, j * 128:(j + 1) * 128], c["identf"][:], inc=s_t)
            p.wait("vector", s_t)
            p.do("vector", "reduce_max", fac[:, j:j + 1], psU[1][0:128, 0:128],
                 axis=AX.X, inc=s_v)
            p.wait("tensor", s_v)
        p.wait("vector", s_v)
        p.dodr("vector", "tensor_scalar_add", fac[:], fac[:], 1e-6)
        p.dodr("vector", "reciprocal", fac[:], fac[:])
        for h in range(8):
            hb = hbuf[h % 2]
            p.wait("sync", s_v)
            p.dma("sync", hb, comb[h * 128:(h + 1) * 128, q0:q0 + QT], s_in)
            for j in range(4):
                bx = j % 2
                p.wait("sync", s_v)
                p.dma("sync", xsl[bx],
                      xloc[q0 + j * 128:q0 + (j + 1) * 128,
                           h * 128:(h + 1) * 128], s_in)
                p.wait("tensor", s_in)
                p.wait("tensor", s_v)
                p.do("tensor", "transpose", psU[1][0:128, 0:128],
                     hb.tensor[0:128, hb.offset + j * 128:hb.offset + (j + 1) * 128]
                     if False else hb[:, j * 128:(j + 1) * 128],
                     c["identf"][:], inc=s_t)
                p.wait("vector", s_t)
                p.wait("vector", s_in)
                p.wait("vector", s_o)
                p.do("vector", "scalar_tensor_tensor", osl[bx],
                     psU[1][0:128, 0:128], fac[:, j:j + 1], xsl[bx],
                     ALU.mult, ALU.add, inc=s_v)
                p.wait("gpsimd", s_v)
                p.dma("gpsimd", xout[q0 + j * 128:q0 + (j + 1) * 128,
                                     h * 128:(h + 1) * 128], osl[bx], s_o)
    p.wait("gpsimd", s_o)
    if fused:
        out_sem = (s_o, s_o.n)
        p.free_since(mark)
        return out_sem
    return p.finalize()



def build_l0a(p=None, io=None):
    fused = p is not None
    if not fused:
        p = Prog("l0a")
    io = io or {}
    mark = p.marker()
    D_ = lambda n, sh, dt: io[n] if n in io else p.dram(n, sh, dt, "ExternalInput")
    xb = D_("xb", [S, D], F32)
    wq = D_("wq", [D, 256], F32)
    wk = D_("wk", [D, 256], F32)
    wv = D_("wv", [D, 256], F32)
    bq = D_("bq", [1, 256], F32)
    bk = D_("bk", [1, 256], F32)
    bv = D_("bv", [1, 256], F32)
    outT = io["outT"] if "outT" in io else p.dram("outT", [D, TLOC], BF16,
                                                  "ExternalOutput")
    if "gate" in io:
        gs, gv = io["gate"]
        for e in ENGINES:
            p.wait(e, gs, gv)
    if DEBUG_L0A:
        dbg_c = p.dram("dbg_cols", [S // 128, 128, 16], F32, "ExternalOutput")
        dbg_sra = p.dram("dbg_sra", [S // 128, 1, 512], F32, "ExternalOutput")

    c = mk_consts(p)
    g = "gpsimd"
    c["ones_row_f"] = p.sbuf([1, 128], F32, "ones_row_f")
    c["ones_col_f"] = p.sbuf([128, 1], F32, "ones_col_f")
    c["triu_f"] = p.sbuf([128, 128], F32, "triu_f")
    c["triu_b"] = p.sbuf([128, 128], BF16, "triu_b")
    p.do(g, "memset", c["ones_row_f"][:], 1.0)
    p.do(g, "memset", c["ones_col_f"][:], 1.0)
    p.do(g, "memset", c["triu_f"][:], 1.0)
    p.do(g, "affine_select", c["triu_f"][:], c["triu_f"][:], [[1, 128]], ALU.is_ge,
         0.0, base=0, channel_multiplier=-1)
    p.do(g, "memset", c["triu_b"][:], 1.0)
    p.do(g, "affine_select", c["triu_b"][:], c["triu_b"][:], [[1, 128]], ALU.is_ge,
         0.0, base=0, channel_multiplier=-1, inc=c["sem"])

    s_in = p.sem("s_in")
    s_t = p.sem("s_t")
    s_v = p.sem("s_v")
    s_s = p.sem("s_s")
    s_g = p.sem("s_g")
    s_o = p.sem("s_o")

    NH = 4
    NC = S // 128
    EPS = 1e-6

    WB = [p.sbuf([128, 768], BF16) for _ in range(8)]
    bbc = p.sbuf([128, 768], F32, "bbc")
    bqkv = p.sbuf([1, 768], F32, "bqkv")
    outS = [p.sbuf([128, S], BF16) for _ in range(2)]
    xlnT_c = [p.sbuf([128, 128], BF16) for _ in range(8)]
    xt_f = [p.sbuf([128, 1024], F32) for _ in range(2)]
    xbf = p.sbuf([128, 1024], BF16, "xbf")
    lt = mk_ln_tmp(p, c)
    qkv_f = [p.sbuf([128, 192], F32) for _ in range(NH)]
    qk_f = [p.sbuf([128, 128], F32) for _ in range(NH)]
    qk_b = [p.sbuf([128, 128], BF16) for _ in range(NH)]
    qT_b = [p.sbuf([64, 128], BF16) for _ in range(NH)]
    kT_b = [p.sbuf([64, 128], BF16) for _ in range(NH)]
    ksqs_f = [p.sbuf([128, 128], F32) for _ in range(NH)]
    vs_b = [p.sbuf([128, 64], BF16) for _ in range(NH)]
    A_b = [p.sbuf([128, 128], BF16) for _ in range(NH)]
    kvprev_b = [p.sbuf([64, 64], BF16) for _ in range(NH)]
    cum1 = [p.sbuf([128, 128], F32) for _ in range(NH)]
    cum2 = [p.sbuf([128, 128], F32) for _ in range(NH)]
    o_f = [p.sbuf([64, 128], F32) for _ in range(NH)]
    facT = p.sbuf([1, 128], F32, "facT")
    facE = p.sbuf([64, 128], F32, "facE")
    colsc = p.sbuf([128, 16 * NH], F32, "colsc")
    sra_sb = p.sbuf([1, 512], F32, "sra_sb")
    srb_sb = p.sbuf([1, 512], F32, "srb_sb")
    ce_sb = p.sbuf([1, 4], F32, "ce_sb")
    cex_sb = p.sbuf([128, NH], F32, "cex_sb")
    nrm = p.sbuf([128, 1], F32, "nrm")
    rnrm = p.sbuf([128, 1], F32, "rnrm")
    stg = [p.sbuf([128, 768], F32) for _ in range(2)]
    sqs = [p.sbuf([128, 64], F32) for _ in range(NH)]

    pC = p.psum([128, 512], F32, "pC")
    pS = p.psum([128, 512], F32, "pS")
    pO = p.psum([128, 512], F32, "pO")
    pKV = p.psum([128, 512], F32, "pKV")
    pA = p.psum([128, 512], F32, "pA")
    pB = p.psum([128, 512], F32, "pB")
    pE = p.psum([128, 512], F32, "pE")
    psT = p.psum([128, 128], BF16, "psT")

    for cc in range(8):
        b = cc % 2
        sl = slice(cc * 128, (cc + 1) * 128)
        p.wait("sync", s_v)
        p.dma("sync", stg[b][:, 0:256], wq[sl, :], s_in)
        p.dma("sync", stg[b][:, 256:512], wk[sl, :], s_in)
        p.dma("sync", stg[b][:, 512:768], wv[sl, :], s_in)
        p.wait("vector", s_in)
        for h in range(NH):
            p.do("vector", "tensor_copy", WB[cc][:, h * 192:h * 192 + 64],
                 stg[b][:, h * 64:h * 64 + 64])
            p.do("vector", "tensor_copy", WB[cc][:, h * 192 + 64:h * 192 + 128],
                 stg[b][:, 256 + h * 64:256 + h * 64 + 64])
            p.do("vector", "tensor_copy", WB[cc][:, h * 192 + 128:h * 192 + 192],
                 stg[b][:, 512 + h * 64:512 + h * 64 + 64],
                 inc=(s_v if h == NH - 1 else None))
    for h in range(NH):
        p.dma("sync", bqkv[0:1, h * 192:h * 192 + 64],
              bq[0:1, h * 64:h * 64 + 64], s_in)
        p.dma("sync", bqkv[0:1, h * 192 + 64:h * 192 + 128],
              bk[0:1, h * 64:h * 64 + 64], s_in)
        p.dma("sync", bqkv[0:1, h * 192 + 128:h * 192 + 192],
              bv[0:1, h * 64:h * 64 + 64], s_in)
    p.wait("tensor", s_in)
    p.wait("tensor", c["sem"])
    for j in range(0, 768, 512):
        w = min(512, 768 - j)
        p.do("tensor", "matmul", pE[:, 0:w], c["ones_row_f"][:], bqkv[0:1, j:j + w],
             start=True, stop=True, inc=s_t)
        p.wait("vector", s_t)
        p.do("vector", "tensor_copy", bbc[:, j:j + w], pE[:, 0:w], inc=s_v)
        p.wait("tensor", s_v)

    def col(h, i):
        return colsc[:, h * 16 + i:h * 16 + i + 1]

    p.do("vector", "memset", sra_sb[:], 0.0)
    p.do("vector", "memset", srb_sb[:], 0.0)
    p.do("vector", "memset", ce_sb[:], 0.0, inc=s_v)

    first_kvbank_mm = [True]

    for ch in range(NC):
        t0 = ch * 128
        p.wait("gpsimd", s_v)
        p.do(g, "iota", nrm[:], [[1, 1]], base=t0 + 1, channel_multiplier=1,
             allow_small_or_imprecise_dtypes=True, inc=s_g)
        p.wait("vector", s_g)
        p.dodr("vector", "reciprocal", rnrm[:], nrm[:])
        buf = ch % 2
        p.wait("sync", s_v)
        p.dma("sync", xt_f[buf][:], xb[t0:t0 + 128, :], s_in)
        p.wait("vector", s_in)
        ln_token_major(p, xt_f[buf][:], lt["sq"][:], lt, s_v, s_s)
        p.wait("vector", s_v)
        p.wait("vector", s_t)
        p.do("vector", "tensor_copy", xbf[:], lt["sq"][:], inc=s_v)
        p.wait("tensor", s_v)
        for cc in range(8):
            p.do("tensor", "transpose", psT[:], xbf[:, cc * 128:(cc + 1) * 128],
                 c["ident"][:], inc=s_t)
            p.wait("scalar", s_t)
            p.do("scalar", "copy", xlnT_c[cc][:], psT[:], inc=s_s)
            p.wait("tensor", s_s)

        for h in range(NH):
            p.wait("tensor", s_s)
            p.wait("tensor", s_v)
            if "vector" in getattr(p, "_dr", {}):
                p.wait("tensor", p._dr["vector"])
            for cc in range(8):
                p.do("tensor", "matmul", pS[:, 128:320], xlnT_c[cc][:],
                     WB[cc][:, h * 192:(h + 1) * 192],
                     start=(cc == 0), stop=(cc == 7),
                     inc=(s_t if cc == 7 else None))
            p.wait("vector", s_t)
            p.dodr("vector", "tensor_add", qkv_f[h][:], pS[:, 128:320],
                   bbc[:, h * 192:(h + 1) * 192])
            p.wait("scalar", p._dr["vector"])
            p.do("scalar", "activation", qk_f[h][:], qkv_f[h][:, 0:128],
                 AF.Sigmoid, inc=s_s)
            p.wait("vector", s_s)
            p.do("vector", "tensor_copy", qk_b[h][:], qk_f[h][:], inc=s_v)
            p.wait("tensor", s_v)
            p.do("tensor", "transpose", psT[:], qk_b[h][:], c["ident"][:], inc=s_t)
            p.wait("scalar", s_t)
            p.do("scalar", "copy", qT_b[h][:], psT[0:64, :])
            p.do("scalar", "copy", kT_b[h][:], psT[64:128, :], inc=s_s)
            p.wait("tensor", s_s)

        for h in range(NH):
            p.wait("tensor", s_s)
            p.wait("tensor", s_v)
            p.do("tensor", "matmul", pC[:, 0:128], c["triu_f"][:], qk_f[h][:],
                 start=True, stop=False)
            p.do("tensor", "matmul", pC[:, 0:128], c["ones_row_f"][:],
                 sra_sb[0:1, h * 128:(h + 1) * 128], start=False, stop=True,
                 inc=s_t)
            p.do("tensor", "matmul", pA[0:1, h * 128:(h + 1) * 128],
                 c["ones_col_f"][:], qk_f[h][:],
                 start=(ch == 0 and h == 0), stop=(ch == NC - 1))
            p.wait("scalar", s_t)
            p.do("scalar", "activation", cum1[h][:], pC[:, 0:128], AF.Identity,
                 bias=c["eps6"][:], inc=s_s)
            p.wait("vector", s_s)
            p.do("vector", "scalar_tensor_tensor", sqs[h][:], qk_f[h][:, 0:64],
                 EPS, cum1[h][:, 64:128], ALU.add, ALU.mult, accum_out=col(h, 0),
                 inc=p._dr["vector"])
            p.wait("vector", p._dr["vector"])
            p.do("vector", "scalar_tensor_tensor", sqs[h][:], qk_f[h][:, 64:128],
                 EPS, cum1[h][:, 0:64], ALU.add, ALU.mult, accum_out=col(h, 1),
                 inc=p._dr["vector"])
            p.wait("vector", p._dr["vector"])
            p.dodr("vector", "reciprocal", col(h, 0), col(h, 0))
            p.dodr("vector", "reciprocal", col(h, 1), col(h, 1))
            p.dodr("vector", "tensor_mul", col(h, 0), col(h, 0), nrm[:])
            p.dodr("vector", "tensor_mul", col(h, 1), col(h, 1), nrm[:])
            p.dodr("vector", "tensor_scalar_mul", ksqs_f[h][:, 0:64],
                   qk_f[h][:, 64:128], col(h, 1))
            p.dodr("vector", "tensor_scalar_mul", ksqs_f[h][:, 64:128],
                   qk_f[h][:, 0:64], col(h, 0))
            p.wait("tensor", p._dr["vector"])
            p.wait("tensor", s_s)
            p.do("tensor", "matmul", pC[:, 128:256], c["triu_f"][:], ksqs_f[h][:],
                 start=True, stop=False)
            p.do("tensor", "matmul", pC[:, 128:256], c["ones_row_f"][:],
                 srb_sb[0:1, h * 128:(h + 1) * 128], start=False, stop=True,
                 inc=s_t)
            p.do("tensor", "matmul", pB[0:1, h * 128:(h + 1) * 128],
                 c["ones_col_f"][:], ksqs_f[h][:],
                 start=(ch == 0 and h == 0), stop=(ch == NC - 1))
            p.wait("scalar", s_t)
            p.do("scalar", "activation", cum2[h][:], pC[:, 128:256], AF.Identity,
                 bias=c["eps6"][:], inc=s_s)
            p.wait("vector", s_s)
            p.do("vector", "scalar_tensor_tensor", sqs[h][:], qk_f[h][:, 0:64],
                 EPS, cum2[h][:, 0:64], ALU.add, ALU.mult, accum_out=col(h, 2),
                 inc=p._dr["vector"])
            p.wait("vector", p._dr["vector"])
            p.do("vector", "scalar_tensor_tensor", sqs[h][:], qk_f[h][:, 64:128],
                 EPS, cum2[h][:, 64:128], ALU.add, ALU.mult, accum_out=col(h, 3),
                 inc=p._dr["vector"])
            p.wait("vector", p._dr["vector"])
            p.dodr("vector", "tensor_mul", col(h, 2), col(h, 2), rnrm[:])
            p.dodr("vector", "tensor_mul", col(h, 3), col(h, 3), rnrm[:])
            p.dodr("vector", "tensor_scalar", col(h, 3), col(h, 3), -1.0, 1.0,
                   ALU.max, ALU.min)
            p.wait("scalar", p._dr["vector"])
            p.do("scalar", "activation", col(h, 8), col(h, 2), AF.Sigmoid)
            p.do("scalar", "activation", col(h, 4), col(h, 3), AF.Exp, inc=s_s)
            p.wait("tensor", s_s)
            p.wait("tensor", p._dr["vector"])
            p.do("tensor", "matmul", pC[:, 256 + h:257 + h], c["triu_f"][:],
                 col(h, 4), start=True, stop=False)
            p.do("tensor", "matmul", pC[:, 256 + h:257 + h], c["ones_row_f"][:],
                 ce_sb[0:1, h:h + 1], start=False, stop=True, inc=s_t)
            p.do("tensor", "matmul", pKV[0:1, 256 + h:257 + h],
                 c["ones_col_f"][:], col(h, 4),
                 start=first_kvbank_mm[0], stop=(ch == NC - 1))
            first_kvbank_mm[0] = False
            p.wait("vector", s_t)
            p.do("vector", "tensor_copy", cex_sb[:, h:h + 1], pC[:, 256 + h:257 + h],
                 inc=s_v)
            p.wait("vector", s_v)
            p.dodr("vector", "reciprocal", col(h, 6), cex_sb[:, h:h + 1])
            p.dodr("vector", "tensor_mul", col(h, 6), col(h, 6), col(h, 4))
            p.dodr("vector", "tensor_mul", col(h, 6), col(h, 6), nrm[:])
            p.dodr("vector", "tensor_scalar_mul", vs_b[h][:], qkv_f[h][:, 128:192],
                   col(h, 6))
            p.wait("tensor", p._dr["vector"])
            p.wait("tensor", s_v)
            p.do("tensor", "matmul", pS[:, 0:128], kT_b[h][:], qT_b[h][:],
                 start=True, stop=True, inc=s_t)
            p.wait("vector", s_t)
            p.dodr("vector", "tensor_mul", A_b[h][:], pS[:, 0:128], c["triu_b"][:])
            if ch == 0:
                p.dodr("vector", "memset", kvprev_b[h][:], 0.0)
            else:
                p.dodr("vector", "tensor_copy", kvprev_b[h][:],
                       pKV[0:64, h * 64:(h + 1) * 64])
            p.wait("tensor", p._dr["vector"])
            p.wait("tensor", s_s)
            p.do("tensor", "matmul", pO[0:64, h * 128:(h + 1) * 128], vs_b[h][:],
                 A_b[h][:], start=True, stop=False)
            p.do("tensor", "matmul", pO[0:64, h * 128:(h + 1) * 128],
                 kvprev_b[h][:], qT_b[h][:], start=False, stop=True, inc=s_t)
            p.do("tensor", "matmul", pKV[0:64, h * 64:(h + 1) * 64],
                 qk_b[h][:, 64:128], vs_b[h][:], start=False,
                 stop=(ch == NC - 1))
            p.wait("vector", s_s)
            p.dodr("vector", "tensor_mul", col(h, 9), col(h, 8), col(h, 0))
            p.dodr("vector", "tensor_mul", col(h, 9), col(h, 9), rnrm[:])
            p.dodr("vector", "tensor_copy", col(h, 7), col(h, 9))
            p.wait("tensor", p._dr["vector"])
            p.do("tensor", "transpose", pE[0:1, 0:128], col(h, 7),
                 c["identf"][:], inc=s_t)
            p.wait("vector", s_t)
            p.do("vector", "tensor_copy", facT[:], pE[0:1, 0:128], inc=s_v)
            p.wait("tensor", s_v)
            p.do("tensor", "matmul", pE[0:64, 128:256], c["ones_row_f"][0:1, 0:64],
                 facT[:], start=True, stop=True, inc=s_t)
            p.wait("scalar", s_t)
            p.do("scalar", "copy", o_f[h][:], pO[0:64, h * 128:(h + 1) * 128],
                 inc=s_s)
            p.wait("vector", s_s)
            row0 = 64 * h
            dst = outS[0] if h < 2 else outS[1]
            rsl = slice((h % 2) * 64, (h % 2) * 64 + 64)
            p.do("vector", "tensor_mul",
                 dst[rsl, t0:t0 + 128], o_f[h][:], pE[0:64, 128:256], inc=s_v)
            p.wait("tensor", s_v)

        if DEBUG_L0A:
            p.wait("gpsimd", s_v)
            p.wait("gpsimd", s_s)
            p.dma("gpsimd", dbg_c[ch], colsc[:, 0:16], s_o)
            p.wait("vector", s_o)
        p.wait("vector", s_t)
        p.do("vector", "tensor_copy", sra_sb[:], pA[0:1, 0:512])
        p.do("vector", "tensor_copy", srb_sb[:], pB[0:1, 0:512])
        p.do("vector", "tensor_copy", ce_sb[:], pKV[0:1, 256:260], inc=s_v)
        if DEBUG_L0A:
            p.wait("gpsimd", s_v)
            p.dma("gpsimd", dbg_sra[ch], sra_sb[:], s_o)
            p.wait("vector", s_o)
        p.wait("tensor", s_v)

    p.wait("gpsimd", s_v)
    for j in range(4):
        cs = slice(2048 * j, 2048 * (j + 1))
        p.dma("gpsimd", outT[256 * j:256 * j + 128, :], outS[0][:, cs], s_o)
        p.dma("gpsimd", outT[256 * j + 128:256 * j + 256, :], outS[1][:, cs], s_o)
    p.wait("gpsimd", s_o)
    if fused:
        out_sem = (s_o, s_o.n)
        p.free_since(mark)
        return out_sem
    return p.finalize()



def _ca(a):
    return np.ascontiguousarray(a)


def kernel(**inp):
    if FUSED:
        return kernel_fused(**inp)
    TL = TLOC
    _ = inp.get("L0_attn_attn_mask")
    x = np.asarray(inp["L0_attn_inputs"], np.float32)

    nc = build_l0a()
    maps = []
    Wq, Wk, Wv = (np.asarray(inp[f"L0_attn_W{z}"], np.float32) for z in "qkv")
    bq, bk, bv = (np.asarray(inp[f"L0_attn_b{z}"], np.float32) for z in "qkv")
    for c in range(NCORE):
        b, gr = c // 4, c % 4
        sl = slice(256 * gr, 256 * (gr + 1))
        maps.append({"xb": _ca(x[b]), "wq": _ca(Wq[:, sl]), "wk": _ca(Wk[:, sl]),
                     "wv": _ca(Wv[:, sl]), "bq": _ca(bq[None, sl]),
                     "bk": _ca(bk[None, sl]), "bv": _ca(bv[None, sl])})
    res = run_bass_kernel_spmd(nc, maps, core_ids=list(range(NCORE)))
    xheadsT = [np.concatenate([res.results[4 * b + gr]["outT"] for gr in range(4)], 0)
               for b in range(2)]

    nc = build_l0b()
    maps = []
    for c in range(NCORE):
        b, j = c // 4, c % 4
        ts = slice(TL * j, TL * (j + 1))
        maps.append({
            "x": _ca(x[b][ts]), "hdT": _ca(xheadsT[b][:, ts]),
            "Wo": np.asarray(inp["L0_attn_Wo"], np.float32),
            "W1": np.asarray(inp["L0_attn_W1"], np.float32),
            "W2": np.asarray(inp["L0_attn_W2"], np.float32),
            "bo": _ca(np.asarray(inp["L0_attn_bo"], np.float32)[None, :]),
            "b1": _ca(np.asarray(inp["L0_attn_b1"], np.float32).reshape(32, 128)),
            "b2": _ca(np.asarray(inp["L0_attn_b2"], np.float32)[None, :]),
            "ln1g": _ca(np.asarray(inp["L0_attn_ln1_g"], np.float32)[None, :]),
            "ln1b": _ca(np.asarray(inp["L0_attn_ln1_b"], np.float32)[None, :]),
            "ln2g": _ca(np.asarray(inp["L0_attn_ln2_g"], np.float32)[None, :]),
            "ln2b": _ca(np.asarray(inp["L0_attn_ln2_b"], np.float32)[None, :]),
        })
    res = run_bass_kernel_spmd(nc, maps, core_ids=list(range(NCORE)))
    x1 = np.stack([np.concatenate([res.results[4 * b + j]["xout"] for j in range(4)], 0)
                   for b in range(2)])

    nc = build_flash("l0f")
    Wkv = np.asarray(inp["L0_ffn_Wkv"], np.float32)
    sel8c = np.kron(np.eye(8), np.ones((1, 64))).astype(np.float32).reshape(8, 512)
    maps = []
    for c in range(NCORE):
        b, j = c // 4, c % 4
        ts = slice(TL * j, TL * (j + 1))
        maps.append({
            "xb": _ca(x1[b]), "xloc": _ca(x1[b][ts]),
            "wq": np.asarray(inp["L0_ffn_Wq"], np.float32),
            "wk": _ca(Wkv[:, 0:512]), "wv": _ca(Wkv[:, 512:1024]),
            "wo": _ca(np.asarray(inp["L0_ffn_Wo"], np.float32).reshape(512, 1024)),
            "ws": np.asarray(inp["L0_ffn_Ws"], np.float32),
            "bo8": np.asarray(inp["L0_ffn_bo"], np.float32),
            "sel8c": sel8c,
        })
    res = run_bass_kernel_spmd(nc, maps, core_ids=list(range(NCORE)))
    x2 = np.stack([np.concatenate([res.results[4 * b + j]["xout"] for j in range(4)], 0)
                   for b in range(2)])

    nc = build_flash("l1a")
    wqkv = np.asarray(inp["L1_attn_w_qkv"], np.float32)
    maps = []
    for c in range(NCORE):
        b, j = c // 4, c % 4
        ts = slice(TL * j, TL * (j + 1))
        maps.append({
            "xb": _ca(x2[b]), "xloc": _ca(x2[b][ts]),
            "wq": _ca(wqkv[:, 0:1024]), "wk": _ca(wqkv[:, 1024:2048]),
            "wv": _ca(wqkv[:, 2048:3072]),
            "wo": np.asarray(inp["L1_attn_w_out"], np.float32),
        })
    res = run_bass_kernel_spmd(nc, maps, core_ids=list(range(NCORE)))
    x3 = np.stack([np.concatenate([res.results[4 * b + j]["xout"] for j in range(4)], 0)
                   for b in range(2)])

    nc = build_l1f()
    import ml_dtypes
    sel8w = np.kron(np.eye(8), np.ones((1, 128))).astype(
        ml_dtypes.bfloat16).reshape(8, 1024)
    base = {
        "gate_w": np.asarray(inp["L1_ffn_gate_w"], np.float32),
        "gbias": _ca(np.asarray(inp["L1_ffn_bias"], np.float32)[None, :]),
        "up_w": np.asarray(inp["L1_ffn_up_w"], np.float32),
        "up_b": np.asarray(inp["L1_ffn_up_b"], np.float32),
        "down_w": np.asarray(inp["L1_ffn_down_w"], np.float32),
        "down_b": np.asarray(inp["L1_ffn_down_b"], np.float32),
        "sw_up": np.asarray(inp["L1_ffn_sw_up"], np.float32),
        "sb_up": _ca(np.asarray(inp["L1_ffn_sb_up"], np.float32).reshape(32, 128)),
        "sw_down": np.asarray(inp["L1_ffn_sw_down"], np.float32),
        "sb_down": _ca(np.asarray(inp["L1_ffn_sb_down"], np.float32).reshape(8, 128)),
        "sel8w": sel8w,
    }
    maps = []
    for c in range(NCORE):
        b, j = c // 4, c % 4
        maps.append(dict(base, xloc=_ca(x3[b][TL * j:TL * (j + 1)])))
    res = run_bass_kernel_spmd(nc, maps, core_ids=list(range(NCORE)))
    x4 = np.stack([np.concatenate([res.results[4 * b + j]["xout"] for j in range(4)], 0)
                   for b in range(2)])
    return x4



RG8 = [[0, 1, 2, 3], [4, 5, 6, 7]]


def build_fused():
    p = Prog("mk")
    dram = p.nc.dram_tensor

    def I(n, sh, dt):
        return p.dram(n, sh, dt, "ExternalInput")

    xb0 = I("xb0", [S, D], F32)
    xloc0 = I("xloc0", [TLOC, D], F32)
    a_wq = I("a_wq", [D, 256], F32)
    a_wk = I("a_wk", [D, 256], F32)
    a_wv = I("a_wv", [D, 256], F32)
    a_bq = I("a_bq", [1, 256], F32)
    a_bk = I("a_bk", [1, 256], F32)
    a_bv = I("a_bv", [1, 256], F32)
    b_Wo = I("b_Wo", [D, D], F32)
    b_W1 = I("b_W1", [D, 4096], F32)
    b_W2 = I("b_W2", [4096, D], F32)
    b_bo = I("b_bo", [1, D], F32)
    b_b1 = I("b_b1", [32, 128], F32)
    b_b2 = I("b_b2", [1, D], F32)
    b_l1g = I("b_l1g", [1, D], F32)
    b_l1b = I("b_l1b", [1, D], F32)
    b_l2g = I("b_l2g", [1, D], F32)
    b_l2b = I("b_l2b", [1, D], F32)
    gidx = I("gidx", [128, 64], I32)
    f_wq = I("f_wq", [D, 512], F32)
    f_wk = I("f_wk", [D, 512], F32)
    f_wv = I("f_wv", [D, 512], F32)
    f_wo = I("f_wo", [512, D], F32)
    f_ws = I("f_ws", [D, 8], F32)
    f_bo8 = I("f_bo8", [8, D], F32)
    f_sel = I("f_sel", [8, 512], F32)
    g_wq = I("g_wq", [D, 1024], F32)
    g_wk = I("g_wk", [D, 1024], F32)
    g_wv = I("g_wv", [D, 1024], F32)
    g_wo = I("g_wo", [D, D], F32)
    m_gw = I("m_gw", [D, 8], F32)
    m_gb = I("m_gb", [1, 8], F32)
    m_uw = I("m_uw", [8, D, 4096], F32)
    m_ub = I("m_ub", [8, 4096], F32)
    m_dw = I("m_dw", [8, 4096, D], F32)
    m_db = I("m_db", [8, D], F32)
    m_su = I("m_su", [D, 4096], F32)
    m_sbu = I("m_sbu", [32, 128], F32)
    m_sd = I("m_sd", [4096, D], F32)
    m_sbd = I("m_sbd", [8, 128], F32)
    m_sel = I("m_sel", [8, 1024], BF16)
    yout = p.dram("yout", [TLOC, D], F32, "ExternalOutput")

    a_out = p.dram("a_out", [256, S], BF16, "Internal")
    gAll = p.dram("gAll", [D, S], BF16, "Internal")
    x1_loc = p.dram("x1_loc", [TLOC, D], F32, "Internal")
    x1_full = p.dram("x1_full", [S, D], F32, "Internal")
    x2_loc = p.dram("x2_loc", [TLOC, D], F32, "Internal")
    x2_full = p.dram("x2_full", [S, D], F32, "Internal")
    x3_loc = p.dram("x3_loc", [TLOC, D], F32, "Internal")

    s_cc = p.sem("cc")
    s_bar = p.sem("bar")

    def barrier():
        for e in ENGINES:
            p.do(e, "nop", inc=s_bar)
        v = s_bar.n
        for e in ENGINES:
            p.wait(e, s_bar, v)

    def collective(kind, src, dst, gate_sem):
        p.wait("gpsimd", *gate_sem)
        p.do("gpsimd", "collective_compute", kind, ALU.bypass, RG8,
             [src.opt()], [dst.opt()], inc=s_cc)

    a_shard = p.dram("a_shard", [D, TLOC], BF16, "Internal")
    p.pref = "A_"
    g1 = build_l0a(p, {"xb": xb0, "wq": a_wq, "wk": a_wk, "wv": a_wv,
                       "bq": a_bq, "bk": a_bk, "bv": a_bv, "outT": a_shard})
    p.pref = ""
    gAll2 = p.dram("gAll2", [4 * D, TLOC], BF16, "Internal")
    collective("AllGather", a_shard, gAll2, g1)
    barrier()
    p.pref = "B_"
    g2 = build_l0b(p, {"x": xloc0, "hdT_view": gAll2, "gidx": gidx,
                       "Wo": b_Wo, "W1": b_W1, "W2": b_W2, "bo": b_bo,
                       "b1": b_b1, "b2": b_b2, "ln1g": b_l1g, "ln1b": b_l1b,
                       "ln2g": b_l2g, "ln2b": b_l2b, "xout": x1_loc,
                       "gate": (s_cc, s_cc.n)})
    p.pref = ""
    collective("AllGather", x1_loc, x1_full, g2)
    barrier()
    p.pref = "F_"
    g3 = build_flash("l0f", p, {"xb": x1_full, "xloc": x1_loc,
                                "wq": f_wq, "wk": f_wk, "wv": f_wv, "wo": f_wo,
                                "ws": f_ws, "bo8": f_bo8, "sel8c": f_sel,
                                "xout": x2_loc, "gate": (s_cc, s_cc.n)})
    p.pref = ""
    collective("AllGather", x2_loc, x2_full, g3)
    barrier()
    p.pref = "G_"
    g4 = build_flash("l1a", p, {"xb": x2_full, "xloc": x2_loc,
                                "wq": g_wq, "wk": g_wk, "wv": g_wv, "wo": g_wo,
                                "xout": x3_loc, "gate": (s_cc, s_cc.n)})
    p.pref = ""
    barrier()
    p.pref = "M_"
    g5 = build_l1f(p, {"xloc": x3_loc, "gate": (g4[0], g4[1]),
                       "gate_w": m_gw, "gbias": m_gb, "up_w": m_uw,
                       "up_b": m_ub, "down_w": m_dw, "down_b": m_db,
                       "sw_up": m_su, "sb_up": m_sbu, "sw_down": m_sd,
                       "sb_down": m_sbd, "sel8w": m_sel, "xout": yout})
    p.pref = ""
    return p.finalize()


FUSED = True


def kernel_fused(**inp):
    TL = TLOC
    x = np.asarray(inp["L0_attn_inputs"], np.float32)
    _ = inp.get("L0_attn_attn_mask")
    import ml_dtypes
    nc = build_fused()
    Wq, Wk, Wv = (np.asarray(inp[f"L0_attn_W{z}"], np.float32) for z in "qkv")
    bq, bk, bv = (np.asarray(inp[f"L0_attn_b{z}"], np.float32) for z in "qkv")
    Wkv = np.asarray(inp["L0_ffn_Wkv"], np.float32)
    wqkv = np.asarray(inp["L1_attn_w_qkv"], np.float32)
    sel8c = np.kron(np.eye(8), np.ones((1, 64))).astype(np.float32).reshape(8, 512)
    sel8w = np.kron(np.eye(8), np.ones((1, 128))).astype(
        ml_dtypes.bfloat16).reshape(8, 1024)
    maps = []
    for c in range(NCORE):
        b, j = c // 4, c % 4
        sl = slice(256 * j, 256 * (j + 1))
        ts = slice(TL * j, TL * (j + 1))
        gi = np.empty((128, 64), np.int32)
        for cc in range(8):
            hr = 128 * cc + np.arange(128)
            base = 1024 * (hr // 256) + 256 * j + (hr % 256)
            for gg in range(8):
                gi[:, cc * 8 + gg] = base * 8 + gg
        maps.append({
            "xb0": _ca(x[b]), "xloc0": _ca(x[b][ts]),
            "a_wq": _ca(Wq[:, sl]), "a_wk": _ca(Wk[:, sl]), "a_wv": _ca(Wv[:, sl]),
            "a_bq": _ca(bq[None, sl]), "a_bk": _ca(bk[None, sl]),
            "a_bv": _ca(bv[None, sl]),
            "b_Wo": np.asarray(inp["L0_attn_Wo"], np.float32),
            "b_W1": np.asarray(inp["L0_attn_W1"], np.float32),
            "b_W2": np.asarray(inp["L0_attn_W2"], np.float32),
            "b_bo": _ca(np.asarray(inp["L0_attn_bo"], np.float32)[None, :]),
            "b_b1": _ca(np.asarray(inp["L0_attn_b1"], np.float32).reshape(32, 128)),
            "b_b2": _ca(np.asarray(inp["L0_attn_b2"], np.float32)[None, :]),
            "b_l1g": _ca(np.asarray(inp["L0_attn_ln1_g"], np.float32)[None, :]),
            "b_l1b": _ca(np.asarray(inp["L0_attn_ln1_b"], np.float32)[None, :]),
            "b_l2g": _ca(np.asarray(inp["L0_attn_ln2_g"], np.float32)[None, :]),
            "b_l2b": _ca(np.asarray(inp["L0_attn_ln2_b"], np.float32)[None, :]),
            "gidx": gi,
            "f_wq": np.asarray(inp["L0_ffn_Wq"], np.float32),
            "f_wk": _ca(Wkv[:, 0:512]), "f_wv": _ca(Wkv[:, 512:1024]),
            "f_wo": _ca(np.asarray(inp["L0_ffn_Wo"], np.float32).reshape(512, 1024)),
            "f_ws": np.asarray(inp["L0_ffn_Ws"], np.float32),
            "f_bo8": np.asarray(inp["L0_ffn_bo"], np.float32),
            "f_sel": sel8c,
            "g_wq": _ca(wqkv[:, 0:1024]), "g_wk": _ca(wqkv[:, 1024:2048]),
            "g_wv": _ca(wqkv[:, 2048:3072]),
            "g_wo": np.asarray(inp["L1_attn_w_out"], np.float32),
            "m_gw": np.asarray(inp["L1_ffn_gate_w"], np.float32),
            "m_gb": _ca(np.asarray(inp["L1_ffn_bias"], np.float32)[None, :]),
            "m_uw": np.asarray(inp["L1_ffn_up_w"], np.float32),
            "m_ub": np.asarray(inp["L1_ffn_up_b"], np.float32),
            "m_dw": np.asarray(inp["L1_ffn_down_w"], np.float32),
            "m_db": np.asarray(inp["L1_ffn_down_b"], np.float32),
            "m_su": np.asarray(inp["L1_ffn_sw_up"], np.float32),
            "m_sbu": _ca(np.asarray(inp["L1_ffn_sb_up"], np.float32).reshape(32, 128)),
            "m_sd": np.asarray(inp["L1_ffn_sw_down"], np.float32),
            "m_sbd": _ca(np.asarray(inp["L1_ffn_sb_down"], np.float32).reshape(8, 128)),
            "m_sel": sel8w,
        })
    res = run_bass_kernel_spmd(nc, maps, core_ids=list(range(NCORE)))
    return np.stack([np.concatenate([res.results[4 * b + j]["yout"]
                                     for j in range(4)], 0) for b in range(2)])


# version 07
# speedup vs baseline: 1.1569x; 1.0047x over previous
import numpy as np
from contextlib import ExitStack
import concourse.bass as bass
import concourse.mybir as mybir
from concourse.bass_utils import run_bass_kernel_spmd
from concourse.bass_core_rust import IndirectOffsetOnAxis

I32 = None

F32 = mybir.dt.float32
BF16 = mybir.dt.bfloat16
AF = mybir.ActivationFunctionType
ALU = mybir.AluOpType
AX = mybir.AxisListType
I32 = mybir.dt.int32

ENGINES = ("sync", "tensor", "vector", "scalar", "gpsimd")

B, S, D = 2, 8192, 1024
NCORE = 8
TLOC = B * S // NCORE


class Sem:
    __slots__ = ("h", "n")

    def __init__(self, h):
        self.h = h
        self.n = 0


class Prog:

    def __init__(self, name="prog"):
        self.nc = bass.Bass("TRN2", target_bir_lowering=False, debug=False)
        self.st = ExitStack()
        self.q = {e: [] for e in ENGINES}
        self._si = 0
        self._ni = 0

    pref = ""

    def sem(self, name=None):
        name = self.pref + (name or f"sm{self._si}")
        self._si += 1
        return Sem(self.st.enter_context(self.nc.semaphore(name)))

    def sbuf(self, shape, dt, name=None):
        name = self.pref + (name or f"sb{self._ni}")
        self._ni += 1
        cm = self.nc.sbuf_tensor(name, shape, dt)
        t = cm.__enter__()
        if not hasattr(self, "_open"):
            self._open = {}
        self._open[id(t)] = cm
        if not hasattr(self, "_order"):
            self._order = []
        self._order.append(t)
        return t

    def free(self, t):
        cm = self._open.pop(id(t))
        if hasattr(self, "_order"):
            try:
                self._order.remove(t)
            except ValueError:
                pass
        cm.__exit__(None, None, None)

    def marker(self):
        return len(getattr(self, "_order", []))

    def free_since(self, mark):
        order = getattr(self, "_order", [])
        while len(order) > mark:
            self.free(order[-1])

    def psum(self, shape, dt=F32, name=None):
        name = self.pref + (name or f"ps{self._ni}")
        self._ni += 1
        cm = self.nc.psum_tensor(name, shape, dt)
        t = cm.__enter__()
        if not hasattr(self, "_open"):
            self._open = {}
        self._open[id(t)] = cm
        if not hasattr(self, "_order"):
            self._order = []
        self._order.append(t)
        return t

    def dram(self, name, shape, dt, kind):
        return self.nc.dram_tensor(self.pref + name, shape, dt, kind=kind).ap()

    def do(self, eng, meth, *args, inc=None, inc_v=1, **kw):
        if inc is not None:
            inc.n += inc_v

            def f(E, meth=meth, args=args, kw=kw, h=inc.h, v=inc_v):
                getattr(E, meth)(*args, **kw).then_inc(h, v)
        else:
            def f(E, meth=meth, args=args, kw=kw):
                getattr(E, meth)(*args, **kw)
        self.q[eng].append(f)

    def wait(self, eng, sem, n=None):
        v = sem.n if n is None else n
        if v <= 0:
            return

        def f(E, h=sem.h, v=v):
            E.wait_ge(h, v)
        self.q[eng].append(f)

    def dma(self, eng, out, in_, sem=None, **kw):
        self.do(eng, "dma_start", inc=sem, inc_v=16, out=out, in_=in_, **kw)

    def dodr(self, eng, meth, *args, **kw):
        if not hasattr(self, "_dr"):
            self._dr = {}
        if eng not in self._dr:
            self._dr[eng] = self.sem(f"dr_{eng}")
        s = self._dr[eng]
        self.do(eng, meth, *args, inc=s, **kw)
        self.wait(eng, s)

    def finalize(self):
        with self.nc.Block() as blk:
            for e in ENGINES:
                ops = self.q[e]
                if ops:
                    getattr(blk, e)(lambda E, ops=ops: [f(E) for f in ops])
        for cm in reversed(list(getattr(self, "_open", {}).values())):
            cm.__exit__(None, None, None)
        self._open = {}
        self.st.close()
        return self.nc



def mk_consts(p):
    c = {}
    c["ones_row"] = p.sbuf([1, 128], BF16, "ones_row")
    c["ones_col"] = p.sbuf([128, 1], BF16, "ones_col")
    c["ident"] = p.sbuf([128, 128], BF16, "ident")
    c["identf"] = p.sbuf([128, 128], F32, "identf")
    c["eps_ln"] = p.sbuf([128, 1], F32, "eps_ln")
    c["eps6"] = p.sbuf([128, 1], F32, "eps6")
    g = "gpsimd"
    p.do(g, "memset", c["ones_row"][:], 1.0)
    p.do(g, "memset", c["ones_col"][:], 1.0)
    p.do(g, "memset", c["eps_ln"][:], 1e-5)
    p.do(g, "memset", c["eps6"][:], 1e-6)
    p.do(g, "memset", c["ident"][:], 1.0)
    p.do(g, "affine_select", c["ident"][:], c["ident"][:], [[1, 128]], ALU.is_equal,
         0.0, base=0, channel_multiplier=-1)
    p.do(g, "memset", c["identf"][:], 1.0)
    c["sem"] = p.sem("consts")
    p.do(g, "affine_select", c["identf"][:], c["identf"][:], [[1, 128]], ALU.is_equal,
         0.0, base=0, channel_multiplier=-1, inc=c["sem"])
    return c


def ln_token_major(p, x_ap, out_ap, tmp, sv, ss, g_bcast=None, b_bcast=None,
                   nfree=1024):
    v, s = "vector", "scalar"
    inv_n = float(1.0 / nfree)
    p.do(v, "reduce_sum", tmp["mean"][:], x_ap, axis=AX.X, inc=sv)
    p.wait(v, sv)
    p.do(v, "tensor_scalar_mul", tmp["mean"][:], tmp["mean"][:], inv_n, inc=sv)
    p.wait(v, sv)
    p.wait(v, ss)
    p.do(v, "tensor_scalar", tmp["xc"][:], x_ap, tmp["mean"][:], None,
         ALU.subtract, inc=sv)
    p.wait(s, sv)
    p.do(s, "activation", tmp["sq"][:], tmp["xc"][:], AF.Square,
         accum_out=tmp["ssq"][:], inc=ss)
    p.wait(s, ss)
    p.do(s, "activation", tmp["rstd"][:], tmp["ssq"][:], AF.Sqrt,
         bias=tmp["eps_ln"][:], scale=inv_n, inc=ss)
    p.wait(v, ss)
    p.do(v, "reciprocal", tmp["rstd"][:], tmp["rstd"][:], inc=sv)
    p.wait(v, sv)
    if g_bcast is None:
        p.do(v, "tensor_scalar_mul", out_ap, tmp["xc"][:], tmp["rstd"][:], inc=sv)
    else:
        p.dodr(v, "tensor_scalar_mul", tmp["xc"][:], tmp["xc"][:], tmp["rstd"][:])
        p.dodr(v, "tensor_mul", tmp["xc"][:], tmp["xc"][:], g_bcast)
        p.do(v, "tensor_add", out_ap, tmp["xc"][:], b_bcast, inc=sv)


def mk_ln_tmp(p, c, nfree=1024):
    return {
        "mean": p.sbuf([128, 1], F32), "ssq": p.sbuf([128, 1], F32),
        "rstd": p.sbuf([128, 1], F32), "xc": p.sbuf([128, nfree], F32),
        "sq": p.sbuf([128, nfree], F32), "eps_ln": c["eps_ln"],
    }


def bcast_row(p, c, row_ap, out_sbuf, ps_bank, sv, st, n=1024, eng="vector"):
    for j in range(0, n, 512):
        w = min(512, n - j)
        p.do("tensor", "matmul", ps_bank[:, 0:w], c["ones_row_f"][:],
             row_ap[:, j:j + w], start=True, stop=True, inc=st)
        p.wait(eng, st)
        p.do(eng, "tensor_copy", out_sbuf[:, j:j + w], ps_bank[:, 0:w], inc=sv)
        p.wait("tensor", sv)



DEBUG_L0B = False


def build_l0b(p=None, io=None):
    fused = p is not None
    if not fused:
        p = Prog("l0b")
    io = io or {}
    mark = p.marker()
    D_ = lambda n, sh, dt: io[n] if n in io else p.dram(n, sh, dt, "ExternalInput")
    x = D_("x", [TLOC, D], F32)
    if fused:
        hdT = io["hdT_view"].rearrange("r (w t) -> (r w) t", t=256)
        gidx = io["gidx"]
    else:
        hdT = D_("hdT", [D, TLOC], BF16)
    Wo = D_("Wo", [D, D], F32)
    W1 = D_("W1", [D, 4096], F32)
    W2 = D_("W2", [4096, D], F32)
    bo = D_("bo", [1, D], F32)
    b1 = D_("b1", [32, 128], F32)
    b2 = D_("b2", [1, D], F32)
    ln1g = D_("ln1g", [1, D], F32)
    ln1b = D_("ln1b", [1, D], F32)
    ln2g = D_("ln2g", [1, D], F32)
    ln2b = D_("ln2b", [1, D], F32)
    xout = io["xout"] if "xout" in io else p.dram("xout", [TLOC, D], F32,
                                                  "ExternalOutput")
    if "gate" in io:
        gs, gv = io["gate"]
        for e in ENGINES:
            p.wait(e, gs, gv)
    if DEBUG_L0B:
        dbg = {nm: p.dram("dbg_" + nm, [256, D], F32, "ExternalOutput")
               for nm in ("xln", "hpre", "h", "y")}

    c = mk_consts(p)
    c["ones_row_f"] = p.sbuf([1, 128], F32, "ones_row_f")
    p.do("gpsimd", "memset", c["ones_row_f"][:], 1.0, inc=c["sem"])

    s_in = p.sem("s_in")
    s_w2 = p.sem("s_w2")
    s_t = p.sem("s_t")
    s_v = p.sem("s_v")
    s_s = p.sem("s_s")
    s_o = p.sem("s_o")
    if fused:
        gidx_sb = p.sbuf([128, 64], I32, "gidx_sb")
        p.dma("gpsimd", gidx_sb[:], gidx, s_o)
        p.wait("gpsimd", s_o)

    WoB = [p.sbuf([128, D], BF16) for _ in range(8)]
    W1B = [p.sbuf([128, 4096], BF16) for _ in range(8)]
    bc = {}
    for nm in ("bo", "b2", "l1g", "l1b", "l2g", "l2b"):
        bc[nm] = p.sbuf([128, D], F32, f"bc_{nm}")
    b1c = p.sbuf([128, 32], F32, "b1c")
    GT = 256
    x_g = p.sbuf([128, 2 * 1024], F32, "x_g")
    xln_g = p.sbuf([128, 2 * 1024], F32, "xln_g")
    hpre_g = p.sbuf([128, 2 * 1024], F32, "hpre_g")
    h_g = p.sbuf([128, 2 * 1024], F32, "h_g")
    hbf_g = p.sbuf([128, 2 * 1024], BF16, "hbf_g")
    hdT_g = [p.sbuf([128, GT], BF16) for _ in range(8)]
    hT_g = [p.sbuf([128, GT], BF16) for _ in range(8)]
    uT = [p.sbuf([128, GT], BF16) for _ in range(2)]
    W2B = [p.sbuf([128, D], BF16) for _ in range(2)]
    w2stg = [p.sbuf([128, D], F32) for _ in range(2)]
    y_g = p.sbuf([128, 2 * 1024], F32, "y_g")
    lt = mk_ln_tmp(p, c)
    psA = p.psum([128, 512], F32, "psA")
    psT = p.psum([128, 128], BF16, "psT")
    psU = [p.psum([128, GT], F32) for _ in range(2)]
    psY = [p.psum([128, 512], F32) for _ in range(4)]

    rows = {"bo": x_g[0:1, 0:1024], "b2": xln_g[0:1, 0:1024],
            "l1g": hpre_g[0:1, 0:1024], "l1b": h_g[0:1, 0:1024],
            "l2g": y_g[0:1, 0:1024], "l2b": y_g[0:1, 1024:2048]}
    for nm, ap in (("bo", bo), ("b2", b2), ("l1g", ln1g), ("l1b", ln1b),
                   ("l2g", ln2g), ("l2b", ln2b)):
        p.dma("sync", rows[nm], ap, s_in)
    b1s = lt["xc"][0:32, 0:128]
    p.dma("sync", b1s, b1, s_in)

    w2c_cnt = []

    def cast_weight_chunks(dst_list, src, ncols):
        cnt = w2c_cnt
        for i, dst in enumerate(dst_list):
            for k in range(0, ncols, 1024):
                idx = len(cnt)
                buf = idx % 2
                if idx >= 2:
                    p.wait("sync", s_v, cnt[idx - 2])
                else:
                    p.wait("sync", s_v)
                p.dma("sync", w2stg[buf][:], src[i * 128:(i + 1) * 128, k:k + 1024],
                      s_in)
                p.wait("vector", s_in)
                p.do("vector", "tensor_copy", dst[:, k:k + 1024], w2stg[buf][:],
                     inc=s_v)
                cnt.append(s_v.n)
    cast_weight_chunks(WoB, Wo, 1024)
    cast_weight_chunks(W1B, W1, 4096)

    p.wait("tensor", s_in, 7 * 16)
    p.wait("tensor", c["sem"])
    p.do("tensor", "transpose", psA[:, 0:32], b1s, c["identf"][0:32, 0:32], inc=s_t)
    p.wait("vector", s_t)
    p.do("vector", "tensor_copy", b1c[:], psA[:, 0:32], inc=s_v)
    p.wait("tensor", s_v)

    for nm in ("bo", "b2", "l1g", "l1b", "l2g", "l2b"):
        bcast_row(p, c, rows[nm], bc[nm], psA, s_v, s_t)
    prep_sv = s_v.n

    NG = TLOC // GT

    def ts(ti):
        return slice(ti * 1024, (ti + 1) * 1024)

    for g in range(NG):
        t0 = g * GT
        if g == 0:
            p.wait("gpsimd", s_v, prep_sv)
            p.wait("sync", s_v, prep_sv)
        for ti in range(2):
            p.dma("sync", x_g[:, ts(ti)], x[t0 + ti * 128:t0 + (ti + 1) * 128, :], s_in)
        for cc in range(8):
            if fused:
                p.do("gpsimd", "indirect_dma_start", hdT_g[cc][:], None,
                     hdT,
                     IndirectOffsetOnAxis(gidx_sb[:, cc * 8 + g:cc * 8 + g + 1], 0),
                     inc=s_o, inc_v=16)
            else:
                p.dma("gpsimd", hdT_g[cc][:], hdT[cc * 128:(cc + 1) * 128, t0:t0 + GT],
                      s_o)

        for ti in range(2):
            p.wait("vector", s_in)
            ln_token_major(p, x_g[:, ts(ti)], xln_g[:, ts(ti)], lt, s_v, s_s)

        p.wait("tensor", s_o)
        p.wait("tensor", s_v)
        for ti in range(2):
            for f in range(2):
                ps = psY[2 * ti + f]
                for cc in range(8):
                    p.do("tensor", "matmul", ps[:],
                         hdT_g[cc][:, ti * 128:(ti + 1) * 128],
                         WoB[cc][:, f * 512:(f + 1) * 512],
                         start=(cc == 0), stop=(cc == 7), inc=(s_t if cc == 7 else None))
            p.wait("vector", s_t)
            for f in range(2):
                ps = psY[2 * ti + f]
                fs = slice(ti * 1024 + f * 512, ti * 1024 + (f + 1) * 512)
                p.dodr("vector", "tensor_add", hpre_g[:, fs], ps[:],
                       bc["bo"][:, f * 512:(f + 1) * 512])
                p.do("vector", "tensor_add", hpre_g[:, fs], hpre_g[:, fs],
                     xln_g[:, fs], inc=s_v)
            ln_token_major(p, hpre_g[:, ts(ti)], h_g[:, ts(ti)], lt, s_v, s_s,
                           g_bcast=bc["l1g"][:], b_bcast=bc["l1b"][:])
            p.wait("vector", s_v)
            p.do("vector", "tensor_copy", hbf_g[:, ts(ti)], h_g[:, ts(ti)], inc=s_v)
        p.wait("tensor", s_v)
        for cc in range(8):
            for ti in range(2):
                p.do("tensor", "transpose", psT[:],
                     hbf_g[:, ti * 1024 + cc * 128:ti * 1024 + (cc + 1) * 128],
                     c["ident"][:], inc=s_t)
                p.wait("scalar", s_t)
                p.do("scalar", "copy", hT_g[cc][:, ti * 128:(ti + 1) * 128], psT[:],
                     inc=s_s)
                p.wait("tensor", s_s)

        for i in range(32):
            buf = i % 2
            sidx = len(w2c_cnt)
            sb = sidx % 2
            p.wait("sync", s_v, w2c_cnt[sidx - 2])
            p.dma("sync", w2stg[sb][:], W2[i * 128:(i + 1) * 128, :], s_w2)
            p.wait("vector", s_w2)
            p.wait("vector", s_t)
            p.do("vector", "tensor_copy", W2B[buf][:], w2stg[sb][:], inc=s_v)
            w2c_cnt.append(s_v.n)
            p.wait("tensor", s_s)
            for cc in range(8):
                p.do("tensor", "matmul", psU[buf][:], W1B[cc][:, i * 128:(i + 1) * 128],
                     hT_g[cc][:], start=(cc == 0), stop=(cc == 7),
                     inc=(s_t if cc == 7 else None))
            p.wait("scalar", s_t)
            p.do("scalar", "activation", uT[buf][:], psU[buf][:], AF.Gelu,
                 bias=b1c[:, i:i + 1], inc=s_s)
            p.wait("tensor", s_s)
            p.wait("tensor", s_v)
            for ti in range(2):
                for f in range(2):
                    p.do("tensor", "matmul", psY[2 * ti + f][:],
                         uT[buf][:, ti * 128:(ti + 1) * 128],
                         W2B[buf][:, f * 512:(f + 1) * 512],
                         start=(i == 0), stop=(i == 31),
                         inc=(s_t if i == 31 else None))
        p.wait("vector", s_t)
        if DEBUG_L0B and g == 0:
            p.wait("gpsimd", s_v)
            for ti in range(2):
                tsl = slice(ti * 128, (ti + 1) * 128)
                p.dma("gpsimd", dbg["xln"][tsl, :], xln_g[:, ts(ti)], s_o)
                p.dma("gpsimd", dbg["hpre"][tsl, :], hpre_g[:, ts(ti)], s_o)
                p.dma("gpsimd", dbg["h"][tsl, :], h_g[:, ts(ti)], s_o)
        for ti in range(2):
            for f in range(2):
                fs = slice(ti * 1024 + f * 512, ti * 1024 + (f + 1) * 512)
                p.dodr("vector", "tensor_add", y_g[:, fs], psY[2 * ti + f][:],
                       bc["b2"][:, f * 512:(f + 1) * 512])
            p.do("vector", "tensor_add", y_g[:, ts(ti)], y_g[:, ts(ti)],
                 h_g[:, ts(ti)], inc=s_v)
            if DEBUG_L0B and g == 0:
                p.wait("gpsimd", s_v)
                p.dma("gpsimd", dbg["y"][ti * 128:(ti + 1) * 128, :],
                      y_g[:, ts(ti)], s_o)
                p.wait("vector", s_o)
            ln_token_major(p, y_g[:, ts(ti)], y_g[:, ts(ti)], lt, s_v, s_s,
                           g_bcast=bc["l2g"][:], b_bcast=bc["l2b"][:])
            p.wait("vector", s_v)
            p.do("vector", "tensor_add", y_g[:, ts(ti)], y_g[:, ts(ti)],
                 x_g[:, ts(ti)], inc=s_v)
            p.wait("gpsimd", s_v)
            p.dma("gpsimd", xout[t0 + ti * 128:t0 + (ti + 1) * 128, :],
                  y_g[:, ts(ti)], s_o)
        p.wait("sync", s_v)
        p.wait("gpsimd", s_o)

    p.wait("gpsimd", s_o)
    if fused:
        out_sem = (s_o, s_o.n)
        p.free_since(mark)
        return out_sem
    return p.finalize()



DEBUG_FLASH = False
DEBUG_FLASH2 = False
FLASH_STOP_AFTER = None
L1F_STOP = None
L1F_NE = 9
DEBUG_L1F = False
DEBUG_L0A = False


def build_flash(kind, p=None, io=None):
    l1a = kind == "l1a"
    H = 16 if l1a else 8
    QF = 64 * H
    NP = H // 2
    SB = S
    fused = p is not None
    if not fused:
        p = Prog(kind)
    io = io or {}
    mark = p.marker()
    D_ = lambda n, sh, dt: io[n] if n in io else p.dram(n, sh, dt, "ExternalInput")
    xb = D_("xb", [SB, D], F32)
    xloc = D_("xloc", [TLOC, D], F32)
    wq = D_("wq", [D, QF], F32)
    wk = D_("wk", [D, QF], F32)
    wv = D_("wv", [D, QF], F32)
    wo = D_("wo", [D, D] if l1a else [QF, D], F32)
    if not l1a:
        ws = D_("ws", [D, 8], F32)
        bo8 = D_("bo8", [8, D], F32)
    xout = io["xout"] if "xout" in io else p.dram("xout", [TLOC, D], F32,
                                                  "ExternalOutput")
    if "gate" in io:
        gs, gv = io["gate"]
        for e in ENGINES:
            p.wait(e, gs, gv)
    if DEBUG_FLASH:
        dbg_out = p.dram("dbg_outT", [QF, TLOC], F32, "ExternalOutput")
        dbg_q = p.dram("dbg_qT", [QF, TLOC], F32, "ExternalOutput")
    if DEBUG_FLASH2:
        dbg_kts = p.dram("dbg_kts", [QF, SB], BF16, "ExternalOutput")
        dbg_vsc = p.dram("dbg_vsc", [H, SB // 128, 128, 64], BF16, "ExternalOutput")
        dbg_q2 = p.dram("dbg_qT2", [QF, TLOC], BF16, "ExternalOutput")
    kts = p.dram("kts", [QF, SB], BF16, "Internal")
    vsc = p.dram("vsc", [H, SB // 128, 128, 64], BF16, "Internal")

    c = mk_consts(p)
    c["ones_row_f"] = p.sbuf([1, 128], F32, "ones_row_f")
    p.do("gpsimd", "memset", c["ones_row_f"][:], 1.0, inc=c["sem"])
    if not l1a:
        sel8_in = io["sel8c"] if "sel8c" in io else p.dram("sel8c", [8, 512], F32,
                                                           "ExternalInput")
        c["sel8"] = p.sbuf([8, 512], F32, "sel8")

    s_in = p.sem("s_in")
    s_vl = p.sem("s_vl")
    s_kv = p.sem("s_kv")
    s_t = p.sem("s_t")
    s_v = p.sem("s_v")
    s_s = p.sem("s_s")
    s_o = p.sem("s_o")
    if not l1a:
        p.dma("sync", c["sel8"][:], sel8_in, s_in)

    QTp = [p.sbuf([128, TLOC], BF16) for _ in range(NP)]
    outTp = [p.sbuf([128, TLOC], BF16) for _ in range(NP)]
    KTp = p.sbuf([128, SB], BF16, "KTp")
    V1 = p.sbuf([128, 64 * 65], BF16, "V1")
    Pt = [p.sbuf([128, 1024], BF16) for _ in range(2)]
    xlnT_g = [p.sbuf([128, 512], BF16) for _ in range(8)]
    xt_f = [p.sbuf([128, 1024], F32) for _ in range(2)]
    xln_f = p.sbuf([128, 1024], F32, "xln_f")
    xbf = p.sbuf([128, 1024], BF16, "xbf")
    evac = [p.sbuf([128, 512], BF16, "evac%d" % _i) for _i in range(2)]
    otmp = p.sbuf([64, 512], F32, "otmp")
    otmp2 = p.sbuf([64, 512], F32, "otmp2")
    rd = p.sbuf([1, 512], F32, "rd")
    lt = mk_ln_tmp(p, c)
    if not l1a:
        s_tok = p.sbuf([128, 8], F32, "s_tok")
        g_m1 = p.sbuf([128, 1], F32)
        g_t1 = p.sbuf([128, 8], F32)
        g_t2 = p.sbuf([128, 8], F32)
        g_bf = p.sbuf([128, 8], BF16)
        gatedT = p.sbuf([8, TLOC], F32, "gatedT")
        gatedT_bf = p.sbuf([8, TLOC], BF16, "gatedT_bf")
        bo8B = p.sbuf([8, D], BF16, "bo8B")

    psT = p.psum([128, 128], BF16, "psT")
    psA = p.psum([128, 512], F32, "psA")
    psB = p.psum([128, 512], F32, "psB")
    psC = p.psum([128, 512], F32, "psC")

    WqB = [p.sbuf([128, QF], BF16) for _ in range(8)]
    WkB = [p.sbuf([128, QF], BF16) for _ in range(8)]
    WvB = [p.sbuf([128, QF], BF16) for _ in range(8)]
    wstg = [p.sbuf([128, 1024], F32) for _ in range(2)]
    if not l1a:
        WsB = [p.sbuf([128, 8], BF16) for _ in range(8)]

    _cstC = []

    def cast_chunks(dst_list, src, ncols):
        for i, dst in enumerate(dst_list):
            for k in range(0, ncols, 1024):
                idx = len(_cstC)
                buf = idx % 2
                if idx >= 2:
                    p.wait("sync", s_v, _cstC[idx - 2])
                else:
                    p.wait("sync", s_v)
                p.dma("sync", wstg[buf][:, 0:min(1024, ncols)],
                      src[i * 128:(i + 1) * 128, k:k + min(1024, ncols)], s_in)
                p.wait("vector", s_in)
                p.do("vector", "tensor_copy", dst[:, k:k + min(1024, ncols)],
                     wstg[buf][:, 0:min(1024, ncols)], inc=s_v)
                _cstC.append(s_v.n)

    cast_chunks(WqB, wq, QF)
    cast_chunks(WkB, wk, QF)
    cast_chunks(WvB, wv, QF)
    if not l1a:
        for i in range(8):
            p.wait("sync", s_v)
            p.dma("sync", wstg[i % 2][:, 0:8], ws[i * 128:(i + 1) * 128, :], s_in)
            p.wait("vector", s_in)
            p.do("vector", "tensor_copy", WsB[i][:], wstg[i % 2][:, 0:8], inc=s_v)
        p.dma("gpsimd", bo8B[:], bo8, s_vl)

    def stream_tiles(src, t0, nt, emit_consumer, local_q):
        for q in range(nt):
            buf = q % 2
            p.wait("sync", s_v)
            p.dma("sync", xt_f[buf][:], src[t0 + q * 128:t0 + (q + 1) * 128, :], s_in)
            p.wait("vector", s_in)
            ln_token_major(p, xt_f[buf][:], xln_f[:], lt, s_v, s_s)
            p.wait("vector", s_v)
            p.wait("vector", s_t)
            p.do("vector", "tensor_copy", xbf[:], xln_f[:], inc=s_v)
            p.wait("tensor", s_v)
            p.wait("tensor", c["sem"])
            for cc in range(8):
                p.do("tensor", "transpose", psT[:], xbf[:, cc * 128:(cc + 1) * 128],
                     c["ident"][:], inc=s_t)
                p.wait("scalar", s_t)
                p.do("scalar", "copy", xlnT_g[cc][:, q * 128:(q + 1) * 128], psT[:],
                     inc=s_s)
                p.wait("tensor", s_s)
        emit_consumer()

    kv_cnt = []

    def kv_evac_wait():
        if len(kv_cnt) >= 2:
            p.wait("scalar", s_kv, kv_cnt[-2])
        else:
            p.wait("scalar", s_kv)
        return evac[len(kv_cnt) % 2]

    def kv_consumer(t0):
        for kf in range(QF // 128):
            p.wait("tensor", s_s)
            for cc in range(8):
                p.do("tensor", "matmul", psA[:], WkB[cc][:, kf * 128:(kf + 1) * 128],
                     xlnT_g[cc][:], start=(cc == 0), stop=(cc == 7),
                     inc=(s_t if cc == 7 else None))
            p.wait("scalar", s_t)
            eb = kv_evac_wait()
            p.do("scalar", "copy", eb[:], psA[:], inc=s_s)
            p.wait("gpsimd", s_s)
            p.dma("gpsimd", kts[kf * 128:(kf + 1) * 128, t0:t0 + 512], eb[:], s_kv)
            kv_cnt.append(s_kv.n)
        for q in range(4):
            for half in range(QF // 512):
                fs = slice(half * 512, (half + 1) * 512)
                p.wait("tensor", s_s)
                for cc in range(8):
                    p.do("tensor", "matmul", psB[:],
                         xlnT_g[cc][:, q * 128:(q + 1) * 128], WvB[cc][:, fs],
                         start=(cc == 0), stop=(cc == 7),
                         inc=(s_t if cc == 7 else None))
                p.wait("scalar", s_t)
                eb = kv_evac_wait()
                p.do("scalar", "copy", eb[:], psB[:], inc=s_s)
                p.wait("gpsimd", s_s)
                j = (t0 + q * 128) // 128
                p.dma("gpsimd",
                      vsc[half * 8:(half + 1) * 8, j].rearrange("h p d -> p h d"),
                      eb[:].rearrange("p (h d) -> p h d", d=64),
                      s_kv)
                kv_cnt.append(s_kv.n)

    for g in range(SB // 512):
        stream_tiles(xb, g * 512, 4, lambda g=g: kv_consumer(g * 512), False)

    def q_consumer(t0):
        for qf in range(QF // 128):
            p.wait("tensor", s_s)
            for cc in range(8):
                p.do("tensor", "matmul", psA[:], WqB[cc][:, qf * 128:(qf + 1) * 128],
                     xlnT_g[cc][:], start=(cc == 0), stop=(cc == 7),
                     inc=(s_t if cc == 7 else None))
            p.wait("scalar", s_t)
            p.do("scalar", "activation", QTp[qf][:, t0:t0 + 512], psA[:], AF.Copy,
                 scale=0.125, inc=s_s)
            p.wait("tensor", s_s)
        if (not l1a) and FLASH_STOP_AFTER != "nogate":
            for q in range(4):
                tsl = slice(t0 + q * 128, t0 + (q + 1) * 128)
                for cc in range(8):
                    p.do("tensor", "matmul", psC[0:128, 0:8],
                         xlnT_g[cc][:, q * 128:(q + 1) * 128], WsB[cc][:],
                         start=(cc == 0), stop=(cc == 7),
                         inc=(s_t if cc == 7 else None))
                p.wait("scalar", s_t)
                p.do("scalar", "activation", s_tok[:], psC[0:128, 0:8], AF.Sigmoid,
                     inc=s_s)
                if FLASH_STOP_AFTER == "gate1":
                    continue
                v = "vector"
                p.wait(v, s_s)
                p.dodr(v, "reduce_max", g_m1[:], s_tok[:], axis=AX.X)
                p.dodr(v, "tensor_scalar", g_t1[:], s_tok[:], g_m1[:], None,
                       ALU.is_equal)
                p.dodr(v, "scalar_tensor_tensor", g_t2[:], g_t1[:], -1e30, s_tok[:],
                       ALU.mult, ALU.add)
                p.dodr(v, "reduce_max", g_m1[:], g_t2[:], axis=AX.X)
                p.dodr(v, "tensor_scalar", g_t1[:], g_t2[:], g_m1[:], None,
                       ALU.is_equal)
                p.dodr(v, "scalar_tensor_tensor", g_t2[:], g_t1[:], -1e30, g_t2[:],
                       ALU.mult, ALU.add)
                p.dodr(v, "reduce_max", g_m1[:], g_t2[:], axis=AX.X)
                p.dodr(v, "tensor_scalar", g_t1[:], s_tok[:], g_m1[:], None,
                       ALU.is_ge)
                p.dodr(v, "tensor_mul", g_t2[:], s_tok[:], g_t1[:])
                p.do(v, "tensor_copy", g_bf[:], g_t2[:], inc=s_v)
                if FLASH_STOP_AFTER == "gate2":
                    continue
                p.wait("tensor", s_v)
                p.do("tensor", "transpose", psT[0:8, :], g_bf[:], c["ident"][:],
                     inc=s_t)
                p.wait("scalar", s_t)
                p.do("scalar", "copy", gatedT[0:8, tsl], psT[0:8, :], inc=s_s)
                p.wait("vector", s_s)
                p.do("vector", "tensor_copy", gatedT_bf[0:8, tsl], gatedT[0:8, tsl],
                     inc=s_v)
                p.wait("tensor", s_v)
                p.wait("tensor", s_s)

    if FLASH_STOP_AFTER == "p2":
        p.wait("gpsimd", s_kv, s_kv.n)
        p.dma("gpsimd", xout[0:128, :], xt_f[0][:], s_o)
        p.wait("gpsimd", s_o)
        return p.finalize()
    for g in range(TLOC // 512):
        stream_tiles(xloc, g * 512, 4,
                     lambda g=g: q_consumer(g * 512), True)

    kv_done = s_kv.n
    q_done = s_s.n
    if DEBUG_FLASH2:
        p.wait("gpsimd", s_kv, kv_done)
        p.wait("gpsimd", s_s, q_done)
        p.dma("gpsimd", dbg_kts, kts, s_o)
        p.dma("gpsimd", dbg_vsc, vsc, s_o)
        for j in range(NP):
            p.dma("gpsimd", dbg_q2[j * 128:(j + 1) * 128, :], QTp[j][:], s_o)
    if not l1a:
        for t in reversed(WsB):
            p.free(t)
    for t in reversed(wstg):
        p.free(t)
    for t in reversed(WvB):
        p.free(t)
    for t in reversed(WkB):
        p.free(t)
    for t in reversed(WqB):
        p.free(t)
    psW = [p.psum([128, 1024], F32) for _ in range(2)]
    WoB_n = 8 if l1a else 4
    WoB = [p.sbuf([128, D], BF16) for _ in range(WoB_n)]
    wstg2 = [p.sbuf([128, 1024], F32) for _ in range(2)]
    p.wait("sync", s_t)
    for i in range(WoB_n):
        buf = i % 2
        p.wait("sync", s_v)
        p.dma("sync", wstg2[buf][:], wo[i * 128:(i + 1) * 128, :], s_in)
        p.wait("vector", s_in)
        p.do("vector", "tensor_copy", WoB[i][:], wstg2[buf][:], inc=s_v)

    p.do("gpsimd", "memset",
         V1[:].rearrange("p (j x) -> p j x", x=65)[:, :, 64:65], 1.0, inc=s_kv,
         inc_v=1)
    ones_done = s_kv.n

    if FLASH_STOP_AFTER == "p3":
        p.wait("gpsimd", s_kv, kv_done)
        p.wait("gpsimd", s_s, q_done)
        p.dma("gpsimd", xout[0:128, :], xt_f[0][:], s_o)
        p.wait("gpsimd", s_o)
        return p.finalize()

    exp_cnt = []
    pv_cnt = []
    p.wait("sync", s_kv, kv_done)
    p.wait("gpsimd", s_kv, kv_done)
    for hp in range(NP):
        p.wait("sync", s_t)
        p.dma("sync", KTp[:], kts[hp * 128:(hp + 1) * 128, :], s_in)
        for hh in range(2):
            h = 2 * hp + hh
            base = 64 * hh
            p.wait("gpsimd", s_t)
            p.wait("gpsimd", s_kv, ones_done)
            p.do("gpsimd", "dma_start", inc=s_vl, inc_v=16,
                 out=V1[:].rearrange("p (j x) -> p j x", x=65)[:, :, 0:64],
                 in_=vsc[h].rearrange("j p d -> p j d"))
            for qtp in range(TLOC // 1024):
                q0 = qtp * 1024
                p.wait("tensor", s_v)
                p.wait("tensor", s_in)
                p.wait("tensor", s_vl)
                exp_cnt.clear()
                NJ = SB // 128

                def emit_scores(j):
                    b = j % 2
                    if j >= 2:
                        p.wait("tensor", s_s, exp_cnt[j - 2])
                    for half in range(2):
                        p.do("tensor", "matmul",
                             psW[b][:, half * 512:(half + 1) * 512],
                             KTp[base:base + 64, j * 128:(j + 1) * 128],
                             QTp[hp][base:base + 64,
                                     q0 + half * 512:q0 + (half + 1) * 512],
                             start=True, stop=True,
                             inc=(s_t if half == 1 else None))

                def emit_exp(j):
                    b = j % 2
                    p.wait("scalar", s_t)
                    p.do("scalar", "activation", Pt[b][:], psW[b][:], AF.Exp,
                         inc=s_s)
                    exp_cnt.append(s_s.n)

                def emit_pv(j):
                    b = j % 2
                    p.wait("tensor", s_s, exp_cnt[j])
                    for half in range(2):
                        p.do("tensor", "matmul",
                             (psA if half == 0 else psB)[0:65, :],
                             V1[:, j * 65:j * 65 + 65],
                             Pt[b][:, half * 512:(half + 1) * 512],
                             start=(j == 0), stop=(j == NJ - 1),
                             inc=(s_t if half == 1 else None))

                emit_scores(0)
                emit_exp(0)
                for j in range(1, NJ):
                    emit_scores(j)
                    emit_exp(j)
                    emit_pv(j - 1)
                emit_pv(NJ - 1)
                for half in range(2):
                    ps = psA if half == 0 else psB
                    qsl = slice(q0 + half * 512, q0 + (half + 1) * 512)
                    p.wait("scalar", s_t)
                    p.wait("scalar", s_v)
                    p.do("scalar", "copy", otmp[:], ps[0:64, :], inc=s_s)
                    p.wait("vector", s_t)
                    p.do("vector", "reciprocal", rd[:], ps[64:65, :], inc=s_v)
                    p.wait("tensor", s_v)
                    p.wait("tensor", s_s)
                    p.wait("vector", s_s)
                    if l1a:
                        p.do("tensor", "matmul", psC[0:64, :], c["ones_row_f"][0:1, 0:64],
                             rd[:], start=True, stop=True, inc=s_t)
                        p.wait("vector", s_t)
                        p.do("vector", "tensor_mul", outTp[hp][base:base + 64, qsl],
                             otmp[:], psC[0:64, :], inc=s_v)
                    else:
                        p.do("tensor", "matmul", psC[0:64, :],
                             c["sel8"][:, h * 64:(h + 1) * 64], gatedT[0:8, qsl],
                             start=True, stop=True, inc=s_t)
                        p.wait("vector", s_t)
                        p.do("vector", "tensor_mul", otmp2[:], otmp[:], psC[0:64, :],
                             inc=s_v)
                        p.wait("vector", s_v)
                        p.wait("tensor", s_v)
                        p.do("tensor", "matmul", psC[0:64, :], c["ones_row_f"][0:1, 0:64],
                             rd[:], start=True, stop=True, inc=s_t)
                        p.wait("vector", s_t)
                        p.do("vector", "tensor_mul", outTp[hp][base:base + 64, qsl],
                             otmp2[:], psC[0:64, :], inc=s_v)
                    p.wait("tensor", s_v)

    if DEBUG_FLASH:
        dstg = p.sbuf([128, TLOC], F32, "dstg")
        for j in range(NP):
            p.wait("vector", s_o)
            p.do("vector", "tensor_copy", dstg[:], outTp[j][:], inc=s_v)
            p.wait("gpsimd", s_v)
            p.dma("gpsimd", dbg_out[j * 128:(j + 1) * 128, :], dstg[:], s_o)
            p.wait("vector", s_o)
            p.do("vector", "tensor_copy", dstg[:], QTp[j][:], inc=s_v)
            p.wait("gpsimd", s_v)
            p.dma("gpsimd", dbg_q[j * 128:(j + 1) * 128, :], dstg[:], s_o)
        dbg_wo = p.dram("dbg_wo", [WoB_n * 128, D], F32, "ExternalOutput")
        for j in range(WoB_n):
            p.wait("vector", s_o)
            p.do("vector", "tensor_copy", dstg[:, 0:1024], WoB[j][:], inc=s_v)
            p.wait("gpsimd", s_v)
            p.dma("gpsimd", dbg_wo[j * 128:(j + 1) * 128, :], dstg[:, 0:1024], s_o)
        p.wait("vector", s_o)

    if FLASH_STOP_AFTER == "flash":
        p.wait("gpsimd", s_v)
        p.dma("gpsimd", xout[0:128, :], xt_f[0][:], s_o)
        p.wait("gpsimd", s_o)
        return p.finalize()

    for t in range(TLOC // 128):
        buf = t % 2
        p.wait("sync", s_v)
        p.dma("sync", xt_f[buf][:], xloc[t * 128:(t + 1) * 128, :], s_in)
        for f in range(2):
            ps = psW[f][:, 0:512]
            p.wait("tensor", s_v)
            p.wait("tensor", s_in)
            p.wait("tensor", s_vl)
            for j in range(NP):
                p.do("tensor", "matmul", ps, outTp[j][:, t * 128:(t + 1) * 128],
                     WoB[j][:, f * 512:(f + 1) * 512],
                     start=(j == 0), stop=(j == NP - 1 and l1a),
                     inc=(s_t if (j == NP - 1 and l1a) else None))
            if not l1a:
                p.do("tensor", "matmul", ps, gatedT_bf[0:8, t * 128:(t + 1) * 128],
                     bo8B[0:8, f * 512:(f + 1) * 512], start=False, stop=True,
                     inc=s_t)
            p.wait("vector", s_t)
            p.wait("vector", s_in)
            p.wait("vector", s_o)
            p.do("vector", "tensor_add", lt["xc"][:, 0:512], ps,
                 xt_f[buf][:, f * 512:(f + 1) * 512], inc=s_v)
            p.wait("gpsimd", s_v)
            p.dma("gpsimd", xout[t * 128:(t + 1) * 128, f * 512:(f + 1) * 512],
                  lt["xc"][:, 0:512], s_o)
    p.wait("gpsimd", s_o)
    if fused:
        out_sem = (s_o, s_o.n)
        p.free_since(mark)
        return out_sem
    return p.finalize()



def build_l1f(p=None, io=None):
    fused = p is not None
    if not fused:
        p = Prog("l1f")
    io = io or {}
    mark = p.marker()
    D_ = lambda n, sh, dt: io[n] if n in io else p.dram(n, sh, dt, "ExternalInput")
    xloc = D_("xloc", [TLOC, D], F32)
    gate_w = D_("gate_w", [D, 8], F32)
    gbias = D_("gbias", [1, 8], F32)
    up_w = D_("up_w", [8, D, 4096], F32)
    up_b = D_("up_b", [8, 4096], F32)
    down_w = D_("down_w", [8, 4096, D], F32)
    down_b = D_("down_b", [8, D], F32)
    sw_up = D_("sw_up", [D, 4096], F32)
    sb_up = D_("sb_up", [32, 128], F32)
    sw_down = D_("sw_down", [4096, D], F32)
    sb_down = D_("sb_down", [8, 128], F32)
    sel8w = D_("sel8w", [8, 1024], BF16)
    xout = io["xout"] if "xout" in io else p.dram("xout", [TLOC, D], F32,
                                                  "ExternalOutput")
    if "gate" in io:
        gs, gv = io["gate"]
        for e in ENGINES:
            p.wait(e, gs, gv)
    if DEBUG_L1F:
        dbg_w = p.dram("dbg_w01T", [8, TLOC], F32, "ExternalOutput")
        dbg_u = p.dram("dbg_uscr", [32, 128, 512], BF16, "ExternalOutput")
        dbg_cb = p.dram("dbg_comb", [D, TLOC], F32, "ExternalOutput")
    comb = p.dram("comb", [D, TLOC], F32, "Internal")
    uscr = p.dram("uscr", [32, 128, 512], BF16, "Internal")

    c = mk_consts(p)
    c["ones_row_f"] = p.sbuf([1, 128], F32, "ones_row_f")
    p.do("gpsimd", "memset", c["ones_row_f"][:], 1.0, inc=c["sem"])

    s_in = p.sem("s_in")
    s_t = p.sem("s_t")
    s_v = p.sem("s_v")
    s_s = p.sem("s_s")
    s_o = p.sem("s_o")

    QT = 512
    NQ = TLOC // QT
    UpB = [p.sbuf([128, 4096], BF16) for _ in range(8)]
    xbf = UpB[0][:, 0:1024]
    DnB = [p.sbuf([128, D], BF16) for _ in range(32)]
    uT2 = [p.sbuf([128, QT], BF16) for _ in range(2)]
    ub2 = [p.sbuf([128, QT], BF16) for _ in range(2)]
    xlnT = [p.sbuf([128, TLOC], BF16) for _ in range(8)]
    GwBt = p.sbuf([128, 64], BF16, "GwBt")
    GwB = [GwBt[:, cc * 8:(cc + 1) * 8] for cc in range(8)]
    w01T = p.sbuf([8, TLOC], BF16, "w01T")
    sel8B = p.sbuf([8, 1024], BF16, "sel8B")
    wb_sb = p.sbuf([128, QT], F32, "wb_sb")
    ev_one = p.sbuf([128, QT], F32, "ev_one")
    ev_sb = [ev_one[:], ev_one[:]]
    up_bT = p.sbuf([128, 32 * 8], F32, "up_bT")
    db_sb = p.sbuf([128, 8 * 8], F32, "db_sb")
    sbu_c = p.sbuf([128, 32], F32, "sbu_c")
    sbd_c = p.sbuf([128, 8], F32, "sbd_c")
    xt_one = p.sbuf([128, 1024], F32, "xt_one")
    xt_f = [xt_one, xt_one]
    fac = p.sbuf([128, 4], F32, "fac")
    stg = [p.sbuf([128, 512], F32) for _ in range(2)]
    lt = mk_ln_tmp(p, c)
    g_s = p.sbuf([128, 8], F32)
    g_m = p.sbuf([128, 1], F32)
    g_a = p.sbuf([128, 8], F32)
    g_k = p.sbuf([128, 8], F32)
    g_bf = p.sbuf([128, 8], BF16)
    gb_bc = p.sbuf([128, 8], F32, "gb_bc")
    gbrow = p.sbuf([1, 8], F32, "gbrow")
    hbuf = [xt_f[0][:, 0:QT], xt_f[0][:, QT:2 * QT]]
    macc = None
    xsl = [lt["xc"][:, 0:128], lt["xc"][:, 128:256]]
    osl = [lt["xc"][:, 256:384], lt["xc"][:, 384:512]]
    macc = lt["sq"][:, 0:QT]
    psT = p.psum([128, 128], BF16, "psT")
    psU = [p.psum([128, QT], F32) for _ in range(2)]
    psD = [p.psum([128, QT], F32) for _ in range(5)]

    p.dma("sync", sel8B[:], sel8w, s_in)
    p.dma("sync", gbrow[:], gbias, s_in)
    for cc in range(8):
        p.wait("sync", s_v)
        p.dma("sync", stg[cc % 2][:, 0:8], gate_w[cc * 128:(cc + 1) * 128, :], s_in)
        p.wait("vector", s_in)
        p.do("vector", "tensor_copy", GwB[cc], stg[cc % 2][:, 0:8], inc=s_v)
    p.wait("tensor", s_in, 32)
    p.wait("tensor", c["sem"])
    p.do("tensor", "matmul", psU[0][0:128, 0:8], c["ones_row_f"][:], gbrow[:],
         start=True, stop=True, inc=s_t)
    p.wait("vector", s_t)
    p.do("vector", "tensor_copy", gb_bc[:], psU[0][0:128, 0:8], inc=s_v)
    p.wait("tensor", s_v)

    def load_T(src, rows, dst, dstc0, nsl=1):
        for c0 in range(0, nsl, 4):
            cn = min(4, nsl - c0)
            p.wait("sync", s_v)
            p.wait("sync", s_s)
            p.dma("sync", stg[0][0:rows, 0:cn * 128],
                  src[:, c0 * 128:(c0 + cn) * 128], s_in)
            for jj in range(cn):
                j = c0 + jj
                p.wait("tensor", s_in)
                p.wait("tensor", s_s)
                p.do("tensor", "transpose", psU[1][0:128, 0:rows],
                     stg[0][0:rows, jj * 128:(jj + 1) * 128],
                     c["identf"][0:rows, 0:rows], inc=s_t)
                p.wait("scalar", s_t)
                p.do("scalar", "copy",
                     dst[:, dstc0 + j * rows:dstc0 + (j + 1) * rows],
                     psU[1][0:128, 0:rows], inc=s_s)

    for e in range(8):
        load_T(up_b[e].rearrange("(i x) -> i x", x=128), 32, up_bT, e * 32, nsl=1)
    for e in range(8):
        load_T(down_b[e].rearrange("(i x) -> i x", x=128), 8, db_sb, e * 8, nsl=1)
    load_T(sb_up, 32, sbu_c, 0, nsl=1)
    load_T(sb_down, 8, sbd_c, 0, nsl=1)
    p.wait("tensor", s_s)

    if L1F_STOP == "prep":
        p.wait("gpsimd", s_s, s_s.n)
        p.dma("gpsimd", xout[0:128, :], stg[0][0:128, 0:512], s_o)
        p.wait("gpsimd", s_o)
        return p.finalize()

    for t in range(TLOC // 128):
        t0 = t * 128
        buf = t % 2
        p.wait("sync", s_v)
        p.dma("sync", xt_f[buf][:], xloc[t0:t0 + 128, :], s_in)
        p.wait("vector", s_in)
        ln_token_major(p, xt_f[buf][:], lt["sq"][:], lt, s_v, s_s)
        p.wait("vector", s_v)
        p.wait("vector", s_t)
        p.do("vector", "tensor_copy", xbf, lt["sq"][:], inc=s_v)
        p.wait("tensor", s_v)
        for cc in range(8):
            p.do("tensor", "transpose", psT[:], xbf[:, cc * 128:(cc + 1) * 128]
                 if False else UpB[0][:, cc * 128:(cc + 1) * 128],
                 c["ident"][:], inc=s_t)
            p.wait("scalar", s_t)
            p.do("scalar", "copy", xlnT[cc][:, t0:t0 + 128], psT[:], inc=s_s)
            p.wait("tensor", s_s)
        for cc in range(8):
            p.do("tensor", "matmul", psU[0][0:128, 0:8], xlnT[cc][:, t0:t0 + 128],
                 GwB[cc], start=(cc == 0), stop=(cc == 7),
                 inc=(s_t if cc == 7 else None))
        v = "vector"
        p.wait(v, s_t)
        p.dodr(v, "tensor_add", g_a[:], psU[0][0:128, 0:8], gb_bc[:])
        p.wait("scalar", s_v)
        p.wait("scalar", p._dr["vector"])
        p.do("scalar", "activation", g_s[:], g_a[:], AF.Sigmoid, inc=s_s)
        p.wait(v, s_s)
        p.dodr(v, "reduce_max", g_m[:], g_s[:], axis=AX.X)
        p.dodr(v, "tensor_scalar", g_a[:], g_s[:], g_m[:], None, ALU.is_equal)
        p.dodr(v, "scalar_tensor_tensor", g_a[:], g_a[:], -1e30, g_s[:],
               ALU.mult, ALU.add)
        p.dodr(v, "reduce_max", g_m[:], g_a[:], axis=AX.X)
        p.dodr(v, "tensor_scalar", g_k[:], g_s[:], g_m[:], None, ALU.is_ge)
        p.dodr(v, "tensor_mul", g_k[:], g_s[:], g_k[:])
        p.dodr(v, "reduce_sum", g_m[:], g_k[:], axis=AX.X)
        p.dodr(v, "tensor_scalar_add", g_m[:], g_m[:], 1e-6)
        p.dodr(v, "reciprocal", g_m[:], g_m[:])
        p.dodr(v, "tensor_scalar_mul", g_m[:], g_m[:], 0.1)
        p.dodr(v, "tensor_scalar_mul", g_k[:], g_k[:], g_m[:])
        p.do(v, "tensor_copy", g_bf[:], g_k[:], inc=s_v)
        p.wait("tensor", s_v)
        p.do("tensor", "transpose", psT[0:8, :], g_bf[:], c["ident"][:], inc=s_t)
        p.wait("scalar", s_t)
        p.do("scalar", "copy", w01T[0:8, t0:t0 + 128], psT[0:8, :], inc=s_s)
        p.wait("tensor", s_s)
        p.wait("vector", s_s)

    if L1F_STOP == "p1":
        p.wait("gpsimd", s_s, s_s.n)
        p.dma("gpsimd", xout[0:128, :], xt_f[0][:], s_o)
        p.wait("gpsimd", s_o)
        return p.finalize()

    region_cnt = {}
    ew_cnt = []
    for e in range(L1F_NE):
        shared = e == 8
        up_src = sw_up if shared else up_w[e]
        dn_src = sw_down if shared else down_w[e]
        p.wait("vector", s_t)
        srcs = ([(UpB[cc][:, k4 * 512:(k4 + 1) * 512],
                  up_src[cc * 128:(cc + 1) * 128, k4 * 512:(k4 + 1) * 512])
                 for cc in range(8) for k4 in range(8)] +
                [(DnB[i][:, k2 * 512:(k2 + 1) * 512],
                  dn_src[i * 128:(i + 1) * 128, k2 * 512:(k2 + 1) * 512])
                 for i in range(32) for k2 in range(2)])
        for dst, src_ap in srcs:
            idx = len(ew_cnt)
            b = idx % 2
            if idx >= 2:
                p.wait("sync", s_v, ew_cnt[idx - 2])
            else:
                p.wait("sync", s_v)
            p.dma("sync", stg[b][:], src_ap, s_in)
            p.wait("vector", s_in)
            p.do("vector", "tensor_copy", dst, stg[b][:], inc=s_v)
            ew_cnt.append(s_v.n)

        if L1F_STOP == "wonly":
            continue
        for g in range(NQ):
            q0 = g * QT
            if L1F_STOP == "noup":
                break
            if L1F_STOP == "nowb":
                pass
            elif shared:
                p.wait("vector", s_o)
                p.do("vector", "memset", wb_sb[:], 0.1, inc=s_v)
            else:
                p.wait("tensor", s_v)
                p.wait("tensor", s_s)
                p.do("tensor", "matmul", psU[0][:, 0:QT],
                     sel8B[:, e * 128:(e + 1) * 128], w01T[0:8, q0:q0 + QT],
                     start=True, stop=True, inc=s_t)
                p.wait("vector", s_t)
                p.do("vector", "tensor_copy", wb_sb[:], psU[0][:, 0:QT], inc=s_v)
            if L1F_STOP == "nogelu":
                continue
            if not hasattr(p, "_l1f_gelu"):
                p._l1f_gelu = []
                p._l1f_udma = []
            gelu_cnt = p._l1f_gelu
            udma_cnt = p._l1f_udma
            gb0 = len(gelu_cnt)
            ub0 = len(udma_cnt)
            p.wait("tensor", s_v)

            def emit_up(i):
                b = i % 2
                if gb0 + i >= 2:
                    p.wait("tensor", s_s, gelu_cnt[gb0 + i - 2])
                for cc in range(8):
                    p.do("tensor", "matmul", psU[b][:, 0:QT],
                         UpB[cc][:, i * 128:(i + 1) * 128], xlnT[cc][:, q0:q0 + QT],
                         start=(cc == 0), stop=(cc == 7),
                         inc=(s_t if cc == 7 else None))

            def emit_gelu(i):
                b = i % 2
                p.wait("scalar", s_t)
                if ub0 + i >= 2:
                    p.wait("scalar", s_o, udma_cnt[ub0 + i - 2])
                bias_col = (sbu_c[:, i:i + 1] if shared
                            else up_bT[:, e * 32 + i:e * 32 + i + 1])
                p.do("scalar", "activation", uT2[b][:], psU[b][:, 0:QT],
                     AF.Gelu, bias=bias_col, inc=s_s)
                gelu_cnt.append(s_s.n)

            def emit_sweepA(i):
                b = i % 2
                p.wait("tensor", s_s, gelu_cnt[gb0 + i])
                for hi, h in enumerate(range(0, 5)):
                    p.do("tensor", "matmul", psD[hi][:, 0:QT],
                         DnB[i][:, h * 128:(h + 1) * 128], uT2[b][:],
                         start=(i == 0), stop=(i == 31),
                         inc=(s_t if (i == 31 and hi == 4) else None))
                p.wait("gpsimd", s_s, gelu_cnt[gb0 + i])
                p.dma("gpsimd", uscr[i], uT2[b][:], s_o)
                udma_cnt.append(s_o.n)

            emit_up(0)
            emit_gelu(0)
            for i in range(1, 32):
                emit_up(i)
                emit_gelu(i)
                emit_sweepA(i - 1)
            emit_sweepA(31)

            def emit_evac(hi, h):
                eb = ev_sb[0]
                db_col = (sbd_c[:, h:h + 1] if shared
                          else db_sb[:, e * 8 + h:e * 8 + h + 1])
                p.wait("vector", s_o)
                p.do("vector", "scalar_tensor_tensor", eb[:], psD[hi][:, 0:QT],
                     db_col, wb_sb[:], ALU.add, ALU.mult, inc=s_v)
                p.wait("gpsimd", s_v)
                key = (g, h)
                if key in region_cnt:
                    p.wait("gpsimd", s_o, region_cnt[key])
                p.dma("gpsimd", comb[h * 128:(h + 1) * 128, q0:q0 + QT], eb[:],
                      s_o, accum_op=(ALU.bypass if e == 0 else ALU.add))
                region_cnt[key] = s_o.n

            if L1F_STOP not in ("noswa", "noevac"):
                p.wait("vector", s_t)
                for hi, h in enumerate(range(0, 5)):
                    emit_evac(hi, h)
            if L1F_STOP in ("noswb", "noswa", "noevac"):
                continue
            p.wait("tensor", s_v)
            p.wait("sync", s_t)
            mmB_cnt = []
            for i in range(32):
                b = i % 2
                p.wait("sync", s_o)
                if i >= 2:
                    p.wait("sync", s_t, mmB_cnt[i - 2])
                p.dma("sync", ub2[b][:], uscr[i], s_in)
                p.wait("tensor", s_in)
                for hi, h in enumerate(range(5, 8)):
                    p.do("tensor", "matmul", psD[hi][:, 0:QT],
                         DnB[i][:, h * 128:(h + 1) * 128], ub2[b][:],
                         start=(i == 0), stop=(i == 31),
                         inc=(s_t if hi == 2 else None))
                mmB_cnt.append(s_t.n)
            p.wait("vector", s_t)
            for hi, h in enumerate(range(5, 8)):
                emit_evac(hi, h)

    if L1F_STOP == "p2":
        p.wait("gpsimd", s_o, s_o.n)
        p.dma("gpsimd", xout[0:128, :], xt_f[0][:], s_o)
        p.wait("gpsimd", s_o)
        return p.finalize()

    if DEBUG_L1F:
        for gq in range(4):
            p.wait("vector", s_o)
            p.do("vector", "tensor_copy", ev_sb[0][0:8, 0:512],
                 w01T[0:8, gq * 512:(gq + 1) * 512], inc=s_v)
            p.wait("gpsimd", s_v)
            p.dma("gpsimd", dbg_w[0:8, gq * 512:(gq + 1) * 512],
                  ev_sb[0][0:8, 0:512], s_o)
        p.wait("gpsimd", s_o)
        p.dma("gpsimd", dbg_u, uscr, s_o)
        p.dma("gpsimd", dbg_cb, comb, s_o)
        p.wait("gpsimd", s_o)

    all_acc = s_o.n
    p.wait("sync", s_o, all_acc)
    for g in range(NQ):
        q0 = g * QT
        for h in range(8):
            hb = hbuf[h % 2]
            p.wait("sync", s_v)
            p.wait("sync", s_s)
            p.dma("sync", hb, comb[h * 128:(h + 1) * 128, q0:q0 + QT], s_in)
            p.wait("scalar", s_in)
            p.wait("scalar", s_v)
            p.wait("scalar", p._dr.get("vector", s_v))
            p.do("scalar", "activation", ev_sb[0][:], hb, AF.Abs, inc=s_s)
            p.wait("vector", s_s)
            if h == 0:
                p.do("vector", "tensor_copy", macc, ev_sb[0][:], inc=s_v)
            else:
                p.dodr("vector", "tensor_max", macc, macc, ev_sb[0][:])
        p.wait("tensor", p._dr["vector"])
        for j in range(4):
            p.wait("tensor", s_v)
            p.wait("tensor", s_s)
            p.do("tensor", "transpose", psU[1][0:128, 0:128],
                 lt["sq"][:, j * 128:(j + 1) * 128], c["identf"][:], inc=s_t)
            p.wait("vector", s_t)
            p.do("vector", "reduce_max", fac[:, j:j + 1], psU[1][0:128, 0:128],
                 axis=AX.X, inc=s_v)
            p.wait("tensor", s_v)
        p.wait("vector", s_v)
        p.dodr("vector", "tensor_scalar_add", fac[:], fac[:], 1e-6)
        p.dodr("vector", "reciprocal", fac[:], fac[:])
        for h in range(8):
            hb = hbuf[h % 2]
            p.wait("sync", s_v)
            p.dma("sync", hb, comb[h * 128:(h + 1) * 128, q0:q0 + QT], s_in)
            for j in range(4):
                bx = j % 2
                p.wait("sync", s_v)
                p.dma("sync", xsl[bx],
                      xloc[q0 + j * 128:q0 + (j + 1) * 128,
                           h * 128:(h + 1) * 128], s_in)
                p.wait("tensor", s_in)
                p.wait("tensor", s_v)
                p.do("tensor", "transpose", psU[1][0:128, 0:128],
                     hb.tensor[0:128, hb.offset + j * 128:hb.offset + (j + 1) * 128]
                     if False else hb[:, j * 128:(j + 1) * 128],
                     c["identf"][:], inc=s_t)
                p.wait("vector", s_t)
                p.wait("vector", s_in)
                p.wait("vector", s_o)
                p.do("vector", "scalar_tensor_tensor", osl[bx],
                     psU[1][0:128, 0:128], fac[:, j:j + 1], xsl[bx],
                     ALU.mult, ALU.add, inc=s_v)
                p.wait("gpsimd", s_v)
                p.dma("gpsimd", xout[q0 + j * 128:q0 + (j + 1) * 128,
                                     h * 128:(h + 1) * 128], osl[bx], s_o)
    p.wait("gpsimd", s_o)
    if fused:
        out_sem = (s_o, s_o.n)
        p.free_since(mark)
        return out_sem
    return p.finalize()



def build_l0a(p=None, io=None):
    fused = p is not None
    if not fused:
        p = Prog("l0a")
    io = io or {}
    mark = p.marker()
    D_ = lambda n, sh, dt: io[n] if n in io else p.dram(n, sh, dt, "ExternalInput")
    xb = D_("xb", [S, D], F32)
    wq = D_("wq", [D, 256], F32)
    wk = D_("wk", [D, 256], F32)
    wv = D_("wv", [D, 256], F32)
    bq = D_("bq", [1, 256], F32)
    bk = D_("bk", [1, 256], F32)
    bv = D_("bv", [1, 256], F32)
    outT = io["outT"] if "outT" in io else p.dram("outT", [D, TLOC], BF16,
                                                  "ExternalOutput")
    if "gate" in io:
        gs, gv = io["gate"]
        for e in ENGINES:
            p.wait(e, gs, gv)
    if DEBUG_L0A:
        dbg_c = p.dram("dbg_cols", [S // 128, 128, 16], F32, "ExternalOutput")
        dbg_sra = p.dram("dbg_sra", [S // 128, 1, 512], F32, "ExternalOutput")

    c = mk_consts(p)
    g = "gpsimd"
    c["ones_row_f"] = p.sbuf([1, 128], F32, "ones_row_f")
    c["ones_col_f"] = p.sbuf([128, 1], F32, "ones_col_f")
    c["triu_f"] = p.sbuf([128, 128], F32, "triu_f")
    c["triu_b"] = p.sbuf([128, 128], BF16, "triu_b")
    p.do(g, "memset", c["ones_row_f"][:], 1.0)
    p.do(g, "memset", c["ones_col_f"][:], 1.0)
    p.do(g, "memset", c["triu_f"][:], 1.0)
    p.do(g, "affine_select", c["triu_f"][:], c["triu_f"][:], [[1, 128]], ALU.is_ge,
         0.0, base=0, channel_multiplier=-1)
    p.do(g, "memset", c["triu_b"][:], 1.0)
    p.do(g, "affine_select", c["triu_b"][:], c["triu_b"][:], [[1, 128]], ALU.is_ge,
         0.0, base=0, channel_multiplier=-1, inc=c["sem"])

    s_in = p.sem("s_in")
    s_t = p.sem("s_t")
    s_v = p.sem("s_v")
    s_s = p.sem("s_s")
    s_g = p.sem("s_g")
    s_o = p.sem("s_o")

    NH = 4
    NC = S // 128
    EPS = 1e-6

    WB = [p.sbuf([128, 768], BF16) for _ in range(8)]
    bbc = p.sbuf([128, 768], F32, "bbc")
    bqkv = p.sbuf([1, 768], F32, "bqkv")
    outS = [p.sbuf([128, S], BF16) for _ in range(2)]
    xlnT_c = [p.sbuf([128, 128], BF16) for _ in range(8)]
    xt_f = [p.sbuf([128, 1024], F32) for _ in range(2)]
    xbf = p.sbuf([128, 1024], BF16, "xbf")
    lt = mk_ln_tmp(p, c)
    qkv_f = [p.sbuf([128, 192], F32) for _ in range(NH)]
    qk_f = [p.sbuf([128, 128], F32) for _ in range(NH)]
    qk_b = [p.sbuf([128, 128], BF16) for _ in range(NH)]
    qT_b = [p.sbuf([64, 128], BF16) for _ in range(NH)]
    kT_b = [p.sbuf([64, 128], BF16) for _ in range(NH)]
    ksqs_f = [p.sbuf([128, 128], F32) for _ in range(NH)]
    vs_b = [p.sbuf([128, 64], BF16) for _ in range(NH)]
    A_b = [p.sbuf([128, 128], BF16) for _ in range(NH)]
    kvprev_b = [p.sbuf([64, 64], BF16) for _ in range(NH)]
    cum1 = [p.sbuf([128, 128], F32) for _ in range(NH)]
    cum2 = [p.sbuf([128, 128], F32) for _ in range(NH)]
    o_f = [p.sbuf([64, 128], F32) for _ in range(NH)]
    facT = p.sbuf([1, 128], F32, "facT")
    facE = p.sbuf([64, 128], F32, "facE")
    colsc = p.sbuf([128, 16 * NH], F32, "colsc")
    sra_sb = p.sbuf([1, 512], F32, "sra_sb")
    srb_sb = p.sbuf([1, 512], F32, "srb_sb")
    ce_sb = p.sbuf([1, 4], F32, "ce_sb")
    cex_sb = p.sbuf([128, NH], F32, "cex_sb")
    nrm = p.sbuf([128, 1], F32, "nrm")
    rnrm = p.sbuf([128, 1], F32, "rnrm")
    stg = [p.sbuf([128, 768], F32) for _ in range(2)]
    sqs = [p.sbuf([128, 64], F32) for _ in range(NH)]

    pC = p.psum([128, 512], F32, "pC")
    pS = p.psum([128, 512], F32, "pS")
    pO = p.psum([128, 512], F32, "pO")
    pKV = p.psum([128, 512], F32, "pKV")
    pA = p.psum([128, 512], F32, "pA")
    pB = p.psum([128, 512], F32, "pB")
    pE = p.psum([128, 512], F32, "pE")
    psT = p.psum([128, 128], BF16, "psT")

    for cc in range(8):
        b = cc % 2
        sl = slice(cc * 128, (cc + 1) * 128)
        p.wait("sync", s_v)
        p.dma("sync", stg[b][:, 0:256], wq[sl, :], s_in)
        p.dma("sync", stg[b][:, 256:512], wk[sl, :], s_in)
        p.dma("sync", stg[b][:, 512:768], wv[sl, :], s_in)
        p.wait("vector", s_in)
        for h in range(NH):
            p.do("vector", "tensor_copy", WB[cc][:, h * 192:h * 192 + 64],
                 stg[b][:, h * 64:h * 64 + 64])
            p.do("vector", "tensor_copy", WB[cc][:, h * 192 + 64:h * 192 + 128],
                 stg[b][:, 256 + h * 64:256 + h * 64 + 64])
            p.do("vector", "tensor_copy", WB[cc][:, h * 192 + 128:h * 192 + 192],
                 stg[b][:, 512 + h * 64:512 + h * 64 + 64],
                 inc=(s_v if h == NH - 1 else None))
    for h in range(NH):
        p.dma("sync", bqkv[0:1, h * 192:h * 192 + 64],
              bq[0:1, h * 64:h * 64 + 64], s_in)
        p.dma("sync", bqkv[0:1, h * 192 + 64:h * 192 + 128],
              bk[0:1, h * 64:h * 64 + 64], s_in)
        p.dma("sync", bqkv[0:1, h * 192 + 128:h * 192 + 192],
              bv[0:1, h * 64:h * 64 + 64], s_in)
    p.wait("tensor", s_in)
    p.wait("tensor", c["sem"])
    for j in range(0, 768, 512):
        w = min(512, 768 - j)
        p.do("tensor", "matmul", pE[:, 0:w], c["ones_row_f"][:], bqkv[0:1, j:j + w],
             start=True, stop=True, inc=s_t)
        p.wait("vector", s_t)
        p.do("vector", "tensor_copy", bbc[:, j:j + w], pE[:, 0:w], inc=s_v)
        p.wait("tensor", s_v)

    def col(h, i):
        return colsc[:, h * 16 + i:h * 16 + i + 1]

    p.do("vector", "memset", sra_sb[:], 0.0)
    p.do("vector", "memset", srb_sb[:], 0.0)
    p.do("vector", "memset", ce_sb[:], 0.0, inc=s_v)

    first_kvbank_mm = [True]

    for ch in range(NC):
        t0 = ch * 128
        p.wait("gpsimd", s_v)
        p.do(g, "iota", nrm[:], [[1, 1]], base=t0 + 1, channel_multiplier=1,
             allow_small_or_imprecise_dtypes=True, inc=s_g)
        p.wait("vector", s_g)
        p.dodr("vector", "reciprocal", rnrm[:], nrm[:])
        buf = ch % 2
        p.wait("sync", s_v)
        p.dma("sync", xt_f[buf][:], xb[t0:t0 + 128, :], s_in)
        p.wait("vector", s_in)
        ln_token_major(p, xt_f[buf][:], lt["sq"][:], lt, s_v, s_s)
        p.wait("vector", s_v)
        p.wait("vector", s_t)
        p.do("vector", "tensor_copy", xbf[:], lt["sq"][:], inc=s_v)
        p.wait("tensor", s_v)
        for cc in range(8):
            p.do("tensor", "transpose", psT[:], xbf[:, cc * 128:(cc + 1) * 128],
                 c["ident"][:], inc=s_t)
            p.wait("scalar", s_t)
            p.do("scalar", "copy", xlnT_c[cc][:], psT[:], inc=s_s)
            p.wait("tensor", s_s)

        for h in range(NH):
            p.wait("tensor", s_s)
            p.wait("tensor", s_v)
            if "vector" in getattr(p, "_dr", {}):
                p.wait("tensor", p._dr["vector"])
            for cc in range(8):
                p.do("tensor", "matmul", pS[:, 128:320], xlnT_c[cc][:],
                     WB[cc][:, h * 192:(h + 1) * 192],
                     start=(cc == 0), stop=(cc == 7),
                     inc=(s_t if cc == 7 else None))
            p.wait("vector", s_t)
            p.dodr("vector", "tensor_add", qkv_f[h][:], pS[:, 128:320],
                   bbc[:, h * 192:(h + 1) * 192])
            p.wait("scalar", p._dr["vector"])
            p.do("scalar", "activation", qk_f[h][:], qkv_f[h][:, 0:128],
                 AF.Sigmoid, inc=s_s)
            p.wait("vector", s_s)
            p.do("vector", "tensor_copy", qk_b[h][:], qk_f[h][:], inc=s_v)
            p.wait("tensor", s_v)
            p.do("tensor", "transpose", psT[:], qk_b[h][:], c["ident"][:], inc=s_t)
            p.wait("scalar", s_t)
            p.do("scalar", "copy", qT_b[h][:], psT[0:64, :])
            p.do("scalar", "copy", kT_b[h][:], psT[64:128, :], inc=s_s)
            p.wait("tensor", s_s)

        for h in range(NH):
            p.wait("tensor", s_s)
            p.wait("tensor", s_v)
            p.do("tensor", "matmul", pC[:, 0:128], c["triu_f"][:], qk_f[h][:],
                 start=True, stop=False)
            p.do("tensor", "matmul", pC[:, 0:128], c["ones_row_f"][:],
                 sra_sb[0:1, h * 128:(h + 1) * 128], start=False, stop=True,
                 inc=s_t)
            p.do("tensor", "matmul", pA[0:1, h * 128:(h + 1) * 128],
                 c["ones_col_f"][:], qk_f[h][:],
                 start=(ch == 0 and h == 0), stop=(ch == NC - 1))
            p.wait("scalar", s_t)
            p.do("scalar", "activation", cum1[h][:], pC[:, 0:128], AF.Identity,
                 bias=c["eps6"][:], inc=s_s)
            p.wait("vector", s_s)
            p.do("vector", "scalar_tensor_tensor", sqs[h][:], qk_f[h][:, 0:64],
                 EPS, cum1[h][:, 64:128], ALU.add, ALU.mult, accum_out=col(h, 0),
                 inc=p._dr["vector"])
            p.wait("vector", p._dr["vector"])
            p.do("vector", "scalar_tensor_tensor", sqs[h][:], qk_f[h][:, 64:128],
                 EPS, cum1[h][:, 0:64], ALU.add, ALU.mult, accum_out=col(h, 1),
                 inc=p._dr["vector"])
            p.wait("vector", p._dr["vector"])
            p.dodr("vector", "reciprocal", col(h, 0), col(h, 0))
            p.dodr("vector", "reciprocal", col(h, 1), col(h, 1))
            p.dodr("vector", "tensor_mul", col(h, 0), col(h, 0), nrm[:])
            p.dodr("vector", "tensor_mul", col(h, 1), col(h, 1), nrm[:])
            p.dodr("vector", "tensor_scalar_mul", ksqs_f[h][:, 0:64],
                   qk_f[h][:, 64:128], col(h, 1))
            p.dodr("vector", "tensor_scalar_mul", ksqs_f[h][:, 64:128],
                   qk_f[h][:, 0:64], col(h, 0))
            p.wait("tensor", p._dr["vector"])
            p.wait("tensor", s_s)
            p.do("tensor", "matmul", pC[:, 128:256], c["triu_f"][:], ksqs_f[h][:],
                 start=True, stop=False)
            p.do("tensor", "matmul", pC[:, 128:256], c["ones_row_f"][:],
                 srb_sb[0:1, h * 128:(h + 1) * 128], start=False, stop=True,
                 inc=s_t)
            p.do("tensor", "matmul", pB[0:1, h * 128:(h + 1) * 128],
                 c["ones_col_f"][:], ksqs_f[h][:],
                 start=(ch == 0 and h == 0), stop=(ch == NC - 1))
            p.wait("scalar", s_t)
            p.do("scalar", "activation", cum2[h][:], pC[:, 128:256], AF.Identity,
                 bias=c["eps6"][:], inc=s_s)
            p.wait("vector", s_s)
            p.do("vector", "scalar_tensor_tensor", sqs[h][:], qk_f[h][:, 0:64],
                 EPS, cum2[h][:, 0:64], ALU.add, ALU.mult, accum_out=col(h, 2),
                 inc=p._dr["vector"])
            p.wait("vector", p._dr["vector"])
            p.do("vector", "scalar_tensor_tensor", sqs[h][:], qk_f[h][:, 64:128],
                 EPS, cum2[h][:, 64:128], ALU.add, ALU.mult, accum_out=col(h, 3),
                 inc=p._dr["vector"])
            p.wait("vector", p._dr["vector"])
            p.dodr("vector", "tensor_mul", col(h, 2), col(h, 2), rnrm[:])
            p.dodr("vector", "tensor_mul", col(h, 3), col(h, 3), rnrm[:])
            p.dodr("vector", "tensor_scalar", col(h, 3), col(h, 3), -1.0, 1.0,
                   ALU.max, ALU.min)
            p.wait("scalar", p._dr["vector"])
            p.do("scalar", "activation", col(h, 8), col(h, 2), AF.Sigmoid)
            p.do("scalar", "activation", col(h, 4), col(h, 3), AF.Exp, inc=s_s)
            p.wait("tensor", s_s)
            p.wait("tensor", p._dr["vector"])
            p.do("tensor", "matmul", pC[:, 256 + h:257 + h], c["triu_f"][:],
                 col(h, 4), start=True, stop=False)
            p.do("tensor", "matmul", pC[:, 256 + h:257 + h], c["ones_row_f"][:],
                 ce_sb[0:1, h:h + 1], start=False, stop=True, inc=s_t)
            p.do("tensor", "matmul", pKV[0:1, 256 + h:257 + h],
                 c["ones_col_f"][:], col(h, 4),
                 start=first_kvbank_mm[0], stop=(ch == NC - 1))
            first_kvbank_mm[0] = False
            p.wait("vector", s_t)
            p.do("vector", "tensor_copy", cex_sb[:, h:h + 1], pC[:, 256 + h:257 + h],
                 inc=s_v)
            p.wait("vector", s_v)
            p.dodr("vector", "reciprocal", col(h, 6), cex_sb[:, h:h + 1])
            p.dodr("vector", "tensor_mul", col(h, 6), col(h, 6), col(h, 4))
            p.dodr("vector", "tensor_mul", col(h, 6), col(h, 6), nrm[:])
            p.dodr("vector", "tensor_scalar_mul", vs_b[h][:], qkv_f[h][:, 128:192],
                   col(h, 6))
            p.wait("tensor", p._dr["vector"])
            p.wait("tensor", s_v)
            p.do("tensor", "matmul", pS[:, 0:128], kT_b[h][:], qT_b[h][:],
                 start=True, stop=True, inc=s_t)
            p.wait("vector", s_t)
            p.dodr("vector", "tensor_mul", A_b[h][:], pS[:, 0:128], c["triu_b"][:])
            if ch == 0:
                p.dodr("vector", "memset", kvprev_b[h][:], 0.0)
            else:
                p.dodr("vector", "tensor_copy", kvprev_b[h][:],
                       pKV[0:64, h * 64:(h + 1) * 64])
            p.wait("tensor", p._dr["vector"])
            p.wait("tensor", s_s)
            p.do("tensor", "matmul", pO[0:64, h * 128:(h + 1) * 128], vs_b[h][:],
                 A_b[h][:], start=True, stop=False)
            p.do("tensor", "matmul", pO[0:64, h * 128:(h + 1) * 128],
                 kvprev_b[h][:], qT_b[h][:], start=False, stop=True, inc=s_t)
            p.do("tensor", "matmul", pKV[0:64, h * 64:(h + 1) * 64],
                 qk_b[h][:, 64:128], vs_b[h][:], start=False,
                 stop=(ch == NC - 1))
            p.wait("vector", s_s)
            p.dodr("vector", "tensor_mul", col(h, 9), col(h, 8), col(h, 0))
            p.dodr("vector", "tensor_mul", col(h, 9), col(h, 9), rnrm[:])
            p.dodr("vector", "tensor_copy", col(h, 7), col(h, 9))
            p.wait("tensor", p._dr["vector"])
            p.do("tensor", "transpose", pE[0:1, 0:128], col(h, 7),
                 c["identf"][:], inc=s_t)
            p.wait("vector", s_t)
            p.do("vector", "tensor_copy", facT[:], pE[0:1, 0:128], inc=s_v)
            p.wait("tensor", s_v)
            p.do("tensor", "matmul", pE[0:64, 128:256], c["ones_row_f"][0:1, 0:64],
                 facT[:], start=True, stop=True, inc=s_t)
            p.wait("scalar", s_t)
            p.do("scalar", "copy", o_f[h][:], pO[0:64, h * 128:(h + 1) * 128],
                 inc=s_s)
            p.wait("vector", s_s)
            row0 = 64 * h
            dst = outS[0] if h < 2 else outS[1]
            rsl = slice((h % 2) * 64, (h % 2) * 64 + 64)
            p.do("vector", "tensor_mul",
                 dst[rsl, t0:t0 + 128], o_f[h][:], pE[0:64, 128:256], inc=s_v)
            p.wait("tensor", s_v)

        if DEBUG_L0A:
            p.wait("gpsimd", s_v)
            p.wait("gpsimd", s_s)
            p.dma("gpsimd", dbg_c[ch], colsc[:, 0:16], s_o)
            p.wait("vector", s_o)
        p.wait("vector", s_t)
        p.do("vector", "tensor_copy", sra_sb[:], pA[0:1, 0:512])
        p.do("vector", "tensor_copy", srb_sb[:], pB[0:1, 0:512])
        p.do("vector", "tensor_copy", ce_sb[:], pKV[0:1, 256:260], inc=s_v)
        if DEBUG_L0A:
            p.wait("gpsimd", s_v)
            p.dma("gpsimd", dbg_sra[ch], sra_sb[:], s_o)
            p.wait("vector", s_o)
        p.wait("tensor", s_v)

    p.wait("gpsimd", s_v)
    for j in range(4):
        cs = slice(2048 * j, 2048 * (j + 1))
        p.dma("gpsimd", outT[256 * j:256 * j + 128, :], outS[0][:, cs], s_o)
        p.dma("gpsimd", outT[256 * j + 128:256 * j + 256, :], outS[1][:, cs], s_o)
    p.wait("gpsimd", s_o)
    if fused:
        out_sem = (s_o, s_o.n)
        p.free_since(mark)
        return out_sem
    return p.finalize()



def _ca(a):
    return np.ascontiguousarray(a)


def kernel(**inp):
    if FUSED:
        return kernel_fused(**inp)
    TL = TLOC
    _ = inp.get("L0_attn_attn_mask")
    x = np.asarray(inp["L0_attn_inputs"], np.float32)

    nc = build_l0a()
    maps = []
    Wq, Wk, Wv = (np.asarray(inp[f"L0_attn_W{z}"], np.float32) for z in "qkv")
    bq, bk, bv = (np.asarray(inp[f"L0_attn_b{z}"], np.float32) for z in "qkv")
    for c in range(NCORE):
        b, gr = c // 4, c % 4
        sl = slice(256 * gr, 256 * (gr + 1))
        maps.append({"xb": _ca(x[b]), "wq": _ca(Wq[:, sl]), "wk": _ca(Wk[:, sl]),
                     "wv": _ca(Wv[:, sl]), "bq": _ca(bq[None, sl]),
                     "bk": _ca(bk[None, sl]), "bv": _ca(bv[None, sl])})
    res = run_bass_kernel_spmd(nc, maps, core_ids=list(range(NCORE)))
    xheadsT = [np.concatenate([res.results[4 * b + gr]["outT"] for gr in range(4)], 0)
               for b in range(2)]

    nc = build_l0b()
    maps = []
    for c in range(NCORE):
        b, j = c // 4, c % 4
        ts = slice(TL * j, TL * (j + 1))
        maps.append({
            "x": _ca(x[b][ts]), "hdT": _ca(xheadsT[b][:, ts]),
            "Wo": np.asarray(inp["L0_attn_Wo"], np.float32),
            "W1": np.asarray(inp["L0_attn_W1"], np.float32),
            "W2": np.asarray(inp["L0_attn_W2"], np.float32),
            "bo": _ca(np.asarray(inp["L0_attn_bo"], np.float32)[None, :]),
            "b1": _ca(np.asarray(inp["L0_attn_b1"], np.float32).reshape(32, 128)),
            "b2": _ca(np.asarray(inp["L0_attn_b2"], np.float32)[None, :]),
            "ln1g": _ca(np.asarray(inp["L0_attn_ln1_g"], np.float32)[None, :]),
            "ln1b": _ca(np.asarray(inp["L0_attn_ln1_b"], np.float32)[None, :]),
            "ln2g": _ca(np.asarray(inp["L0_attn_ln2_g"], np.float32)[None, :]),
            "ln2b": _ca(np.asarray(inp["L0_attn_ln2_b"], np.float32)[None, :]),
        })
    res = run_bass_kernel_spmd(nc, maps, core_ids=list(range(NCORE)))
    x1 = np.stack([np.concatenate([res.results[4 * b + j]["xout"] for j in range(4)], 0)
                   for b in range(2)])

    nc = build_flash("l0f")
    Wkv = np.asarray(inp["L0_ffn_Wkv"], np.float32)
    sel8c = np.kron(np.eye(8), np.ones((1, 64))).astype(np.float32).reshape(8, 512)
    maps = []
    for c in range(NCORE):
        b, j = c // 4, c % 4
        ts = slice(TL * j, TL * (j + 1))
        maps.append({
            "xb": _ca(x1[b]), "xloc": _ca(x1[b][ts]),
            "wq": np.asarray(inp["L0_ffn_Wq"], np.float32),
            "wk": _ca(Wkv[:, 0:512]), "wv": _ca(Wkv[:, 512:1024]),
            "wo": _ca(np.asarray(inp["L0_ffn_Wo"], np.float32).reshape(512, 1024)),
            "ws": np.asarray(inp["L0_ffn_Ws"], np.float32),
            "bo8": np.asarray(inp["L0_ffn_bo"], np.float32),
            "sel8c": sel8c,
        })
    res = run_bass_kernel_spmd(nc, maps, core_ids=list(range(NCORE)))
    x2 = np.stack([np.concatenate([res.results[4 * b + j]["xout"] for j in range(4)], 0)
                   for b in range(2)])

    nc = build_flash("l1a")
    wqkv = np.asarray(inp["L1_attn_w_qkv"], np.float32)
    maps = []
    for c in range(NCORE):
        b, j = c // 4, c % 4
        ts = slice(TL * j, TL * (j + 1))
        maps.append({
            "xb": _ca(x2[b]), "xloc": _ca(x2[b][ts]),
            "wq": _ca(wqkv[:, 0:1024]), "wk": _ca(wqkv[:, 1024:2048]),
            "wv": _ca(wqkv[:, 2048:3072]),
            "wo": np.asarray(inp["L1_attn_w_out"], np.float32),
        })
    res = run_bass_kernel_spmd(nc, maps, core_ids=list(range(NCORE)))
    x3 = np.stack([np.concatenate([res.results[4 * b + j]["xout"] for j in range(4)], 0)
                   for b in range(2)])

    nc = build_l1f()
    import ml_dtypes
    sel8w = np.kron(np.eye(8), np.ones((1, 128))).astype(
        ml_dtypes.bfloat16).reshape(8, 1024)
    base = {
        "gate_w": np.asarray(inp["L1_ffn_gate_w"], np.float32),
        "gbias": _ca(np.asarray(inp["L1_ffn_bias"], np.float32)[None, :]),
        "up_w": np.asarray(inp["L1_ffn_up_w"], np.float32),
        "up_b": np.asarray(inp["L1_ffn_up_b"], np.float32),
        "down_w": np.asarray(inp["L1_ffn_down_w"], np.float32),
        "down_b": np.asarray(inp["L1_ffn_down_b"], np.float32),
        "sw_up": np.asarray(inp["L1_ffn_sw_up"], np.float32),
        "sb_up": _ca(np.asarray(inp["L1_ffn_sb_up"], np.float32).reshape(32, 128)),
        "sw_down": np.asarray(inp["L1_ffn_sw_down"], np.float32),
        "sb_down": _ca(np.asarray(inp["L1_ffn_sb_down"], np.float32).reshape(8, 128)),
        "sel8w": sel8w,
    }
    maps = []
    for c in range(NCORE):
        b, j = c // 4, c % 4
        maps.append(dict(base, xloc=_ca(x3[b][TL * j:TL * (j + 1)])))
    res = run_bass_kernel_spmd(nc, maps, core_ids=list(range(NCORE)))
    x4 = np.stack([np.concatenate([res.results[4 * b + j]["xout"] for j in range(4)], 0)
                   for b in range(2)])
    return x4



RG8 = [[0, 1, 2, 3], [4, 5, 6, 7]]


def build_fused():
    p = Prog("mk")
    dram = p.nc.dram_tensor

    def I(n, sh, dt):
        return p.dram(n, sh, dt, "ExternalInput")

    xb0 = I("xb0", [S, D], F32)
    xloc0 = I("xloc0", [TLOC, D], F32)
    a_wq = I("a_wq", [D, 256], F32)
    a_wk = I("a_wk", [D, 256], F32)
    a_wv = I("a_wv", [D, 256], F32)
    a_bq = I("a_bq", [1, 256], F32)
    a_bk = I("a_bk", [1, 256], F32)
    a_bv = I("a_bv", [1, 256], F32)
    b_Wo = I("b_Wo", [D, D], F32)
    b_W1 = I("b_W1", [D, 4096], F32)
    b_W2 = I("b_W2", [4096, D], F32)
    b_bo = I("b_bo", [1, D], F32)
    b_b1 = I("b_b1", [32, 128], F32)
    b_b2 = I("b_b2", [1, D], F32)
    b_l1g = I("b_l1g", [1, D], F32)
    b_l1b = I("b_l1b", [1, D], F32)
    b_l2g = I("b_l2g", [1, D], F32)
    b_l2b = I("b_l2b", [1, D], F32)
    gidx = I("gidx", [128, 64], I32)
    f_wq = I("f_wq", [D, 512], F32)
    f_wk = I("f_wk", [D, 512], F32)
    f_wv = I("f_wv", [D, 512], F32)
    f_wo = I("f_wo", [512, D], F32)
    f_ws = I("f_ws", [D, 8], F32)
    f_bo8 = I("f_bo8", [8, D], F32)
    f_sel = I("f_sel", [8, 512], F32)
    g_wq = I("g_wq", [D, 1024], F32)
    g_wk = I("g_wk", [D, 1024], F32)
    g_wv = I("g_wv", [D, 1024], F32)
    g_wo = I("g_wo", [D, D], F32)
    m_gw = I("m_gw", [D, 8], F32)
    m_gb = I("m_gb", [1, 8], F32)
    m_uw = I("m_uw", [8, D, 4096], F32)
    m_ub = I("m_ub", [8, 4096], F32)
    m_dw = I("m_dw", [8, 4096, D], F32)
    m_db = I("m_db", [8, D], F32)
    m_su = I("m_su", [D, 4096], F32)
    m_sbu = I("m_sbu", [32, 128], F32)
    m_sd = I("m_sd", [4096, D], F32)
    m_sbd = I("m_sbd", [8, 128], F32)
    m_sel = I("m_sel", [8, 1024], BF16)
    yout = p.dram("yout", [TLOC, D], F32, "ExternalOutput")

    a_out = p.dram("a_out", [256, S], BF16, "Internal")
    gAll = p.dram("gAll", [D, S], BF16, "Internal")
    x1_loc = p.dram("x1_loc", [TLOC, D], F32, "Internal")
    x1_full = p.dram("x1_full", [S, D], F32, "Internal")
    x2_loc = p.dram("x2_loc", [TLOC, D], F32, "Internal")
    x2_full = p.dram("x2_full", [S, D], F32, "Internal")
    x3_loc = p.dram("x3_loc", [TLOC, D], F32, "Internal")

    s_cc = p.sem("cc")
    s_bar = p.sem("bar")

    def barrier():
        for e in ENGINES:
            p.do(e, "nop", inc=s_bar)
        v = s_bar.n
        for e in ENGINES:
            p.wait(e, s_bar, v)

    def collective(kind, src, dst, gate_sem):
        p.wait("gpsimd", *gate_sem)
        p.do("gpsimd", "collective_compute", kind, ALU.bypass, RG8,
             [src.opt()], [dst.opt()], inc=s_cc)

    a_shard = p.dram("a_shard", [D, TLOC], BF16, "Internal")
    p.pref = "A_"
    g1 = build_l0a(p, {"xb": xb0, "wq": a_wq, "wk": a_wk, "wv": a_wv,
                       "bq": a_bq, "bk": a_bk, "bv": a_bv, "outT": a_shard})
    p.pref = ""
    gAll2 = p.dram("gAll2", [4 * D, TLOC], BF16, "Internal")
    collective("AllGather", a_shard, gAll2, g1)
    barrier()
    p.pref = "B_"
    g2 = build_l0b(p, {"x": xloc0, "hdT_view": gAll2, "gidx": gidx,
                       "Wo": b_Wo, "W1": b_W1, "W2": b_W2, "bo": b_bo,
                       "b1": b_b1, "b2": b_b2, "ln1g": b_l1g, "ln1b": b_l1b,
                       "ln2g": b_l2g, "ln2b": b_l2b, "xout": x1_loc,
                       "gate": (s_cc, s_cc.n)})
    p.pref = ""
    collective("AllGather", x1_loc, x1_full, g2)
    barrier()
    p.pref = "F_"
    g3 = build_flash("l0f", p, {"xb": x1_full, "xloc": x1_loc,
                                "wq": f_wq, "wk": f_wk, "wv": f_wv, "wo": f_wo,
                                "ws": f_ws, "bo8": f_bo8, "sel8c": f_sel,
                                "xout": x2_loc, "gate": (s_cc, s_cc.n)})
    p.pref = ""
    collective("AllGather", x2_loc, x2_full, g3)
    barrier()
    p.pref = "G_"
    g4 = build_flash("l1a", p, {"xb": x2_full, "xloc": x2_loc,
                                "wq": g_wq, "wk": g_wk, "wv": g_wv, "wo": g_wo,
                                "xout": x3_loc, "gate": (s_cc, s_cc.n)})
    p.pref = ""
    barrier()
    p.pref = "M_"
    g5 = build_l1f(p, {"xloc": x3_loc, "gate": (g4[0], g4[1]),
                       "gate_w": m_gw, "gbias": m_gb, "up_w": m_uw,
                       "up_b": m_ub, "down_w": m_dw, "down_b": m_db,
                       "sw_up": m_su, "sb_up": m_sbu, "sw_down": m_sd,
                       "sb_down": m_sbd, "sel8w": m_sel, "xout": yout})
    p.pref = ""
    return p.finalize()


FUSED = True


def kernel_fused(**inp):
    TL = TLOC
    x = np.asarray(inp["L0_attn_inputs"], np.float32)
    _ = inp.get("L0_attn_attn_mask")
    import ml_dtypes
    nc = build_fused()
    Wq, Wk, Wv = (np.asarray(inp[f"L0_attn_W{z}"], np.float32) for z in "qkv")
    bq, bk, bv = (np.asarray(inp[f"L0_attn_b{z}"], np.float32) for z in "qkv")
    Wkv = np.asarray(inp["L0_ffn_Wkv"], np.float32)
    wqkv = np.asarray(inp["L1_attn_w_qkv"], np.float32)
    sel8c = np.kron(np.eye(8), np.ones((1, 64))).astype(np.float32).reshape(8, 512)
    sel8w = np.kron(np.eye(8), np.ones((1, 128))).astype(
        ml_dtypes.bfloat16).reshape(8, 1024)
    maps = []
    for c in range(NCORE):
        b, j = c // 4, c % 4
        sl = slice(256 * j, 256 * (j + 1))
        ts = slice(TL * j, TL * (j + 1))
        gi = np.empty((128, 64), np.int32)
        for cc in range(8):
            hr = 128 * cc + np.arange(128)
            base = 1024 * (hr // 256) + 256 * j + (hr % 256)
            for gg in range(8):
                gi[:, cc * 8 + gg] = base * 8 + gg
        maps.append({
            "xb0": _ca(x[b]), "xloc0": _ca(x[b][ts]),
            "a_wq": _ca(Wq[:, sl]), "a_wk": _ca(Wk[:, sl]), "a_wv": _ca(Wv[:, sl]),
            "a_bq": _ca(bq[None, sl]), "a_bk": _ca(bk[None, sl]),
            "a_bv": _ca(bv[None, sl]),
            "b_Wo": np.asarray(inp["L0_attn_Wo"], np.float32),
            "b_W1": np.asarray(inp["L0_attn_W1"], np.float32),
            "b_W2": np.asarray(inp["L0_attn_W2"], np.float32),
            "b_bo": _ca(np.asarray(inp["L0_attn_bo"], np.float32)[None, :]),
            "b_b1": _ca(np.asarray(inp["L0_attn_b1"], np.float32).reshape(32, 128)),
            "b_b2": _ca(np.asarray(inp["L0_attn_b2"], np.float32)[None, :]),
            "b_l1g": _ca(np.asarray(inp["L0_attn_ln1_g"], np.float32)[None, :]),
            "b_l1b": _ca(np.asarray(inp["L0_attn_ln1_b"], np.float32)[None, :]),
            "b_l2g": _ca(np.asarray(inp["L0_attn_ln2_g"], np.float32)[None, :]),
            "b_l2b": _ca(np.asarray(inp["L0_attn_ln2_b"], np.float32)[None, :]),
            "gidx": gi,
            "f_wq": np.asarray(inp["L0_ffn_Wq"], np.float32),
            "f_wk": _ca(Wkv[:, 0:512]), "f_wv": _ca(Wkv[:, 512:1024]),
            "f_wo": _ca(np.asarray(inp["L0_ffn_Wo"], np.float32).reshape(512, 1024)),
            "f_ws": np.asarray(inp["L0_ffn_Ws"], np.float32),
            "f_bo8": np.asarray(inp["L0_ffn_bo"], np.float32),
            "f_sel": sel8c,
            "g_wq": _ca(wqkv[:, 0:1024]), "g_wk": _ca(wqkv[:, 1024:2048]),
            "g_wv": _ca(wqkv[:, 2048:3072]),
            "g_wo": np.asarray(inp["L1_attn_w_out"], np.float32),
            "m_gw": np.asarray(inp["L1_ffn_gate_w"], np.float32),
            "m_gb": _ca(np.asarray(inp["L1_ffn_bias"], np.float32)[None, :]),
            "m_uw": np.asarray(inp["L1_ffn_up_w"], np.float32),
            "m_ub": np.asarray(inp["L1_ffn_up_b"], np.float32),
            "m_dw": np.asarray(inp["L1_ffn_down_w"], np.float32),
            "m_db": np.asarray(inp["L1_ffn_down_b"], np.float32),
            "m_su": np.asarray(inp["L1_ffn_sw_up"], np.float32),
            "m_sbu": _ca(np.asarray(inp["L1_ffn_sb_up"], np.float32).reshape(32, 128)),
            "m_sd": np.asarray(inp["L1_ffn_sw_down"], np.float32),
            "m_sbd": _ca(np.asarray(inp["L1_ffn_sb_down"], np.float32).reshape(8, 128)),
            "m_sel": sel8w,
        })
    res = run_bass_kernel_spmd(nc, maps, core_ids=list(range(NCORE)))
    return np.stack([np.concatenate([res.results[4 * b + j]["yout"]
                                     for j in range(4)], 0) for b in range(2)])


# version 08 (final)
# speedup vs baseline: 1.1622x; 1.0046x over previous
import numpy as np
from contextlib import ExitStack
import concourse.bass as bass
import concourse.mybir as mybir
from concourse.bass_utils import run_bass_kernel_spmd
from concourse.bass_core_rust import IndirectOffsetOnAxis

I32 = None

F32 = mybir.dt.float32
BF16 = mybir.dt.bfloat16
AF = mybir.ActivationFunctionType
ALU = mybir.AluOpType
AX = mybir.AxisListType
I32 = mybir.dt.int32

ENGINES = ("sync", "tensor", "vector", "scalar", "gpsimd")

B, S, D = 2, 8192, 1024
NCORE = 8
TLOC = B * S // NCORE


class Sem:
    __slots__ = ("h", "n")

    def __init__(self, h):
        self.h = h
        self.n = 0


class Prog:

    def __init__(self, name="prog"):
        self.nc = bass.Bass("TRN2", target_bir_lowering=False, debug=False)
        self.st = ExitStack()
        self.q = {e: [] for e in ENGINES}
        self._si = 0
        self._ni = 0

    pref = ""

    def sem(self, name=None):
        name = self.pref + (name or f"sm{self._si}")
        self._si += 1
        return Sem(self.st.enter_context(self.nc.semaphore(name)))

    def sbuf(self, shape, dt, name=None):
        name = self.pref + (name or f"sb{self._ni}")
        self._ni += 1
        cm = self.nc.sbuf_tensor(name, shape, dt)
        t = cm.__enter__()
        if not hasattr(self, "_open"):
            self._open = {}
        self._open[id(t)] = cm
        if not hasattr(self, "_order"):
            self._order = []
        self._order.append(t)
        return t

    def free(self, t):
        cm = self._open.pop(id(t))
        if hasattr(self, "_order"):
            try:
                self._order.remove(t)
            except ValueError:
                pass
        cm.__exit__(None, None, None)

    def marker(self):
        return len(getattr(self, "_order", []))

    def free_since(self, mark):
        order = getattr(self, "_order", [])
        while len(order) > mark:
            self.free(order[-1])

    def psum(self, shape, dt=F32, name=None):
        name = self.pref + (name or f"ps{self._ni}")
        self._ni += 1
        cm = self.nc.psum_tensor(name, shape, dt)
        t = cm.__enter__()
        if not hasattr(self, "_open"):
            self._open = {}
        self._open[id(t)] = cm
        if not hasattr(self, "_order"):
            self._order = []
        self._order.append(t)
        return t

    def dram(self, name, shape, dt, kind):
        return self.nc.dram_tensor(self.pref + name, shape, dt, kind=kind).ap()

    def do(self, eng, meth, *args, inc=None, inc_v=1, **kw):
        if inc is not None:
            inc.n += inc_v

            def f(E, meth=meth, args=args, kw=kw, h=inc.h, v=inc_v):
                getattr(E, meth)(*args, **kw).then_inc(h, v)
        else:
            def f(E, meth=meth, args=args, kw=kw):
                getattr(E, meth)(*args, **kw)
        self.q[eng].append(f)

    def wait(self, eng, sem, n=None):
        v = sem.n if n is None else n
        if v <= 0:
            return

        def f(E, h=sem.h, v=v):
            E.wait_ge(h, v)
        self.q[eng].append(f)

    def dma(self, eng, out, in_, sem=None, **kw):
        self.do(eng, "dma_start", inc=sem, inc_v=16, out=out, in_=in_, **kw)

    def dodr(self, eng, meth, *args, **kw):
        if not hasattr(self, "_dr"):
            self._dr = {}
        if eng not in self._dr:
            self._dr[eng] = self.sem(f"dr_{eng}")
        s = self._dr[eng]
        self.do(eng, meth, *args, inc=s, **kw)
        self.wait(eng, s)

    def finalize(self):
        with self.nc.Block() as blk:
            for e in ENGINES:
                ops = self.q[e]
                if ops:
                    getattr(blk, e)(lambda E, ops=ops: [f(E) for f in ops])
        for cm in reversed(list(getattr(self, "_open", {}).values())):
            cm.__exit__(None, None, None)
        self._open = {}
        self.st.close()
        return self.nc



def mk_consts(p):
    c = {}
    c["ones_row"] = p.sbuf([1, 128], BF16, "ones_row")
    c["ones_col"] = p.sbuf([128, 1], BF16, "ones_col")
    c["ident"] = p.sbuf([128, 128], BF16, "ident")
    c["identf"] = p.sbuf([128, 128], F32, "identf")
    c["eps_ln"] = p.sbuf([128, 1], F32, "eps_ln")
    c["eps6"] = p.sbuf([128, 1], F32, "eps6")
    g = "gpsimd"
    p.do(g, "memset", c["ones_row"][:], 1.0)
    p.do(g, "memset", c["ones_col"][:], 1.0)
    p.do(g, "memset", c["eps_ln"][:], 1e-5)
    p.do(g, "memset", c["eps6"][:], 1e-6)
    p.do(g, "memset", c["ident"][:], 1.0)
    p.do(g, "affine_select", c["ident"][:], c["ident"][:], [[1, 128]], ALU.is_equal,
         0.0, base=0, channel_multiplier=-1)
    p.do(g, "memset", c["identf"][:], 1.0)
    c["sem"] = p.sem("consts")
    p.do(g, "affine_select", c["identf"][:], c["identf"][:], [[1, 128]], ALU.is_equal,
         0.0, base=0, channel_multiplier=-1, inc=c["sem"])
    return c


def ln_token_major(p, x_ap, out_ap, tmp, sv, ss, g_bcast=None, b_bcast=None,
                   nfree=1024):
    v, s = "vector", "scalar"
    inv_n = float(1.0 / nfree)
    p.do(v, "reduce_sum", tmp["mean"][:], x_ap, axis=AX.X, inc=sv)
    p.wait(v, sv)
    p.do(v, "tensor_scalar_mul", tmp["mean"][:], tmp["mean"][:], inv_n, inc=sv)
    p.wait(v, sv)
    p.wait(v, ss)
    p.do(v, "tensor_scalar", tmp["xc"][:], x_ap, tmp["mean"][:], None,
         ALU.subtract, inc=sv)
    p.wait(s, sv)
    p.do(s, "activation", tmp["sq"][:], tmp["xc"][:], AF.Square,
         accum_out=tmp["ssq"][:], inc=ss)
    p.wait(s, ss)
    p.do(s, "activation", tmp["rstd"][:], tmp["ssq"][:], AF.Sqrt,
         bias=tmp["eps_ln"][:], scale=inv_n, inc=ss)
    p.wait(v, ss)
    p.do(v, "reciprocal", tmp["rstd"][:], tmp["rstd"][:], inc=sv)
    p.wait(v, sv)
    if g_bcast is None:
        p.do(v, "tensor_scalar_mul", out_ap, tmp["xc"][:], tmp["rstd"][:], inc=sv)
    else:
        p.dodr(v, "tensor_scalar_mul", tmp["xc"][:], tmp["xc"][:], tmp["rstd"][:])
        p.dodr(v, "tensor_mul", tmp["xc"][:], tmp["xc"][:], g_bcast)
        p.do(v, "tensor_add", out_ap, tmp["xc"][:], b_bcast, inc=sv)


def mk_ln_tmp(p, c, nfree=1024):
    return {
        "mean": p.sbuf([128, 1], F32), "ssq": p.sbuf([128, 1], F32),
        "rstd": p.sbuf([128, 1], F32), "xc": p.sbuf([128, nfree], F32),
        "sq": p.sbuf([128, nfree], F32), "eps_ln": c["eps_ln"],
    }


def bcast_row(p, c, row_ap, out_sbuf, ps_bank, sv, st, n=1024, eng="vector"):
    for j in range(0, n, 512):
        w = min(512, n - j)
        p.do("tensor", "matmul", ps_bank[:, 0:w], c["ones_row_f"][:],
             row_ap[:, j:j + w], start=True, stop=True, inc=st)
        p.wait(eng, st)
        p.do(eng, "tensor_copy", out_sbuf[:, j:j + w], ps_bank[:, 0:w], inc=sv)
        p.wait("tensor", sv)



DEBUG_L0B = False


def build_l0b(p=None, io=None):
    fused = p is not None
    if not fused:
        p = Prog("l0b")
    io = io or {}
    mark = p.marker()
    D_ = lambda n, sh, dt: io[n] if n in io else p.dram(n, sh, dt, "ExternalInput")
    x = D_("x", [TLOC, D], F32)
    if fused:
        hdT = io["hdT_view"].rearrange("r (w t) -> (r w) t", t=256)
        gidx = io["gidx"]
    else:
        hdT = D_("hdT", [D, TLOC], BF16)
    Wo = D_("Wo", [D, D], F32)
    W1 = D_("W1", [D, 4096], F32)
    W2 = D_("W2", [4096, D], F32)
    bo = D_("bo", [1, D], F32)
    b1 = D_("b1", [32, 128], F32)
    b2 = D_("b2", [1, D], F32)
    ln1g = D_("ln1g", [1, D], F32)
    ln1b = D_("ln1b", [1, D], F32)
    ln2g = D_("ln2g", [1, D], F32)
    ln2b = D_("ln2b", [1, D], F32)
    xout = io["xout"] if "xout" in io else p.dram("xout", [TLOC, D], F32,
                                                  "ExternalOutput")
    if "gate" in io:
        gs, gv = io["gate"]
        for e in ENGINES:
            p.wait(e, gs, gv)
    if DEBUG_L0B:
        dbg = {nm: p.dram("dbg_" + nm, [256, D], F32, "ExternalOutput")
               for nm in ("xln", "hpre", "h", "y")}

    c = mk_consts(p)
    c["ones_row_f"] = p.sbuf([1, 128], F32, "ones_row_f")
    p.do("gpsimd", "memset", c["ones_row_f"][:], 1.0, inc=c["sem"])

    s_in = p.sem("s_in")
    s_w2 = p.sem("s_w2")
    s_t = p.sem("s_t")
    s_v = p.sem("s_v")
    s_s = p.sem("s_s")
    s_o = p.sem("s_o")
    if fused:
        gidx_sb = p.sbuf([128, 64], I32, "gidx_sb")
        p.dma("gpsimd", gidx_sb[:], gidx, s_o)
        p.wait("gpsimd", s_o)

    WoB = [p.sbuf([128, D], BF16) for _ in range(8)]
    W1B = [p.sbuf([128, 4096], BF16) for _ in range(8)]
    bc = {}
    for nm in ("bo", "b2", "l1g", "l1b", "l2g", "l2b"):
        bc[nm] = p.sbuf([128, D], F32, f"bc_{nm}")
    b1c = p.sbuf([128, 32], F32, "b1c")
    GT = 256
    x_g = p.sbuf([128, 2 * 1024], F32, "x_g")
    xln_g = p.sbuf([128, 2 * 1024], F32, "xln_g")
    hpre_g = p.sbuf([128, 2 * 1024], F32, "hpre_g")
    h_g = p.sbuf([128, 2 * 1024], F32, "h_g")
    hbf_g = p.sbuf([128, 2 * 1024], BF16, "hbf_g")
    hdT_g = [p.sbuf([128, GT], BF16) for _ in range(8)]
    hT_g = [p.sbuf([128, GT], BF16) for _ in range(8)]
    uT = [p.sbuf([128, GT], BF16) for _ in range(2)]
    W2B = [p.sbuf([128, D], BF16) for _ in range(2)]
    w2stg = [p.sbuf([128, D], F32) for _ in range(2)]
    y_g = p.sbuf([128, 2 * 1024], F32, "y_g")
    lt = mk_ln_tmp(p, c)
    psA = p.psum([128, 512], F32, "psA")
    psT = p.psum([128, 128], BF16, "psT")
    psU = [p.psum([128, GT], F32) for _ in range(2)]
    psY = [p.psum([128, 512], F32) for _ in range(4)]

    rows = {"bo": x_g[0:1, 0:1024], "b2": xln_g[0:1, 0:1024],
            "l1g": hpre_g[0:1, 0:1024], "l1b": h_g[0:1, 0:1024],
            "l2g": y_g[0:1, 0:1024], "l2b": y_g[0:1, 1024:2048]}
    for nm, ap in (("bo", bo), ("b2", b2), ("l1g", ln1g), ("l1b", ln1b),
                   ("l2g", ln2g), ("l2b", ln2b)):
        p.dma("sync", rows[nm], ap, s_in)
    b1s = lt["xc"][0:32, 0:128]
    p.dma("sync", b1s, b1, s_in)

    w2c_cnt = []

    def cast_weight_chunks(dst_list, src, ncols):
        cnt = w2c_cnt
        for i, dst in enumerate(dst_list):
            for k in range(0, ncols, 1024):
                idx = len(cnt)
                buf = idx % 2
                if idx >= 2:
                    p.wait("sync", s_v, cnt[idx - 2])
                else:
                    p.wait("sync", s_v)
                p.dma("sync", w2stg[buf][:], src[i * 128:(i + 1) * 128, k:k + 1024],
                      s_in)
                p.wait("vector", s_in)
                p.do("vector", "tensor_copy", dst[:, k:k + 1024], w2stg[buf][:],
                     inc=s_v)
                cnt.append(s_v.n)
    cast_weight_chunks(WoB, Wo, 1024)
    cast_weight_chunks(W1B, W1, 4096)

    p.wait("tensor", s_in, 7 * 16)
    p.wait("tensor", c["sem"])
    p.do("tensor", "transpose", psA[:, 0:32], b1s, c["identf"][0:32, 0:32], inc=s_t)
    p.wait("vector", s_t)
    p.do("vector", "tensor_copy", b1c[:], psA[:, 0:32], inc=s_v)
    p.wait("tensor", s_v)

    for nm in ("bo", "b2", "l1g", "l1b", "l2g", "l2b"):
        bcast_row(p, c, rows[nm], bc[nm], psA, s_v, s_t)
    prep_sv = s_v.n

    NG = TLOC // GT

    def ts(ti):
        return slice(ti * 1024, (ti + 1) * 1024)

    for g in range(NG):
        t0 = g * GT
        if g == 0:
            p.wait("gpsimd", s_v, prep_sv)
            p.wait("sync", s_v, prep_sv)
        for ti in range(2):
            p.dma("sync", x_g[:, ts(ti)], x[t0 + ti * 128:t0 + (ti + 1) * 128, :], s_in)
        for cc in range(8):
            if fused:
                p.do("gpsimd", "indirect_dma_start", hdT_g[cc][:], None,
                     hdT,
                     IndirectOffsetOnAxis(gidx_sb[:, cc * 8 + g:cc * 8 + g + 1], 0),
                     inc=s_o, inc_v=16)
            else:
                p.dma("gpsimd", hdT_g[cc][:], hdT[cc * 128:(cc + 1) * 128, t0:t0 + GT],
                      s_o)

        for ti in range(2):
            p.wait("vector", s_in)
            ln_token_major(p, x_g[:, ts(ti)], xln_g[:, ts(ti)], lt, s_v, s_s)

        p.wait("tensor", s_o)
        p.wait("tensor", s_v)
        for ti in range(2):
            for f in range(2):
                ps = psY[2 * ti + f]
                for cc in range(8):
                    p.do("tensor", "matmul", ps[:],
                         hdT_g[cc][:, ti * 128:(ti + 1) * 128],
                         WoB[cc][:, f * 512:(f + 1) * 512],
                         start=(cc == 0), stop=(cc == 7), inc=(s_t if cc == 7 else None))
            p.wait("vector", s_t)
            for f in range(2):
                ps = psY[2 * ti + f]
                fs = slice(ti * 1024 + f * 512, ti * 1024 + (f + 1) * 512)
                p.dodr("vector", "tensor_add", hpre_g[:, fs], ps[:],
                       bc["bo"][:, f * 512:(f + 1) * 512])
                p.do("vector", "tensor_add", hpre_g[:, fs], hpre_g[:, fs],
                     xln_g[:, fs], inc=s_v)
            ln_token_major(p, hpre_g[:, ts(ti)], h_g[:, ts(ti)], lt, s_v, s_s,
                           g_bcast=bc["l1g"][:], b_bcast=bc["l1b"][:])
            p.wait("vector", s_v)
            p.do("vector", "tensor_copy", hbf_g[:, ts(ti)], h_g[:, ts(ti)], inc=s_v)
        p.wait("tensor", s_v)
        for cc in range(8):
            for ti in range(2):
                p.do("tensor", "transpose", psT[:],
                     hbf_g[:, ti * 1024 + cc * 128:ti * 1024 + (cc + 1) * 128],
                     c["ident"][:], inc=s_t)
                p.wait("scalar", s_t)
                p.do("scalar", "copy", hT_g[cc][:, ti * 128:(ti + 1) * 128], psT[:],
                     inc=s_s)
                p.wait("tensor", s_s)

        for i in range(32):
            buf = i % 2
            sidx = len(w2c_cnt)
            sb = sidx % 2
            p.wait("sync", s_v, w2c_cnt[sidx - 2])
            p.dma("sync", w2stg[sb][:], W2[i * 128:(i + 1) * 128, :], s_w2)
            p.wait("vector", s_w2)
            p.wait("vector", s_t)
            p.do("vector", "tensor_copy", W2B[buf][:], w2stg[sb][:], inc=s_v)
            w2c_cnt.append(s_v.n)
            p.wait("tensor", s_s)
            for cc in range(8):
                p.do("tensor", "matmul", psU[buf][:], W1B[cc][:, i * 128:(i + 1) * 128],
                     hT_g[cc][:], start=(cc == 0), stop=(cc == 7),
                     inc=(s_t if cc == 7 else None))
            p.wait("scalar", s_t)
            p.do("scalar", "activation", uT[buf][:], psU[buf][:], AF.Gelu,
                 bias=b1c[:, i:i + 1], inc=s_s)
            p.wait("tensor", s_s)
            p.wait("tensor", s_v)
            for ti in range(2):
                for f in range(2):
                    p.do("tensor", "matmul", psY[2 * ti + f][:],
                         uT[buf][:, ti * 128:(ti + 1) * 128],
                         W2B[buf][:, f * 512:(f + 1) * 512],
                         start=(i == 0), stop=(i == 31),
                         inc=(s_t if i == 31 else None))
        p.wait("vector", s_t)
        if DEBUG_L0B and g == 0:
            p.wait("gpsimd", s_v)
            for ti in range(2):
                tsl = slice(ti * 128, (ti + 1) * 128)
                p.dma("gpsimd", dbg["xln"][tsl, :], xln_g[:, ts(ti)], s_o)
                p.dma("gpsimd", dbg["hpre"][tsl, :], hpre_g[:, ts(ti)], s_o)
                p.dma("gpsimd", dbg["h"][tsl, :], h_g[:, ts(ti)], s_o)
        for ti in range(2):
            for f in range(2):
                fs = slice(ti * 1024 + f * 512, ti * 1024 + (f + 1) * 512)
                p.dodr("vector", "tensor_add", y_g[:, fs], psY[2 * ti + f][:],
                       bc["b2"][:, f * 512:(f + 1) * 512])
            p.do("vector", "tensor_add", y_g[:, ts(ti)], y_g[:, ts(ti)],
                 h_g[:, ts(ti)], inc=s_v)
            if DEBUG_L0B and g == 0:
                p.wait("gpsimd", s_v)
                p.dma("gpsimd", dbg["y"][ti * 128:(ti + 1) * 128, :],
                      y_g[:, ts(ti)], s_o)
                p.wait("vector", s_o)
            ln_token_major(p, y_g[:, ts(ti)], y_g[:, ts(ti)], lt, s_v, s_s,
                           g_bcast=bc["l2g"][:], b_bcast=bc["l2b"][:])
            p.wait("vector", s_v)
            p.do("vector", "tensor_add", y_g[:, ts(ti)], y_g[:, ts(ti)],
                 x_g[:, ts(ti)], inc=s_v)
            p.wait("gpsimd", s_v)
            p.dma("gpsimd", xout[t0 + ti * 128:t0 + (ti + 1) * 128, :],
                  y_g[:, ts(ti)], s_o)
        p.wait("sync", s_v)
        p.wait("gpsimd", s_o)

    p.wait("gpsimd", s_o)
    if fused:
        out_sem = (s_o, s_o.n)
        p.free_since(mark)
        return out_sem
    return p.finalize()



DEBUG_FLASH = False
DEBUG_FLASH2 = False
FLASH_STOP_AFTER = None
L1F_STOP = None
L1F_NE = 9
DEBUG_L1F = False
DEBUG_L0A = False


def build_flash(kind, p=None, io=None):
    l1a = kind == "l1a"
    H = 16 if l1a else 8
    QF = 64 * H
    NP = H // 2
    SB = S
    fused = p is not None
    if not fused:
        p = Prog(kind)
    io = io or {}
    mark = p.marker()
    D_ = lambda n, sh, dt: io[n] if n in io else p.dram(n, sh, dt, "ExternalInput")
    xb = D_("xb", [SB, D], F32)
    xloc = D_("xloc", [TLOC, D], F32)
    wq = D_("wq", [D, QF], F32)
    wk = D_("wk", [D, QF], F32)
    wv = D_("wv", [D, QF], F32)
    wo = D_("wo", [D, D] if l1a else [QF, D], F32)
    if not l1a:
        ws = D_("ws", [D, 8], F32)
        bo8 = D_("bo8", [8, D], F32)
    xout = io["xout"] if "xout" in io else p.dram("xout", [TLOC, D], F32,
                                                  "ExternalOutput")
    if "gate" in io:
        gs, gv = io["gate"]
        for e in ENGINES:
            p.wait(e, gs, gv)
    if DEBUG_FLASH:
        dbg_out = p.dram("dbg_outT", [QF, TLOC], F32, "ExternalOutput")
        dbg_q = p.dram("dbg_qT", [QF, TLOC], F32, "ExternalOutput")
    if DEBUG_FLASH2:
        dbg_kts = p.dram("dbg_kts", [QF, SB], BF16, "ExternalOutput")
        dbg_vsc = p.dram("dbg_vsc", [H, SB // 128, 128, 64], BF16, "ExternalOutput")
        dbg_q2 = p.dram("dbg_qT2", [QF, TLOC], BF16, "ExternalOutput")
    kts = p.dram("kts", [QF, SB], BF16, "Internal")
    vsc = p.dram("vsc", [H, SB // 128, 128, 64], BF16, "Internal")

    c = mk_consts(p)
    c["ones_row_f"] = p.sbuf([1, 128], F32, "ones_row_f")
    p.do("gpsimd", "memset", c["ones_row_f"][:], 1.0, inc=c["sem"])
    if not l1a:
        sel8_in = io["sel8c"] if "sel8c" in io else p.dram("sel8c", [8, 512], F32,
                                                           "ExternalInput")
        c["sel8"] = p.sbuf([8, 512], F32, "sel8")

    s_in = p.sem("s_in")
    s_vl = p.sem("s_vl")
    s_kv = p.sem("s_kv")
    s_t = p.sem("s_t")
    s_v = p.sem("s_v")
    s_s = p.sem("s_s")
    s_o = p.sem("s_o")
    if not l1a:
        p.dma("sync", c["sel8"][:], sel8_in, s_in)

    QTp = [p.sbuf([128, TLOC], BF16) for _ in range(NP)]
    outTp = [p.sbuf([128, TLOC], BF16) for _ in range(NP)]
    KTp = p.sbuf([128, SB], BF16, "KTp")
    V1 = p.sbuf([128, 64 * 65], BF16, "V1")
    Pt = [p.sbuf([128, 1024], BF16) for _ in range(2)]
    xlnT_g = [p.sbuf([128, 512], BF16) for _ in range(8)]
    xt_f = [p.sbuf([128, 1024], F32) for _ in range(2)]
    xln_f = p.sbuf([128, 1024], F32, "xln_f")
    xbf = p.sbuf([128, 1024], BF16, "xbf")
    evac = [p.sbuf([128, 512], BF16, "evac%d" % _i) for _i in range(2)]
    otmp = p.sbuf([64, 512], F32, "otmp")
    otmp2 = p.sbuf([64, 512], F32, "otmp2")
    rd = p.sbuf([1, 512], F32, "rd")
    lt = mk_ln_tmp(p, c)
    if not l1a:
        s_tok = p.sbuf([128, 8], F32, "s_tok")
        g_m1 = p.sbuf([128, 1], F32)
        g_t1 = p.sbuf([128, 8], F32)
        g_t2 = p.sbuf([128, 8], F32)
        g_bf = p.sbuf([128, 8], BF16)
        gatedT = p.sbuf([8, TLOC], F32, "gatedT")
        gatedT_bf = p.sbuf([8, TLOC], BF16, "gatedT_bf")
        bo8B = p.sbuf([8, D], BF16, "bo8B")

    psT = p.psum([128, 128], BF16, "psT")
    psA = p.psum([128, 512], F32, "psA")
    psB = p.psum([128, 512], F32, "psB")
    psC = p.psum([128, 512], F32, "psC")

    WqB = [p.sbuf([128, QF], BF16) for _ in range(8)]
    WkB = [p.sbuf([128, QF], BF16) for _ in range(8)]
    WvB = [p.sbuf([128, QF], BF16) for _ in range(8)]
    wstg = [p.sbuf([128, 1024], F32) for _ in range(2)]
    if not l1a:
        WsB = [p.sbuf([128, 8], BF16) for _ in range(8)]

    _cstC = []

    def cast_chunks(dst_list, src, ncols):
        for i, dst in enumerate(dst_list):
            for k in range(0, ncols, 1024):
                idx = len(_cstC)
                buf = idx % 2
                if idx >= 2:
                    p.wait("sync", s_v, _cstC[idx - 2])
                else:
                    p.wait("sync", s_v)
                p.dma("sync", wstg[buf][:, 0:min(1024, ncols)],
                      src[i * 128:(i + 1) * 128, k:k + min(1024, ncols)], s_in)
                p.wait("vector", s_in)
                p.do("vector", "tensor_copy", dst[:, k:k + min(1024, ncols)],
                     wstg[buf][:, 0:min(1024, ncols)], inc=s_v)
                _cstC.append(s_v.n)

    cast_chunks(WqB, wq, QF)
    cast_chunks(WkB, wk, QF)
    cast_chunks(WvB, wv, QF)
    if not l1a:
        for i in range(8):
            p.wait("sync", s_v)
            p.dma("sync", wstg[i % 2][:, 0:8], ws[i * 128:(i + 1) * 128, :], s_in)
            p.wait("vector", s_in)
            p.do("vector", "tensor_copy", WsB[i][:], wstg[i % 2][:, 0:8], inc=s_v)
        p.dma("gpsimd", bo8B[:], bo8, s_vl)

    st_cnt = []

    def stream_tiles(src, t0, nt, emit_consumer, local_q):
        for q in range(nt):
            buf = len(st_cnt) % 2
            if len(st_cnt) >= 2:
                p.wait("sync", s_v, st_cnt[-2])
            else:
                p.wait("sync", s_v)
            p.dma("sync", xt_f[buf][:], src[t0 + q * 128:t0 + (q + 1) * 128, :], s_in)
            p.wait("vector", s_in)
            ln_token_major(p, xt_f[buf][:], xln_f[:], lt, s_v, s_s)
            st_cnt.append(s_v.n)
            p.wait("vector", s_v)
            p.wait("vector", s_t)
            p.do("vector", "tensor_copy", xbf[:], xln_f[:], inc=s_v)
            p.wait("tensor", s_v)
            p.wait("tensor", c["sem"])
            for cc in range(8):
                p.do("tensor", "transpose", psT[:], xbf[:, cc * 128:(cc + 1) * 128],
                     c["ident"][:], inc=s_t)
                p.wait("scalar", s_t)
                p.do("scalar", "copy", xlnT_g[cc][:, q * 128:(q + 1) * 128], psT[:],
                     inc=s_s)
                p.wait("tensor", s_s)
        emit_consumer()

    kv_cnt = []

    def kv_evac_wait():
        if len(kv_cnt) >= 2:
            p.wait("scalar", s_kv, kv_cnt[-2])
        else:
            p.wait("scalar", s_kv)
        return evac[len(kv_cnt) % 2]

    def kv_consumer(t0):
        for kf in range(QF // 128):
            p.wait("tensor", s_s)
            for cc in range(8):
                p.do("tensor", "matmul", psA[:], WkB[cc][:, kf * 128:(kf + 1) * 128],
                     xlnT_g[cc][:], start=(cc == 0), stop=(cc == 7),
                     inc=(s_t if cc == 7 else None))
            p.wait("scalar", s_t)
            eb = kv_evac_wait()
            p.do("scalar", "copy", eb[:], psA[:], inc=s_s)
            p.wait("gpsimd", s_s)
            p.dma("gpsimd", kts[kf * 128:(kf + 1) * 128, t0:t0 + 512], eb[:], s_kv)
            kv_cnt.append(s_kv.n)
        for q in range(4):
            for half in range(QF // 512):
                fs = slice(half * 512, (half + 1) * 512)
                p.wait("tensor", s_s)
                for cc in range(8):
                    p.do("tensor", "matmul", psB[:],
                         xlnT_g[cc][:, q * 128:(q + 1) * 128], WvB[cc][:, fs],
                         start=(cc == 0), stop=(cc == 7),
                         inc=(s_t if cc == 7 else None))
                p.wait("scalar", s_t)
                eb = kv_evac_wait()
                p.do("scalar", "copy", eb[:], psB[:], inc=s_s)
                p.wait("gpsimd", s_s)
                j = (t0 + q * 128) // 128
                p.dma("gpsimd",
                      vsc[half * 8:(half + 1) * 8, j].rearrange("h p d -> p h d"),
                      eb[:].rearrange("p (h d) -> p h d", d=64),
                      s_kv)
                kv_cnt.append(s_kv.n)

    for g in range(SB // 512):
        stream_tiles(xb, g * 512, 4, lambda g=g: kv_consumer(g * 512), False)

    def q_consumer(t0):
        for qf in range(QF // 128):
            p.wait("tensor", s_s)
            for cc in range(8):
                p.do("tensor", "matmul", psA[:], WqB[cc][:, qf * 128:(qf + 1) * 128],
                     xlnT_g[cc][:], start=(cc == 0), stop=(cc == 7),
                     inc=(s_t if cc == 7 else None))
            p.wait("scalar", s_t)
            p.do("scalar", "activation", QTp[qf][:, t0:t0 + 512], psA[:], AF.Copy,
                 scale=0.125, inc=s_s)
            p.wait("tensor", s_s)
        if (not l1a) and FLASH_STOP_AFTER != "nogate":
            for q in range(4):
                tsl = slice(t0 + q * 128, t0 + (q + 1) * 128)
                for cc in range(8):
                    p.do("tensor", "matmul", psC[0:128, 0:8],
                         xlnT_g[cc][:, q * 128:(q + 1) * 128], WsB[cc][:],
                         start=(cc == 0), stop=(cc == 7),
                         inc=(s_t if cc == 7 else None))
                p.wait("scalar", s_t)
                p.do("scalar", "activation", s_tok[:], psC[0:128, 0:8], AF.Sigmoid,
                     inc=s_s)
                if FLASH_STOP_AFTER == "gate1":
                    continue
                v = "vector"
                p.wait(v, s_s)
                p.dodr(v, "reduce_max", g_m1[:], s_tok[:], axis=AX.X)
                p.dodr(v, "tensor_scalar", g_t1[:], s_tok[:], g_m1[:], None,
                       ALU.is_equal)
                p.dodr(v, "scalar_tensor_tensor", g_t2[:], g_t1[:], -1e30, s_tok[:],
                       ALU.mult, ALU.add)
                p.dodr(v, "reduce_max", g_m1[:], g_t2[:], axis=AX.X)
                p.dodr(v, "tensor_scalar", g_t1[:], g_t2[:], g_m1[:], None,
                       ALU.is_equal)
                p.dodr(v, "scalar_tensor_tensor", g_t2[:], g_t1[:], -1e30, g_t2[:],
                       ALU.mult, ALU.add)
                p.dodr(v, "reduce_max", g_m1[:], g_t2[:], axis=AX.X)
                p.dodr(v, "tensor_scalar", g_t1[:], s_tok[:], g_m1[:], None,
                       ALU.is_ge)
                p.dodr(v, "tensor_mul", g_t2[:], s_tok[:], g_t1[:])
                p.do(v, "tensor_copy", g_bf[:], g_t2[:], inc=s_v)
                if FLASH_STOP_AFTER == "gate2":
                    continue
                p.wait("tensor", s_v)
                p.do("tensor", "transpose", psT[0:8, :], g_bf[:], c["ident"][:],
                     inc=s_t)
                p.wait("scalar", s_t)
                p.do("scalar", "copy", gatedT[0:8, tsl], psT[0:8, :], inc=s_s)
                p.wait("vector", s_s)
                p.do("vector", "tensor_copy", gatedT_bf[0:8, tsl], gatedT[0:8, tsl],
                     inc=s_v)
                p.wait("tensor", s_v)
                p.wait("tensor", s_s)

    if FLASH_STOP_AFTER == "p2":
        p.wait("gpsimd", s_kv, s_kv.n)
        p.dma("gpsimd", xout[0:128, :], xt_f[0][:], s_o)
        p.wait("gpsimd", s_o)
        return p.finalize()
    for g in range(TLOC // 512):
        stream_tiles(xloc, g * 512, 4,
                     lambda g=g: q_consumer(g * 512), True)

    kv_done = s_kv.n
    q_done = s_s.n
    if DEBUG_FLASH2:
        p.wait("gpsimd", s_kv, kv_done)
        p.wait("gpsimd", s_s, q_done)
        p.dma("gpsimd", dbg_kts, kts, s_o)
        p.dma("gpsimd", dbg_vsc, vsc, s_o)
        for j in range(NP):
            p.dma("gpsimd", dbg_q2[j * 128:(j + 1) * 128, :], QTp[j][:], s_o)
    if not l1a:
        for t in reversed(WsB):
            p.free(t)
    for t in reversed(wstg):
        p.free(t)
    for t in reversed(WvB):
        p.free(t)
    for t in reversed(WkB):
        p.free(t)
    for t in reversed(WqB):
        p.free(t)
    psW = [p.psum([128, 1024], F32) for _ in range(2)]
    WoB_n = 8 if l1a else 4
    WoB = [p.sbuf([128, D], BF16) for _ in range(WoB_n)]
    wstg2 = [p.sbuf([128, 1024], F32) for _ in range(2)]
    p.wait("sync", s_t)
    for i in range(WoB_n):
        buf = i % 2
        p.wait("sync", s_v)
        p.dma("sync", wstg2[buf][:], wo[i * 128:(i + 1) * 128, :], s_in)
        p.wait("vector", s_in)
        p.do("vector", "tensor_copy", WoB[i][:], wstg2[buf][:], inc=s_v)

    p.do("gpsimd", "memset",
         V1[:].rearrange("p (j x) -> p j x", x=65)[:, :, 64:65], 1.0, inc=s_kv,
         inc_v=1)
    ones_done = s_kv.n

    if FLASH_STOP_AFTER == "p3":
        p.wait("gpsimd", s_kv, kv_done)
        p.wait("gpsimd", s_s, q_done)
        p.dma("gpsimd", xout[0:128, :], xt_f[0][:], s_o)
        p.wait("gpsimd", s_o)
        return p.finalize()

    exp_cnt = []
    pv_cnt = []
    p.wait("sync", s_kv, kv_done)
    p.wait("gpsimd", s_kv, kv_done)
    for hp in range(NP):
        p.wait("sync", s_t)
        p.dma("sync", KTp[:], kts[hp * 128:(hp + 1) * 128, :], s_in)
        for hh in range(2):
            h = 2 * hp + hh
            base = 64 * hh
            p.wait("gpsimd", s_t)
            p.wait("gpsimd", s_kv, ones_done)
            p.do("gpsimd", "dma_start", inc=s_vl, inc_v=16,
                 out=V1[:].rearrange("p (j x) -> p j x", x=65)[:, :, 0:64],
                 in_=vsc[h].rearrange("j p d -> p j d"))
            for qtp in range(TLOC // 1024):
                q0 = qtp * 1024
                p.wait("tensor", s_v)
                p.wait("tensor", s_in)
                p.wait("tensor", s_vl)
                exp_cnt.clear()
                NJ = SB // 128

                def emit_scores(j):
                    b = j % 2
                    if j >= 2:
                        p.wait("tensor", s_s, exp_cnt[j - 2])
                    for half in range(2):
                        p.do("tensor", "matmul",
                             psW[b][:, half * 512:(half + 1) * 512],
                             KTp[base:base + 64, j * 128:(j + 1) * 128],
                             QTp[hp][base:base + 64,
                                     q0 + half * 512:q0 + (half + 1) * 512],
                             start=True, stop=True,
                             inc=(s_t if half == 1 else None))

                def emit_exp(j):
                    b = j % 2
                    p.wait("scalar", s_t)
                    p.do("scalar", "activation", Pt[b][:], psW[b][:], AF.Exp,
                         inc=s_s)
                    exp_cnt.append(s_s.n)

                def emit_pv(j):
                    b = j % 2
                    p.wait("tensor", s_s, exp_cnt[j])
                    for half in range(2):
                        p.do("tensor", "matmul",
                             (psA if half == 0 else psB)[0:65, :],
                             V1[:, j * 65:j * 65 + 65],
                             Pt[b][:, half * 512:(half + 1) * 512],
                             start=(j == 0), stop=(j == NJ - 1),
                             inc=(s_t if half == 1 else None))

                emit_scores(0)
                emit_exp(0)
                for j in range(1, NJ):
                    emit_scores(j)
                    emit_exp(j)
                    emit_pv(j - 1)
                emit_pv(NJ - 1)
                for half in range(2):
                    ps = psA if half == 0 else psB
                    qsl = slice(q0 + half * 512, q0 + (half + 1) * 512)
                    p.wait("scalar", s_t)
                    p.wait("scalar", s_v)
                    p.do("scalar", "copy", otmp[:], ps[0:64, :], inc=s_s)
                    p.wait("vector", s_t)
                    p.do("vector", "reciprocal", rd[:], ps[64:65, :], inc=s_v)
                    p.wait("tensor", s_v)
                    p.wait("tensor", s_s)
                    p.wait("vector", s_s)
                    if l1a:
                        p.do("tensor", "matmul", psC[0:64, :], c["ones_row_f"][0:1, 0:64],
                             rd[:], start=True, stop=True, inc=s_t)
                        p.wait("vector", s_t)
                        p.do("vector", "tensor_mul", outTp[hp][base:base + 64, qsl],
                             otmp[:], psC[0:64, :], inc=s_v)
                    else:
                        p.do("tensor", "matmul", psC[0:64, :],
                             c["sel8"][:, h * 64:(h + 1) * 64], gatedT[0:8, qsl],
                             start=True, stop=True, inc=s_t)
                        p.wait("vector", s_t)
                        p.do("vector", "tensor_mul", otmp2[:], otmp[:], psC[0:64, :],
                             inc=s_v)
                        p.wait("vector", s_v)
                        p.wait("tensor", s_v)
                        p.do("tensor", "matmul", psC[0:64, :], c["ones_row_f"][0:1, 0:64],
                             rd[:], start=True, stop=True, inc=s_t)
                        p.wait("vector", s_t)
                        p.do("vector", "tensor_mul", outTp[hp][base:base + 64, qsl],
                             otmp2[:], psC[0:64, :], inc=s_v)
                    p.wait("tensor", s_v)

    if DEBUG_FLASH:
        dstg = p.sbuf([128, TLOC], F32, "dstg")
        for j in range(NP):
            p.wait("vector", s_o)
            p.do("vector", "tensor_copy", dstg[:], outTp[j][:], inc=s_v)
            p.wait("gpsimd", s_v)
            p.dma("gpsimd", dbg_out[j * 128:(j + 1) * 128, :], dstg[:], s_o)
            p.wait("vector", s_o)
            p.do("vector", "tensor_copy", dstg[:], QTp[j][:], inc=s_v)
            p.wait("gpsimd", s_v)
            p.dma("gpsimd", dbg_q[j * 128:(j + 1) * 128, :], dstg[:], s_o)
        dbg_wo = p.dram("dbg_wo", [WoB_n * 128, D], F32, "ExternalOutput")
        for j in range(WoB_n):
            p.wait("vector", s_o)
            p.do("vector", "tensor_copy", dstg[:, 0:1024], WoB[j][:], inc=s_v)
            p.wait("gpsimd", s_v)
            p.dma("gpsimd", dbg_wo[j * 128:(j + 1) * 128, :], dstg[:, 0:1024], s_o)
        p.wait("vector", s_o)

    if FLASH_STOP_AFTER == "flash":
        p.wait("gpsimd", s_v)
        p.dma("gpsimd", xout[0:128, :], xt_f[0][:], s_o)
        p.wait("gpsimd", s_o)
        return p.finalize()

    for t in range(TLOC // 128):
        buf = t % 2
        p.wait("sync", s_v)
        p.dma("sync", xt_f[buf][:], xloc[t * 128:(t + 1) * 128, :], s_in)
        for f in range(2):
            ps = psW[f][:, 0:512]
            p.wait("tensor", s_v)
            p.wait("tensor", s_in)
            p.wait("tensor", s_vl)
            for j in range(NP):
                p.do("tensor", "matmul", ps, outTp[j][:, t * 128:(t + 1) * 128],
                     WoB[j][:, f * 512:(f + 1) * 512],
                     start=(j == 0), stop=(j == NP - 1 and l1a),
                     inc=(s_t if (j == NP - 1 and l1a) else None))
            if not l1a:
                p.do("tensor", "matmul", ps, gatedT_bf[0:8, t * 128:(t + 1) * 128],
                     bo8B[0:8, f * 512:(f + 1) * 512], start=False, stop=True,
                     inc=s_t)
            p.wait("vector", s_t)
            p.wait("vector", s_in)
            p.wait("vector", s_o)
            p.do("vector", "tensor_add", lt["xc"][:, 0:512], ps,
                 xt_f[buf][:, f * 512:(f + 1) * 512], inc=s_v)
            p.wait("gpsimd", s_v)
            p.dma("gpsimd", xout[t * 128:(t + 1) * 128, f * 512:(f + 1) * 512],
                  lt["xc"][:, 0:512], s_o)
    p.wait("gpsimd", s_o)
    if fused:
        out_sem = (s_o, s_o.n)
        p.free_since(mark)
        return out_sem
    return p.finalize()



def build_l1f(p=None, io=None):
    fused = p is not None
    if not fused:
        p = Prog("l1f")
    io = io or {}
    mark = p.marker()
    D_ = lambda n, sh, dt: io[n] if n in io else p.dram(n, sh, dt, "ExternalInput")
    xloc = D_("xloc", [TLOC, D], F32)
    gate_w = D_("gate_w", [D, 8], F32)
    gbias = D_("gbias", [1, 8], F32)
    up_w = D_("up_w", [8, D, 4096], F32)
    up_b = D_("up_b", [8, 4096], F32)
    down_w = D_("down_w", [8, 4096, D], F32)
    down_b = D_("down_b", [8, D], F32)
    sw_up = D_("sw_up", [D, 4096], F32)
    sb_up = D_("sb_up", [32, 128], F32)
    sw_down = D_("sw_down", [4096, D], F32)
    sb_down = D_("sb_down", [8, 128], F32)
    sel8w = D_("sel8w", [8, 1024], BF16)
    xout = io["xout"] if "xout" in io else p.dram("xout", [TLOC, D], F32,
                                                  "ExternalOutput")
    if "gate" in io:
        gs, gv = io["gate"]
        for e in ENGINES:
            p.wait(e, gs, gv)
    if DEBUG_L1F:
        dbg_w = p.dram("dbg_w01T", [8, TLOC], F32, "ExternalOutput")
        dbg_u = p.dram("dbg_uscr", [32, 128, 512], BF16, "ExternalOutput")
        dbg_cb = p.dram("dbg_comb", [D, TLOC], F32, "ExternalOutput")
    comb = p.dram("comb", [D, TLOC], F32, "Internal")
    uscr = p.dram("uscr", [32, 128, 512], BF16, "Internal")

    c = mk_consts(p)
    c["ones_row_f"] = p.sbuf([1, 128], F32, "ones_row_f")
    p.do("gpsimd", "memset", c["ones_row_f"][:], 1.0, inc=c["sem"])

    s_in = p.sem("s_in")
    s_t = p.sem("s_t")
    s_v = p.sem("s_v")
    s_s = p.sem("s_s")
    s_o = p.sem("s_o")

    QT = 512
    NQ = TLOC // QT
    UpB = [p.sbuf([128, 4096], BF16) for _ in range(8)]
    xbf = UpB[0][:, 0:1024]
    DnB = [p.sbuf([128, D], BF16) for _ in range(32)]
    uT2 = [p.sbuf([128, QT], BF16) for _ in range(2)]
    ub2 = [p.sbuf([128, QT], BF16) for _ in range(2)]
    xlnT = [p.sbuf([128, TLOC], BF16) for _ in range(8)]
    GwBt = p.sbuf([128, 64], BF16, "GwBt")
    GwB = [GwBt[:, cc * 8:(cc + 1) * 8] for cc in range(8)]
    w01T = p.sbuf([8, TLOC], BF16, "w01T")
    sel8B = p.sbuf([8, 1024], BF16, "sel8B")
    wb_sb = p.sbuf([128, QT], F32, "wb_sb")
    ev_one = p.sbuf([128, QT], F32, "ev_one")
    ev_sb = [ev_one[:], ev_one[:]]
    up_bT = p.sbuf([128, 32 * 8], F32, "up_bT")
    db_sb = p.sbuf([128, 8 * 8], F32, "db_sb")
    sbu_c = p.sbuf([128, 32], F32, "sbu_c")
    sbd_c = p.sbuf([128, 8], F32, "sbd_c")
    xt_one = p.sbuf([128, 1024], F32, "xt_one")
    xt_f = [xt_one, xt_one]
    fac = p.sbuf([128, 4], F32, "fac")
    stg = [p.sbuf([128, 512], F32) for _ in range(2)]
    lt = mk_ln_tmp(p, c)
    g_s = p.sbuf([128, 8], F32)
    g_m = p.sbuf([128, 1], F32)
    g_a = p.sbuf([128, 8], F32)
    g_k = p.sbuf([128, 8], F32)
    g_bf = p.sbuf([128, 8], BF16)
    gb_bc = p.sbuf([128, 8], F32, "gb_bc")
    gbrow = p.sbuf([1, 8], F32, "gbrow")
    hbuf = [xt_f[0][:, 0:QT], xt_f[0][:, QT:2 * QT]]
    macc = None
    xsl = [lt["xc"][:, 0:128], lt["xc"][:, 128:256]]
    osl = [lt["xc"][:, 256:384], lt["xc"][:, 384:512]]
    macc = lt["sq"][:, 0:QT]
    psT = p.psum([128, 128], BF16, "psT")
    psU = [p.psum([128, QT], F32) for _ in range(2)]
    psD = [p.psum([128, QT], F32) for _ in range(5)]

    p.dma("sync", sel8B[:], sel8w, s_in)
    p.dma("sync", gbrow[:], gbias, s_in)
    for cc in range(8):
        p.wait("sync", s_v)
        p.dma("sync", stg[cc % 2][:, 0:8], gate_w[cc * 128:(cc + 1) * 128, :], s_in)
        p.wait("vector", s_in)
        p.do("vector", "tensor_copy", GwB[cc], stg[cc % 2][:, 0:8], inc=s_v)
    p.wait("tensor", s_in, 32)
    p.wait("tensor", c["sem"])
    p.do("tensor", "matmul", psU[0][0:128, 0:8], c["ones_row_f"][:], gbrow[:],
         start=True, stop=True, inc=s_t)
    p.wait("vector", s_t)
    p.do("vector", "tensor_copy", gb_bc[:], psU[0][0:128, 0:8], inc=s_v)
    p.wait("tensor", s_v)

    def load_T(src, rows, dst, dstc0, nsl=1):
        for c0 in range(0, nsl, 4):
            cn = min(4, nsl - c0)
            p.wait("sync", s_v)
            p.wait("sync", s_s)
            p.dma("sync", stg[0][0:rows, 0:cn * 128],
                  src[:, c0 * 128:(c0 + cn) * 128], s_in)
            for jj in range(cn):
                j = c0 + jj
                p.wait("tensor", s_in)
                p.wait("tensor", s_s)
                p.do("tensor", "transpose", psU[1][0:128, 0:rows],
                     stg[0][0:rows, jj * 128:(jj + 1) * 128],
                     c["identf"][0:rows, 0:rows], inc=s_t)
                p.wait("scalar", s_t)
                p.do("scalar", "copy",
                     dst[:, dstc0 + j * rows:dstc0 + (j + 1) * rows],
                     psU[1][0:128, 0:rows], inc=s_s)

    for e in range(8):
        load_T(up_b[e].rearrange("(i x) -> i x", x=128), 32, up_bT, e * 32, nsl=1)
    for e in range(8):
        load_T(down_b[e].rearrange("(i x) -> i x", x=128), 8, db_sb, e * 8, nsl=1)
    load_T(sb_up, 32, sbu_c, 0, nsl=1)
    load_T(sb_down, 8, sbd_c, 0, nsl=1)
    p.wait("tensor", s_s)

    if L1F_STOP == "prep":
        p.wait("gpsimd", s_s, s_s.n)
        p.dma("gpsimd", xout[0:128, :], stg[0][0:128, 0:512], s_o)
        p.wait("gpsimd", s_o)
        return p.finalize()

    p1_cnt = []
    for t in range(TLOC // 128):
        t0 = t * 128
        buf = t % 2
        if t >= 2:
            p.wait("sync", s_v, p1_cnt[-2])
        else:
            p.wait("sync", s_v)
        p.dma("sync", xt_f[buf][:], xloc[t0:t0 + 128, :], s_in)
        p.wait("vector", s_in)
        ln_token_major(p, xt_f[buf][:], lt["sq"][:], lt, s_v, s_s)
        p1_cnt.append(s_v.n)
        p.wait("vector", s_v)
        p.wait("vector", s_t)
        p.do("vector", "tensor_copy", xbf, lt["sq"][:], inc=s_v)
        p.wait("tensor", s_v)
        for cc in range(8):
            p.do("tensor", "transpose", psT[:], xbf[:, cc * 128:(cc + 1) * 128]
                 if False else UpB[0][:, cc * 128:(cc + 1) * 128],
                 c["ident"][:], inc=s_t)
            p.wait("scalar", s_t)
            p.do("scalar", "copy", xlnT[cc][:, t0:t0 + 128], psT[:], inc=s_s)
            p.wait("tensor", s_s)
        for cc in range(8):
            p.do("tensor", "matmul", psU[0][0:128, 0:8], xlnT[cc][:, t0:t0 + 128],
                 GwB[cc], start=(cc == 0), stop=(cc == 7),
                 inc=(s_t if cc == 7 else None))
        v = "vector"
        p.wait(v, s_t)
        p.dodr(v, "tensor_add", g_a[:], psU[0][0:128, 0:8], gb_bc[:])
        p.wait("scalar", s_v)
        p.wait("scalar", p._dr["vector"])
        p.do("scalar", "activation", g_s[:], g_a[:], AF.Sigmoid, inc=s_s)
        p.wait(v, s_s)
        p.dodr(v, "reduce_max", g_m[:], g_s[:], axis=AX.X)
        p.dodr(v, "tensor_scalar", g_a[:], g_s[:], g_m[:], None, ALU.is_equal)
        p.dodr(v, "scalar_tensor_tensor", g_a[:], g_a[:], -1e30, g_s[:],
               ALU.mult, ALU.add)
        p.dodr(v, "reduce_max", g_m[:], g_a[:], axis=AX.X)
        p.dodr(v, "tensor_scalar", g_k[:], g_s[:], g_m[:], None, ALU.is_ge)
        p.dodr(v, "tensor_mul", g_k[:], g_s[:], g_k[:])
        p.dodr(v, "reduce_sum", g_m[:], g_k[:], axis=AX.X)
        p.dodr(v, "tensor_scalar_add", g_m[:], g_m[:], 1e-6)
        p.dodr(v, "reciprocal", g_m[:], g_m[:])
        p.dodr(v, "tensor_scalar_mul", g_m[:], g_m[:], 0.1)
        p.dodr(v, "tensor_scalar_mul", g_k[:], g_k[:], g_m[:])
        p.do(v, "tensor_copy", g_bf[:], g_k[:], inc=s_v)
        p.wait("tensor", s_v)
        p.do("tensor", "transpose", psT[0:8, :], g_bf[:], c["ident"][:], inc=s_t)
        p.wait("scalar", s_t)
        p.do("scalar", "copy", w01T[0:8, t0:t0 + 128], psT[0:8, :], inc=s_s)
        p.wait("tensor", s_s)
        p.wait("vector", s_s)

    if L1F_STOP == "p1":
        p.wait("gpsimd", s_s, s_s.n)
        p.dma("gpsimd", xout[0:128, :], xt_f[0][:], s_o)
        p.wait("gpsimd", s_o)
        return p.finalize()

    region_cnt = {}
    ew_cnt = []
    for e in range(L1F_NE):
        shared = e == 8
        up_src = sw_up if shared else up_w[e]
        dn_src = sw_down if shared else down_w[e]
        p.wait("vector", s_t)
        srcs = ([(UpB[cc][:, k4 * 512:(k4 + 1) * 512],
                  up_src[cc * 128:(cc + 1) * 128, k4 * 512:(k4 + 1) * 512])
                 for cc in range(8) for k4 in range(8)] +
                [(DnB[i][:, k2 * 512:(k2 + 1) * 512],
                  dn_src[i * 128:(i + 1) * 128, k2 * 512:(k2 + 1) * 512])
                 for i in range(32) for k2 in range(2)])
        for dst, src_ap in srcs:
            idx = len(ew_cnt)
            b = idx % 2
            if idx >= 2:
                p.wait("sync", s_v, ew_cnt[idx - 2])
            else:
                p.wait("sync", s_v)
            p.dma("sync", stg[b][:], src_ap, s_in)
            p.wait("vector", s_in)
            p.do("vector", "tensor_copy", dst, stg[b][:], inc=s_v)
            ew_cnt.append(s_v.n)

        if L1F_STOP == "wonly":
            continue
        for g in range(NQ):
            q0 = g * QT
            if L1F_STOP == "noup":
                break
            if L1F_STOP == "nowb":
                pass
            elif shared:
                p.wait("vector", s_o)
                p.do("vector", "memset", wb_sb[:], 0.1, inc=s_v)
            else:
                p.wait("tensor", s_v)
                p.wait("tensor", s_s)
                p.do("tensor", "matmul", psU[0][:, 0:QT],
                     sel8B[:, e * 128:(e + 1) * 128], w01T[0:8, q0:q0 + QT],
                     start=True, stop=True, inc=s_t)
                p.wait("vector", s_t)
                p.do("vector", "tensor_copy", wb_sb[:], psU[0][:, 0:QT], inc=s_v)
            if L1F_STOP == "nogelu":
                continue
            if not hasattr(p, "_l1f_gelu"):
                p._l1f_gelu = []
                p._l1f_udma = []
            gelu_cnt = p._l1f_gelu
            udma_cnt = p._l1f_udma
            gb0 = len(gelu_cnt)
            ub0 = len(udma_cnt)
            p.wait("tensor", s_v)

            def emit_up(i):
                b = i % 2
                if gb0 + i >= 2:
                    p.wait("tensor", s_s, gelu_cnt[gb0 + i - 2])
                for cc in range(8):
                    p.do("tensor", "matmul", psU[b][:, 0:QT],
                         UpB[cc][:, i * 128:(i + 1) * 128], xlnT[cc][:, q0:q0 + QT],
                         start=(cc == 0), stop=(cc == 7),
                         inc=(s_t if cc == 7 else None))

            def emit_gelu(i):
                b = i % 2
                p.wait("scalar", s_t)
                if ub0 + i >= 2:
                    p.wait("scalar", s_o, udma_cnt[ub0 + i - 2])
                bias_col = (sbu_c[:, i:i + 1] if shared
                            else up_bT[:, e * 32 + i:e * 32 + i + 1])
                p.do("scalar", "activation", uT2[b][:], psU[b][:, 0:QT],
                     AF.Gelu, bias=bias_col, inc=s_s)
                gelu_cnt.append(s_s.n)

            def emit_sweepA(i):
                b = i % 2
                p.wait("tensor", s_s, gelu_cnt[gb0 + i])
                for hi, h in enumerate(range(0, 5)):
                    p.do("tensor", "matmul", psD[hi][:, 0:QT],
                         DnB[i][:, h * 128:(h + 1) * 128], uT2[b][:],
                         start=(i == 0), stop=(i == 31),
                         inc=(s_t if (i == 31 and hi == 4) else None))
                p.wait("gpsimd", s_s, gelu_cnt[gb0 + i])
                p.dma("gpsimd", uscr[i], uT2[b][:], s_o)
                udma_cnt.append(s_o.n)

            emit_up(0)
            emit_gelu(0)
            for i in range(1, 32):
                emit_up(i)
                emit_gelu(i)
                emit_sweepA(i - 1)
            emit_sweepA(31)

            def emit_evac(hi, h):
                eb = ev_sb[0]
                db_col = (sbd_c[:, h:h + 1] if shared
                          else db_sb[:, e * 8 + h:e * 8 + h + 1])
                p.wait("vector", s_o)
                p.do("vector", "scalar_tensor_tensor", eb[:], psD[hi][:, 0:QT],
                     db_col, wb_sb[:], ALU.add, ALU.mult, inc=s_v)
                p.wait("gpsimd", s_v)
                key = (g, h)
                if key in region_cnt:
                    p.wait("gpsimd", s_o, region_cnt[key])
                p.dma("gpsimd", comb[h * 128:(h + 1) * 128, q0:q0 + QT], eb[:],
                      s_o, accum_op=(ALU.bypass if e == 0 else ALU.add))
                region_cnt[key] = s_o.n

            if L1F_STOP not in ("noswa", "noevac"):
                p.wait("vector", s_t)
                for hi, h in enumerate(range(0, 5)):
                    emit_evac(hi, h)
            if L1F_STOP in ("noswb", "noswa", "noevac"):
                continue
            p.wait("tensor", s_v)
            p.wait("sync", s_t)
            mmB_cnt = []
            for i in range(32):
                b = i % 2
                p.wait("sync", s_o)
                if i >= 2:
                    p.wait("sync", s_t, mmB_cnt[i - 2])
                p.dma("sync", ub2[b][:], uscr[i], s_in)
                p.wait("tensor", s_in)
                for hi, h in enumerate(range(5, 8)):
                    p.do("tensor", "matmul", psD[hi][:, 0:QT],
                         DnB[i][:, h * 128:(h + 1) * 128], ub2[b][:],
                         start=(i == 0), stop=(i == 31),
                         inc=(s_t if hi == 2 else None))
                mmB_cnt.append(s_t.n)
            p.wait("vector", s_t)
            for hi, h in enumerate(range(5, 8)):
                emit_evac(hi, h)

    if L1F_STOP == "p2":
        p.wait("gpsimd", s_o, s_o.n)
        p.dma("gpsimd", xout[0:128, :], xt_f[0][:], s_o)
        p.wait("gpsimd", s_o)
        return p.finalize()

    if DEBUG_L1F:
        for gq in range(4):
            p.wait("vector", s_o)
            p.do("vector", "tensor_copy", ev_sb[0][0:8, 0:512],
                 w01T[0:8, gq * 512:(gq + 1) * 512], inc=s_v)
            p.wait("gpsimd", s_v)
            p.dma("gpsimd", dbg_w[0:8, gq * 512:(gq + 1) * 512],
                  ev_sb[0][0:8, 0:512], s_o)
        p.wait("gpsimd", s_o)
        p.dma("gpsimd", dbg_u, uscr, s_o)
        p.dma("gpsimd", dbg_cb, comb, s_o)
        p.wait("gpsimd", s_o)

    all_acc = s_o.n
    p.wait("sync", s_o, all_acc)
    for g in range(NQ):
        q0 = g * QT
        for h in range(8):
            hb = hbuf[h % 2]
            p.wait("sync", s_v)
            p.wait("sync", s_s)
            p.dma("sync", hb, comb[h * 128:(h + 1) * 128, q0:q0 + QT], s_in)
            p.wait("scalar", s_in)
            p.wait("scalar", s_v)
            p.wait("scalar", p._dr.get("vector", s_v))
            p.do("scalar", "activation", ev_sb[0][:], hb, AF.Abs, inc=s_s)
            p.wait("vector", s_s)
            if h == 0:
                p.do("vector", "tensor_copy", macc, ev_sb[0][:], inc=s_v)
            else:
                p.dodr("vector", "tensor_max", macc, macc, ev_sb[0][:])
        p.wait("tensor", p._dr["vector"])
        for j in range(4):
            p.wait("tensor", s_v)
            p.wait("tensor", s_s)
            p.do("tensor", "transpose", psU[1][0:128, 0:128],
                 lt["sq"][:, j * 128:(j + 1) * 128], c["identf"][:], inc=s_t)
            p.wait("vector", s_t)
            p.do("vector", "reduce_max", fac[:, j:j + 1], psU[1][0:128, 0:128],
                 axis=AX.X, inc=s_v)
            p.wait("tensor", s_v)
        p.wait("vector", s_v)
        p.dodr("vector", "tensor_scalar_add", fac[:], fac[:], 1e-6)
        p.dodr("vector", "reciprocal", fac[:], fac[:])
        for h in range(8):
            hb = hbuf[h % 2]
            p.wait("sync", s_v)
            p.dma("sync", hb, comb[h * 128:(h + 1) * 128, q0:q0 + QT], s_in)
            for j in range(4):
                bx = j % 2
                p.wait("sync", s_v)
                p.dma("sync", xsl[bx],
                      xloc[q0 + j * 128:q0 + (j + 1) * 128,
                           h * 128:(h + 1) * 128], s_in)
                p.wait("tensor", s_in)
                p.wait("tensor", s_v)
                p.do("tensor", "transpose", psU[1][0:128, 0:128],
                     hb.tensor[0:128, hb.offset + j * 128:hb.offset + (j + 1) * 128]
                     if False else hb[:, j * 128:(j + 1) * 128],
                     c["identf"][:], inc=s_t)
                p.wait("vector", s_t)
                p.wait("vector", s_in)
                p.wait("vector", s_o)
                p.do("vector", "scalar_tensor_tensor", osl[bx],
                     psU[1][0:128, 0:128], fac[:, j:j + 1], xsl[bx],
                     ALU.mult, ALU.add, inc=s_v)
                p.wait("gpsimd", s_v)
                p.dma("gpsimd", xout[q0 + j * 128:q0 + (j + 1) * 128,
                                     h * 128:(h + 1) * 128], osl[bx], s_o)
    p.wait("gpsimd", s_o)
    if fused:
        out_sem = (s_o, s_o.n)
        p.free_since(mark)
        return out_sem
    return p.finalize()



def build_l0a(p=None, io=None):
    fused = p is not None
    if not fused:
        p = Prog("l0a")
    io = io or {}
    mark = p.marker()
    D_ = lambda n, sh, dt: io[n] if n in io else p.dram(n, sh, dt, "ExternalInput")
    xb = D_("xb", [S, D], F32)
    wq = D_("wq", [D, 256], F32)
    wk = D_("wk", [D, 256], F32)
    wv = D_("wv", [D, 256], F32)
    bq = D_("bq", [1, 256], F32)
    bk = D_("bk", [1, 256], F32)
    bv = D_("bv", [1, 256], F32)
    outT = io["outT"] if "outT" in io else p.dram("outT", [D, TLOC], BF16,
                                                  "ExternalOutput")
    if "gate" in io:
        gs, gv = io["gate"]
        for e in ENGINES:
            p.wait(e, gs, gv)
    if DEBUG_L0A:
        dbg_c = p.dram("dbg_cols", [S // 128, 128, 16], F32, "ExternalOutput")
        dbg_sra = p.dram("dbg_sra", [S // 128, 1, 512], F32, "ExternalOutput")

    c = mk_consts(p)
    g = "gpsimd"
    c["ones_row_f"] = p.sbuf([1, 128], F32, "ones_row_f")
    c["ones_col_f"] = p.sbuf([128, 1], F32, "ones_col_f")
    c["triu_f"] = p.sbuf([128, 128], F32, "triu_f")
    c["triu_b"] = p.sbuf([128, 128], BF16, "triu_b")
    p.do(g, "memset", c["ones_row_f"][:], 1.0)
    p.do(g, "memset", c["ones_col_f"][:], 1.0)
    p.do(g, "memset", c["triu_f"][:], 1.0)
    p.do(g, "affine_select", c["triu_f"][:], c["triu_f"][:], [[1, 128]], ALU.is_ge,
         0.0, base=0, channel_multiplier=-1)
    p.do(g, "memset", c["triu_b"][:], 1.0)
    p.do(g, "affine_select", c["triu_b"][:], c["triu_b"][:], [[1, 128]], ALU.is_ge,
         0.0, base=0, channel_multiplier=-1, inc=c["sem"])

    s_in = p.sem("s_in")
    s_t = p.sem("s_t")
    s_v = p.sem("s_v")
    s_s = p.sem("s_s")
    s_g = p.sem("s_g")
    s_o = p.sem("s_o")

    NH = 4
    NC = S // 128
    EPS = 1e-6

    WB = [p.sbuf([128, 768], BF16) for _ in range(8)]
    bbc = p.sbuf([128, 768], F32, "bbc")
    bqkv = p.sbuf([1, 768], F32, "bqkv")
    outS = [p.sbuf([128, S], BF16) for _ in range(2)]
    xlnT_c = [p.sbuf([128, 128], BF16) for _ in range(8)]
    xt_f = [p.sbuf([128, 1024], F32) for _ in range(2)]
    xbf = p.sbuf([128, 1024], BF16, "xbf")
    lt = mk_ln_tmp(p, c)
    qkv_f = [p.sbuf([128, 192], F32) for _ in range(NH)]
    qk_f = [p.sbuf([128, 128], F32) for _ in range(NH)]
    qk_b = [p.sbuf([128, 128], BF16) for _ in range(NH)]
    qT_b = [p.sbuf([64, 128], BF16) for _ in range(NH)]
    kT_b = [p.sbuf([64, 128], BF16) for _ in range(NH)]
    ksqs_f = [p.sbuf([128, 128], F32) for _ in range(NH)]
    vs_b = [p.sbuf([128, 64], BF16) for _ in range(NH)]
    A_b = [p.sbuf([128, 128], BF16) for _ in range(NH)]
    kvprev_b = [p.sbuf([64, 64], BF16) for _ in range(NH)]
    cum1 = [p.sbuf([128, 128], F32) for _ in range(NH)]
    cum2 = [p.sbuf([128, 128], F32) for _ in range(NH)]
    o_f = [p.sbuf([64, 128], F32) for _ in range(NH)]
    facT = p.sbuf([1, 128], F32, "facT")
    facE = p.sbuf([64, 128], F32, "facE")
    colsc = p.sbuf([128, 16 * NH], F32, "colsc")
    sra_sb = p.sbuf([1, 512], F32, "sra_sb")
    srb_sb = p.sbuf([1, 512], F32, "srb_sb")
    ce_sb = p.sbuf([1, 4], F32, "ce_sb")
    cex_sb = p.sbuf([128, NH], F32, "cex_sb")
    nrm = p.sbuf([128, 1], F32, "nrm")
    rnrm = p.sbuf([128, 1], F32, "rnrm")
    stg = [p.sbuf([128, 768], F32) for _ in range(2)]
    sqs = [p.sbuf([128, 64], F32) for _ in range(NH)]

    pC = p.psum([128, 512], F32, "pC")
    pS = p.psum([128, 512], F32, "pS")
    pO = p.psum([128, 512], F32, "pO")
    pKV = p.psum([128, 512], F32, "pKV")
    pA = p.psum([128, 512], F32, "pA")
    pB = p.psum([128, 512], F32, "pB")
    pE = p.psum([128, 512], F32, "pE")
    psT = p.psum([128, 128], BF16, "psT")

    for cc in range(8):
        b = cc % 2
        sl = slice(cc * 128, (cc + 1) * 128)
        p.wait("sync", s_v)
        p.dma("sync", stg[b][:, 0:256], wq[sl, :], s_in)
        p.dma("sync", stg[b][:, 256:512], wk[sl, :], s_in)
        p.dma("sync", stg[b][:, 512:768], wv[sl, :], s_in)
        p.wait("vector", s_in)
        for h in range(NH):
            p.do("vector", "tensor_copy", WB[cc][:, h * 192:h * 192 + 64],
                 stg[b][:, h * 64:h * 64 + 64])
            p.do("vector", "tensor_copy", WB[cc][:, h * 192 + 64:h * 192 + 128],
                 stg[b][:, 256 + h * 64:256 + h * 64 + 64])
            p.do("vector", "tensor_copy", WB[cc][:, h * 192 + 128:h * 192 + 192],
                 stg[b][:, 512 + h * 64:512 + h * 64 + 64],
                 inc=(s_v if h == NH - 1 else None))
    for h in range(NH):
        p.dma("sync", bqkv[0:1, h * 192:h * 192 + 64],
              bq[0:1, h * 64:h * 64 + 64], s_in)
        p.dma("sync", bqkv[0:1, h * 192 + 64:h * 192 + 128],
              bk[0:1, h * 64:h * 64 + 64], s_in)
        p.dma("sync", bqkv[0:1, h * 192 + 128:h * 192 + 192],
              bv[0:1, h * 64:h * 64 + 64], s_in)
    p.wait("tensor", s_in)
    p.wait("tensor", c["sem"])
    for j in range(0, 768, 512):
        w = min(512, 768 - j)
        p.do("tensor", "matmul", pE[:, 0:w], c["ones_row_f"][:], bqkv[0:1, j:j + w],
             start=True, stop=True, inc=s_t)
        p.wait("vector", s_t)
        p.do("vector", "tensor_copy", bbc[:, j:j + w], pE[:, 0:w], inc=s_v)
        p.wait("tensor", s_v)

    def col(h, i):
        return colsc[:, h * 16 + i:h * 16 + i + 1]

    p.do("vector", "memset", sra_sb[:], 0.0)
    p.do("vector", "memset", srb_sb[:], 0.0)
    p.do("vector", "memset", ce_sb[:], 0.0, inc=s_v)

    first_kvbank_mm = [True]

    la_cnt = []
    for ch in range(NC):
        t0 = ch * 128
        p.wait("gpsimd", s_v)
        p.do(g, "iota", nrm[:], [[1, 1]], base=t0 + 1, channel_multiplier=1,
             allow_small_or_imprecise_dtypes=True, inc=s_g)
        p.wait("vector", s_g)
        p.dodr("vector", "reciprocal", rnrm[:], nrm[:])
        buf = ch % 2
        if ch >= 2:
            p.wait("sync", s_v, la_cnt[-2])
        else:
            p.wait("sync", s_v)
        p.dma("sync", xt_f[buf][:], xb[t0:t0 + 128, :], s_in)
        p.wait("vector", s_in)
        ln_token_major(p, xt_f[buf][:], lt["sq"][:], lt, s_v, s_s)
        la_cnt.append(s_v.n)
        p.wait("vector", s_v)
        p.wait("vector", s_t)
        p.do("vector", "tensor_copy", xbf[:], lt["sq"][:], inc=s_v)
        p.wait("tensor", s_v)
        for cc in range(8):
            p.do("tensor", "transpose", psT[:], xbf[:, cc * 128:(cc + 1) * 128],
                 c["ident"][:], inc=s_t)
            p.wait("scalar", s_t)
            p.do("scalar", "copy", xlnT_c[cc][:], psT[:], inc=s_s)
            p.wait("tensor", s_s)

        for h in range(NH):
            p.wait("tensor", s_s)
            p.wait("tensor", s_v)
            if "vector" in getattr(p, "_dr", {}):
                p.wait("tensor", p._dr["vector"])
            for cc in range(8):
                p.do("tensor", "matmul", pS[:, 128:320], xlnT_c[cc][:],
                     WB[cc][:, h * 192:(h + 1) * 192],
                     start=(cc == 0), stop=(cc == 7),
                     inc=(s_t if cc == 7 else None))
            p.wait("vector", s_t)
            p.dodr("vector", "tensor_add", qkv_f[h][:], pS[:, 128:320],
                   bbc[:, h * 192:(h + 1) * 192])
            p.wait("scalar", p._dr["vector"])
            p.do("scalar", "activation", qk_f[h][:], qkv_f[h][:, 0:128],
                 AF.Sigmoid, inc=s_s)
            p.wait("vector", s_s)
            p.do("vector", "tensor_copy", qk_b[h][:], qk_f[h][:], inc=s_v)
            p.wait("tensor", s_v)
            p.do("tensor", "transpose", psT[:], qk_b[h][:], c["ident"][:], inc=s_t)
            p.wait("scalar", s_t)
            p.do("scalar", "copy", qT_b[h][:], psT[0:64, :])
            p.do("scalar", "copy", kT_b[h][:], psT[64:128, :], inc=s_s)
            p.wait("tensor", s_s)

        for h in range(NH):
            p.wait("tensor", s_s)
            p.wait("tensor", s_v)
            p.do("tensor", "matmul", pC[:, 0:128], c["triu_f"][:], qk_f[h][:],
                 start=True, stop=False)
            p.do("tensor", "matmul", pC[:, 0:128], c["ones_row_f"][:],
                 sra_sb[0:1, h * 128:(h + 1) * 128], start=False, stop=True,
                 inc=s_t)
            p.do("tensor", "matmul", pA[0:1, h * 128:(h + 1) * 128],
                 c["ones_col_f"][:], qk_f[h][:],
                 start=(ch == 0 and h == 0), stop=(ch == NC - 1))
            p.wait("scalar", s_t)
            p.do("scalar", "activation", cum1[h][:], pC[:, 0:128], AF.Identity,
                 bias=c["eps6"][:], inc=s_s)
            p.wait("vector", s_s)
            p.do("vector", "scalar_tensor_tensor", sqs[h][:], qk_f[h][:, 0:64],
                 EPS, cum1[h][:, 64:128], ALU.add, ALU.mult, accum_out=col(h, 0),
                 inc=p._dr["vector"])
            p.wait("vector", p._dr["vector"])
            p.do("vector", "scalar_tensor_tensor", sqs[h][:], qk_f[h][:, 64:128],
                 EPS, cum1[h][:, 0:64], ALU.add, ALU.mult, accum_out=col(h, 1),
                 inc=p._dr["vector"])
            p.wait("vector", p._dr["vector"])
            p.dodr("vector", "reciprocal", col(h, 0), col(h, 0))
            p.dodr("vector", "reciprocal", col(h, 1), col(h, 1))
            p.dodr("vector", "tensor_mul", col(h, 0), col(h, 0), nrm[:])
            p.dodr("vector", "tensor_mul", col(h, 1), col(h, 1), nrm[:])
            p.dodr("vector", "tensor_scalar_mul", ksqs_f[h][:, 0:64],
                   qk_f[h][:, 64:128], col(h, 1))
            p.dodr("vector", "tensor_scalar_mul", ksqs_f[h][:, 64:128],
                   qk_f[h][:, 0:64], col(h, 0))
            p.wait("tensor", p._dr["vector"])
            p.wait("tensor", s_s)
            p.do("tensor", "matmul", pC[:, 128:256], c["triu_f"][:], ksqs_f[h][:],
                 start=True, stop=False)
            p.do("tensor", "matmul", pC[:, 128:256], c["ones_row_f"][:],
                 srb_sb[0:1, h * 128:(h + 1) * 128], start=False, stop=True,
                 inc=s_t)
            p.do("tensor", "matmul", pB[0:1, h * 128:(h + 1) * 128],
                 c["ones_col_f"][:], ksqs_f[h][:],
                 start=(ch == 0 and h == 0), stop=(ch == NC - 1))
            p.wait("scalar", s_t)
            p.do("scalar", "activation", cum2[h][:], pC[:, 128:256], AF.Identity,
                 bias=c["eps6"][:], inc=s_s)
            p.wait("vector", s_s)
            p.do("vector", "scalar_tensor_tensor", sqs[h][:], qk_f[h][:, 0:64],
                 EPS, cum2[h][:, 0:64], ALU.add, ALU.mult, accum_out=col(h, 2),
                 inc=p._dr["vector"])
            p.wait("vector", p._dr["vector"])
            p.do("vector", "scalar_tensor_tensor", sqs[h][:], qk_f[h][:, 64:128],
                 EPS, cum2[h][:, 64:128], ALU.add, ALU.mult, accum_out=col(h, 3),
                 inc=p._dr["vector"])
            p.wait("vector", p._dr["vector"])
            p.dodr("vector", "tensor_mul", col(h, 2), col(h, 2), rnrm[:])
            p.dodr("vector", "tensor_mul", col(h, 3), col(h, 3), rnrm[:])
            p.dodr("vector", "tensor_scalar", col(h, 3), col(h, 3), -1.0, 1.0,
                   ALU.max, ALU.min)
            p.wait("scalar", p._dr["vector"])
            p.do("scalar", "activation", col(h, 8), col(h, 2), AF.Sigmoid)
            p.do("scalar", "activation", col(h, 4), col(h, 3), AF.Exp, inc=s_s)
            p.wait("tensor", s_s)
            p.wait("tensor", p._dr["vector"])
            p.do("tensor", "matmul", pC[:, 256 + h:257 + h], c["triu_f"][:],
                 col(h, 4), start=True, stop=False)
            p.do("tensor", "matmul", pC[:, 256 + h:257 + h], c["ones_row_f"][:],
                 ce_sb[0:1, h:h + 1], start=False, stop=True, inc=s_t)
            p.do("tensor", "matmul", pKV[0:1, 256 + h:257 + h],
                 c["ones_col_f"][:], col(h, 4),
                 start=first_kvbank_mm[0], stop=(ch == NC - 1))
            first_kvbank_mm[0] = False
            p.wait("vector", s_t)
            p.do("vector", "tensor_copy", cex_sb[:, h:h + 1], pC[:, 256 + h:257 + h],
                 inc=s_v)
            p.wait("vector", s_v)
            p.dodr("vector", "reciprocal", col(h, 6), cex_sb[:, h:h + 1])
            p.dodr("vector", "tensor_mul", col(h, 6), col(h, 6), col(h, 4))
            p.dodr("vector", "tensor_mul", col(h, 6), col(h, 6), nrm[:])
            p.dodr("vector", "tensor_scalar_mul", vs_b[h][:], qkv_f[h][:, 128:192],
                   col(h, 6))
            p.wait("tensor", p._dr["vector"])
            p.wait("tensor", s_v)
            p.do("tensor", "matmul", pS[:, 0:128], kT_b[h][:], qT_b[h][:],
                 start=True, stop=True, inc=s_t)
            p.wait("vector", s_t)
            p.dodr("vector", "tensor_mul", A_b[h][:], pS[:, 0:128], c["triu_b"][:])
            if ch == 0:
                p.dodr("vector", "memset", kvprev_b[h][:], 0.0)
            else:
                p.dodr("vector", "tensor_copy", kvprev_b[h][:],
                       pKV[0:64, h * 64:(h + 1) * 64])
            p.wait("tensor", p._dr["vector"])
            p.wait("tensor", s_s)
            p.do("tensor", "matmul", pO[0:64, h * 128:(h + 1) * 128], vs_b[h][:],
                 A_b[h][:], start=True, stop=False)
            p.do("tensor", "matmul", pO[0:64, h * 128:(h + 1) * 128],
                 kvprev_b[h][:], qT_b[h][:], start=False, stop=True, inc=s_t)
            p.do("tensor", "matmul", pKV[0:64, h * 64:(h + 1) * 64],
                 qk_b[h][:, 64:128], vs_b[h][:], start=False,
                 stop=(ch == NC - 1))
            p.wait("vector", s_s)
            p.dodr("vector", "tensor_mul", col(h, 9), col(h, 8), col(h, 0))
            p.dodr("vector", "tensor_mul", col(h, 9), col(h, 9), rnrm[:])
            p.dodr("vector", "tensor_copy", col(h, 7), col(h, 9))
            p.wait("tensor", p._dr["vector"])
            p.do("tensor", "transpose", pE[0:1, 0:128], col(h, 7),
                 c["identf"][:], inc=s_t)
            p.wait("vector", s_t)
            p.do("vector", "tensor_copy", facT[:], pE[0:1, 0:128], inc=s_v)
            p.wait("tensor", s_v)
            p.do("tensor", "matmul", pE[0:64, 128:256], c["ones_row_f"][0:1, 0:64],
                 facT[:], start=True, stop=True, inc=s_t)
            p.wait("scalar", s_t)
            p.do("scalar", "copy", o_f[h][:], pO[0:64, h * 128:(h + 1) * 128],
                 inc=s_s)
            p.wait("vector", s_s)
            row0 = 64 * h
            dst = outS[0] if h < 2 else outS[1]
            rsl = slice((h % 2) * 64, (h % 2) * 64 + 64)
            p.do("vector", "tensor_mul",
                 dst[rsl, t0:t0 + 128], o_f[h][:], pE[0:64, 128:256], inc=s_v)
            p.wait("tensor", s_v)

        if DEBUG_L0A:
            p.wait("gpsimd", s_v)
            p.wait("gpsimd", s_s)
            p.dma("gpsimd", dbg_c[ch], colsc[:, 0:16], s_o)
            p.wait("vector", s_o)
        p.wait("vector", s_t)
        p.do("vector", "tensor_copy", sra_sb[:], pA[0:1, 0:512])
        p.do("vector", "tensor_copy", srb_sb[:], pB[0:1, 0:512])
        p.do("vector", "tensor_copy", ce_sb[:], pKV[0:1, 256:260], inc=s_v)
        if DEBUG_L0A:
            p.wait("gpsimd", s_v)
            p.dma("gpsimd", dbg_sra[ch], sra_sb[:], s_o)
            p.wait("vector", s_o)
        p.wait("tensor", s_v)

    p.wait("gpsimd", s_v)
    for j in range(4):
        cs = slice(2048 * j, 2048 * (j + 1))
        p.dma("gpsimd", outT[256 * j:256 * j + 128, :], outS[0][:, cs], s_o)
        p.dma("gpsimd", outT[256 * j + 128:256 * j + 256, :], outS[1][:, cs], s_o)
    p.wait("gpsimd", s_o)
    if fused:
        out_sem = (s_o, s_o.n)
        p.free_since(mark)
        return out_sem
    return p.finalize()



def _ca(a):
    return np.ascontiguousarray(a)


def kernel(**inp):
    if FUSED:
        return kernel_fused(**inp)
    TL = TLOC
    _ = inp.get("L0_attn_attn_mask")
    x = np.asarray(inp["L0_attn_inputs"], np.float32)

    nc = build_l0a()
    maps = []
    Wq, Wk, Wv = (np.asarray(inp[f"L0_attn_W{z}"], np.float32) for z in "qkv")
    bq, bk, bv = (np.asarray(inp[f"L0_attn_b{z}"], np.float32) for z in "qkv")
    for c in range(NCORE):
        b, gr = c // 4, c % 4
        sl = slice(256 * gr, 256 * (gr + 1))
        maps.append({"xb": _ca(x[b]), "wq": _ca(Wq[:, sl]), "wk": _ca(Wk[:, sl]),
                     "wv": _ca(Wv[:, sl]), "bq": _ca(bq[None, sl]),
                     "bk": _ca(bk[None, sl]), "bv": _ca(bv[None, sl])})
    res = run_bass_kernel_spmd(nc, maps, core_ids=list(range(NCORE)))
    xheadsT = [np.concatenate([res.results[4 * b + gr]["outT"] for gr in range(4)], 0)
               for b in range(2)]

    nc = build_l0b()
    maps = []
    for c in range(NCORE):
        b, j = c // 4, c % 4
        ts = slice(TL * j, TL * (j + 1))
        maps.append({
            "x": _ca(x[b][ts]), "hdT": _ca(xheadsT[b][:, ts]),
            "Wo": np.asarray(inp["L0_attn_Wo"], np.float32),
            "W1": np.asarray(inp["L0_attn_W1"], np.float32),
            "W2": np.asarray(inp["L0_attn_W2"], np.float32),
            "bo": _ca(np.asarray(inp["L0_attn_bo"], np.float32)[None, :]),
            "b1": _ca(np.asarray(inp["L0_attn_b1"], np.float32).reshape(32, 128)),
            "b2": _ca(np.asarray(inp["L0_attn_b2"], np.float32)[None, :]),
            "ln1g": _ca(np.asarray(inp["L0_attn_ln1_g"], np.float32)[None, :]),
            "ln1b": _ca(np.asarray(inp["L0_attn_ln1_b"], np.float32)[None, :]),
            "ln2g": _ca(np.asarray(inp["L0_attn_ln2_g"], np.float32)[None, :]),
            "ln2b": _ca(np.asarray(inp["L0_attn_ln2_b"], np.float32)[None, :]),
        })
    res = run_bass_kernel_spmd(nc, maps, core_ids=list(range(NCORE)))
    x1 = np.stack([np.concatenate([res.results[4 * b + j]["xout"] for j in range(4)], 0)
                   for b in range(2)])

    nc = build_flash("l0f")
    Wkv = np.asarray(inp["L0_ffn_Wkv"], np.float32)
    sel8c = np.kron(np.eye(8), np.ones((1, 64))).astype(np.float32).reshape(8, 512)
    maps = []
    for c in range(NCORE):
        b, j = c // 4, c % 4
        ts = slice(TL * j, TL * (j + 1))
        maps.append({
            "xb": _ca(x1[b]), "xloc": _ca(x1[b][ts]),
            "wq": np.asarray(inp["L0_ffn_Wq"], np.float32),
            "wk": _ca(Wkv[:, 0:512]), "wv": _ca(Wkv[:, 512:1024]),
            "wo": _ca(np.asarray(inp["L0_ffn_Wo"], np.float32).reshape(512, 1024)),
            "ws": np.asarray(inp["L0_ffn_Ws"], np.float32),
            "bo8": np.asarray(inp["L0_ffn_bo"], np.float32),
            "sel8c": sel8c,
        })
    res = run_bass_kernel_spmd(nc, maps, core_ids=list(range(NCORE)))
    x2 = np.stack([np.concatenate([res.results[4 * b + j]["xout"] for j in range(4)], 0)
                   for b in range(2)])

    nc = build_flash("l1a")
    wqkv = np.asarray(inp["L1_attn_w_qkv"], np.float32)
    maps = []
    for c in range(NCORE):
        b, j = c // 4, c % 4
        ts = slice(TL * j, TL * (j + 1))
        maps.append({
            "xb": _ca(x2[b]), "xloc": _ca(x2[b][ts]),
            "wq": _ca(wqkv[:, 0:1024]), "wk": _ca(wqkv[:, 1024:2048]),
            "wv": _ca(wqkv[:, 2048:3072]),
            "wo": np.asarray(inp["L1_attn_w_out"], np.float32),
        })
    res = run_bass_kernel_spmd(nc, maps, core_ids=list(range(NCORE)))
    x3 = np.stack([np.concatenate([res.results[4 * b + j]["xout"] for j in range(4)], 0)
                   for b in range(2)])

    nc = build_l1f()
    import ml_dtypes
    sel8w = np.kron(np.eye(8), np.ones((1, 128))).astype(
        ml_dtypes.bfloat16).reshape(8, 1024)
    base = {
        "gate_w": np.asarray(inp["L1_ffn_gate_w"], np.float32),
        "gbias": _ca(np.asarray(inp["L1_ffn_bias"], np.float32)[None, :]),
        "up_w": np.asarray(inp["L1_ffn_up_w"], np.float32),
        "up_b": np.asarray(inp["L1_ffn_up_b"], np.float32),
        "down_w": np.asarray(inp["L1_ffn_down_w"], np.float32),
        "down_b": np.asarray(inp["L1_ffn_down_b"], np.float32),
        "sw_up": np.asarray(inp["L1_ffn_sw_up"], np.float32),
        "sb_up": _ca(np.asarray(inp["L1_ffn_sb_up"], np.float32).reshape(32, 128)),
        "sw_down": np.asarray(inp["L1_ffn_sw_down"], np.float32),
        "sb_down": _ca(np.asarray(inp["L1_ffn_sb_down"], np.float32).reshape(8, 128)),
        "sel8w": sel8w,
    }
    maps = []
    for c in range(NCORE):
        b, j = c // 4, c % 4
        maps.append(dict(base, xloc=_ca(x3[b][TL * j:TL * (j + 1)])))
    res = run_bass_kernel_spmd(nc, maps, core_ids=list(range(NCORE)))
    x4 = np.stack([np.concatenate([res.results[4 * b + j]["xout"] for j in range(4)], 0)
                   for b in range(2)])
    return x4



RG8 = [[0, 1, 2, 3], [4, 5, 6, 7]]


def build_fused():
    p = Prog("mk")
    dram = p.nc.dram_tensor

    def I(n, sh, dt):
        return p.dram(n, sh, dt, "ExternalInput")

    xb0 = I("xb0", [S, D], F32)
    xloc0 = I("xloc0", [TLOC, D], F32)
    a_wq = I("a_wq", [D, 256], F32)
    a_wk = I("a_wk", [D, 256], F32)
    a_wv = I("a_wv", [D, 256], F32)
    a_bq = I("a_bq", [1, 256], F32)
    a_bk = I("a_bk", [1, 256], F32)
    a_bv = I("a_bv", [1, 256], F32)
    b_Wo = I("b_Wo", [D, D], F32)
    b_W1 = I("b_W1", [D, 4096], F32)
    b_W2 = I("b_W2", [4096, D], F32)
    b_bo = I("b_bo", [1, D], F32)
    b_b1 = I("b_b1", [32, 128], F32)
    b_b2 = I("b_b2", [1, D], F32)
    b_l1g = I("b_l1g", [1, D], F32)
    b_l1b = I("b_l1b", [1, D], F32)
    b_l2g = I("b_l2g", [1, D], F32)
    b_l2b = I("b_l2b", [1, D], F32)
    gidx = I("gidx", [128, 64], I32)
    f_wq = I("f_wq", [D, 512], F32)
    f_wk = I("f_wk", [D, 512], F32)
    f_wv = I("f_wv", [D, 512], F32)
    f_wo = I("f_wo", [512, D], F32)
    f_ws = I("f_ws", [D, 8], F32)
    f_bo8 = I("f_bo8", [8, D], F32)
    f_sel = I("f_sel", [8, 512], F32)
    g_wq = I("g_wq", [D, 1024], F32)
    g_wk = I("g_wk", [D, 1024], F32)
    g_wv = I("g_wv", [D, 1024], F32)
    g_wo = I("g_wo", [D, D], F32)
    m_gw = I("m_gw", [D, 8], F32)
    m_gb = I("m_gb", [1, 8], F32)
    m_uw = I("m_uw", [8, D, 4096], F32)
    m_ub = I("m_ub", [8, 4096], F32)
    m_dw = I("m_dw", [8, 4096, D], F32)
    m_db = I("m_db", [8, D], F32)
    m_su = I("m_su", [D, 4096], F32)
    m_sbu = I("m_sbu", [32, 128], F32)
    m_sd = I("m_sd", [4096, D], F32)
    m_sbd = I("m_sbd", [8, 128], F32)
    m_sel = I("m_sel", [8, 1024], BF16)
    yout = p.dram("yout", [TLOC, D], F32, "ExternalOutput")

    a_out = p.dram("a_out", [256, S], BF16, "Internal")
    gAll = p.dram("gAll", [D, S], BF16, "Internal")
    x1_loc = p.dram("x1_loc", [TLOC, D], F32, "Internal")
    x1_full = p.dram("x1_full", [S, D], F32, "Internal")
    x2_loc = p.dram("x2_loc", [TLOC, D], F32, "Internal")
    x2_full = p.dram("x2_full", [S, D], F32, "Internal")
    x3_loc = p.dram("x3_loc", [TLOC, D], F32, "Internal")

    s_cc = p.sem("cc")
    s_bar = p.sem("bar")

    def barrier():
        for e in ENGINES:
            p.do(e, "nop", inc=s_bar)
        v = s_bar.n
        for e in ENGINES:
            p.wait(e, s_bar, v)

    def collective(kind, src, dst, gate_sem):
        p.wait("gpsimd", *gate_sem)
        p.do("gpsimd", "collective_compute", kind, ALU.bypass, RG8,
             [src.opt()], [dst.opt()], inc=s_cc)

    a_shard = p.dram("a_shard", [D, TLOC], BF16, "Internal")
    p.pref = "A_"
    g1 = build_l0a(p, {"xb": xb0, "wq": a_wq, "wk": a_wk, "wv": a_wv,
                       "bq": a_bq, "bk": a_bk, "bv": a_bv, "outT": a_shard})
    p.pref = ""
    gAll2 = p.dram("gAll2", [4 * D, TLOC], BF16, "Internal")
    collective("AllGather", a_shard, gAll2, g1)
    barrier()
    p.pref = "B_"
    g2 = build_l0b(p, {"x": xloc0, "hdT_view": gAll2, "gidx": gidx,
                       "Wo": b_Wo, "W1": b_W1, "W2": b_W2, "bo": b_bo,
                       "b1": b_b1, "b2": b_b2, "ln1g": b_l1g, "ln1b": b_l1b,
                       "ln2g": b_l2g, "ln2b": b_l2b, "xout": x1_loc,
                       "gate": (s_cc, s_cc.n)})
    p.pref = ""
    collective("AllGather", x1_loc, x1_full, g2)
    barrier()
    p.pref = "F_"
    g3 = build_flash("l0f", p, {"xb": x1_full, "xloc": x1_loc,
                                "wq": f_wq, "wk": f_wk, "wv": f_wv, "wo": f_wo,
                                "ws": f_ws, "bo8": f_bo8, "sel8c": f_sel,
                                "xout": x2_loc, "gate": (s_cc, s_cc.n)})
    p.pref = ""
    collective("AllGather", x2_loc, x2_full, g3)
    barrier()
    p.pref = "G_"
    g4 = build_flash("l1a", p, {"xb": x2_full, "xloc": x2_loc,
                                "wq": g_wq, "wk": g_wk, "wv": g_wv, "wo": g_wo,
                                "xout": x3_loc, "gate": (s_cc, s_cc.n)})
    p.pref = ""
    barrier()
    p.pref = "M_"
    g5 = build_l1f(p, {"xloc": x3_loc, "gate": (g4[0], g4[1]),
                       "gate_w": m_gw, "gbias": m_gb, "up_w": m_uw,
                       "up_b": m_ub, "down_w": m_dw, "down_b": m_db,
                       "sw_up": m_su, "sb_up": m_sbu, "sw_down": m_sd,
                       "sb_down": m_sbd, "sel8w": m_sel, "xout": yout})
    p.pref = ""
    return p.finalize()


FUSED = True


def kernel_fused(**inp):
    TL = TLOC
    x = np.asarray(inp["L0_attn_inputs"], np.float32)
    _ = inp.get("L0_attn_attn_mask")
    import ml_dtypes
    nc = build_fused()
    Wq, Wk, Wv = (np.asarray(inp[f"L0_attn_W{z}"], np.float32) for z in "qkv")
    bq, bk, bv = (np.asarray(inp[f"L0_attn_b{z}"], np.float32) for z in "qkv")
    Wkv = np.asarray(inp["L0_ffn_Wkv"], np.float32)
    wqkv = np.asarray(inp["L1_attn_w_qkv"], np.float32)
    sel8c = np.kron(np.eye(8), np.ones((1, 64))).astype(np.float32).reshape(8, 512)
    sel8w = np.kron(np.eye(8), np.ones((1, 128))).astype(
        ml_dtypes.bfloat16).reshape(8, 1024)
    maps = []
    for c in range(NCORE):
        b, j = c // 4, c % 4
        sl = slice(256 * j, 256 * (j + 1))
        ts = slice(TL * j, TL * (j + 1))
        gi = np.empty((128, 64), np.int32)
        for cc in range(8):
            hr = 128 * cc + np.arange(128)
            base = 1024 * (hr // 256) + 256 * j + (hr % 256)
            for gg in range(8):
                gi[:, cc * 8 + gg] = base * 8 + gg
        maps.append({
            "xb0": _ca(x[b]), "xloc0": _ca(x[b][ts]),
            "a_wq": _ca(Wq[:, sl]), "a_wk": _ca(Wk[:, sl]), "a_wv": _ca(Wv[:, sl]),
            "a_bq": _ca(bq[None, sl]), "a_bk": _ca(bk[None, sl]),
            "a_bv": _ca(bv[None, sl]),
            "b_Wo": np.asarray(inp["L0_attn_Wo"], np.float32),
            "b_W1": np.asarray(inp["L0_attn_W1"], np.float32),
            "b_W2": np.asarray(inp["L0_attn_W2"], np.float32),
            "b_bo": _ca(np.asarray(inp["L0_attn_bo"], np.float32)[None, :]),
            "b_b1": _ca(np.asarray(inp["L0_attn_b1"], np.float32).reshape(32, 128)),
            "b_b2": _ca(np.asarray(inp["L0_attn_b2"], np.float32)[None, :]),
            "b_l1g": _ca(np.asarray(inp["L0_attn_ln1_g"], np.float32)[None, :]),
            "b_l1b": _ca(np.asarray(inp["L0_attn_ln1_b"], np.float32)[None, :]),
            "b_l2g": _ca(np.asarray(inp["L0_attn_ln2_g"], np.float32)[None, :]),
            "b_l2b": _ca(np.asarray(inp["L0_attn_ln2_b"], np.float32)[None, :]),
            "gidx": gi,
            "f_wq": np.asarray(inp["L0_ffn_Wq"], np.float32),
            "f_wk": _ca(Wkv[:, 0:512]), "f_wv": _ca(Wkv[:, 512:1024]),
            "f_wo": _ca(np.asarray(inp["L0_ffn_Wo"], np.float32).reshape(512, 1024)),
            "f_ws": np.asarray(inp["L0_ffn_Ws"], np.float32),
            "f_bo8": np.asarray(inp["L0_ffn_bo"], np.float32),
            "f_sel": sel8c,
            "g_wq": _ca(wqkv[:, 0:1024]), "g_wk": _ca(wqkv[:, 1024:2048]),
            "g_wv": _ca(wqkv[:, 2048:3072]),
            "g_wo": np.asarray(inp["L1_attn_w_out"], np.float32),
            "m_gw": np.asarray(inp["L1_ffn_gate_w"], np.float32),
            "m_gb": _ca(np.asarray(inp["L1_ffn_bias"], np.float32)[None, :]),
            "m_uw": np.asarray(inp["L1_ffn_up_w"], np.float32),
            "m_ub": np.asarray(inp["L1_ffn_up_b"], np.float32),
            "m_dw": np.asarray(inp["L1_ffn_down_w"], np.float32),
            "m_db": np.asarray(inp["L1_ffn_down_b"], np.float32),
            "m_su": np.asarray(inp["L1_ffn_sw_up"], np.float32),
            "m_sbu": _ca(np.asarray(inp["L1_ffn_sb_up"], np.float32).reshape(32, 128)),
            "m_sd": np.asarray(inp["L1_ffn_sw_down"], np.float32),
            "m_sbd": _ca(np.asarray(inp["L1_ffn_sb_down"], np.float32).reshape(8, 128)),
            "m_sel": sel8w,
        })
    res = run_bass_kernel_spmd(nc, maps, core_ids=list(range(NCORE)))
    return np.stack([np.concatenate([res.results[4 * b + j]["yout"]
                                     for j in range(4)], 0) for b in range(2)])
